# version 16
# baseline (speedup 1.0000x reference)
import numpy as np
from contextlib import ExitStack

import concourse.bass as bass
import concourse.mybir as mybir
from concourse.bass_utils import run_bass_kernel_spmd

F32 = mybir.dt.float32
BF16 = mybir.dt.bfloat16
AF = mybir.ActivationFunctionType
ALU = mybir.AluOpType
AX = mybir.AxisListType

NCORE = 8
B = 2
D = 1024
NT = 8
P = 128
L = NT * P
DC = D // P

TWO_PI = 6.283185307179586
C_HI = 6.28125
C_MID = TWO_PI - C_HI


def _consts_np():
    ident = np.eye(P, dtype=np.float32)
    onesrow = np.ones((1, P), np.float32)
    onescol = np.ones((P, 1), np.float32)
    cols = {
        'recip_d': 1.0 / D,
        'neg_recip_d': -1.0 / D,
        'eps_ln': 1e-5,
        'eps_rms': 1e-6,
        'one': 1.0,
        'neg_one': -1.0,
        'half': 0.5,
        'zero': 0.0,
        'magic': 8388608.0,
        'inv_2pi': 1.0 / TWO_PI,
        'neg_chi': -C_HI,
        'neg_cmid': -C_MID,
        'half_pi': np.pi / 2,
        'clamp4095': 4095.0,
        'c_g1': 0.044715,
        'c_g2': 0.7978845608028654,
        'count_k': 1024.0,
        'c64': 64.0,
        'inv32': 0.03125,
        'quarter': 0.25,
        'j1': 1.0,
        'j2': 2.0,
        'j3': 3.0,
    }
    names = sorted(cols)
    colmat = np.stack([np.full(P, cols[n], np.float32) for n in names], axis=1)
    colidx = {n: i for i, n in enumerate(names)}
    return ident, onesrow, onescol, colmat, colidx


IDENT_NP, ONESROW_NP, ONESCOL_NP, COLMAT_NP, COLIDX = _consts_np()
NCOLS = COLMAT_NP.shape[1]


class Sems:
    def __init__(self, nc, ctx, names):
        self.h = {}
        self.c = {}
        for n in names:
            self.h[n] = ctx.enter_context(nc.semaphore(n))
            self.c[n] = 0

    def inc(self, name, by=1):
        self.c[name] += by
        return self.h[name], by

    def val(self, name):
        return self.c[name]

    def hdl(self, name):
        return self.h[name]


class EngOps:
    def __init__(self, eng, sems, name):
        self.eng = eng
        self.sems = sems
        self.name = name

    def __call__(self, fn):
        self.eng.wait_ge(self.sems.hdl(self.name), self.sems.val(self.name))
        return fn().then_inc(*self.sems.inc(self.name))


def _dio(nc, F, name, shape, dtype, kind):
    if F is None:
        return nc.dram_tensor(name, shape, dtype, kind=kind).ap()
    if name in F['io']:
        return F['io'][name]
    k = 'Internal' if kind == 'ExternalOutput' else kind
    if kind == 'ExternalOutput' and (F['pfx'] + name) in F.get('dbg', ()):
        k = 'ExternalOutput'
    ap = nc.dram_tensor(name, shape, dtype, kind=k).ap()
    F['reg'][F['pfx'] + name] = ap
    return ap


class _NcProxy:
    def __init__(self, nc, pfx):
        object.__setattr__(self, '_nc', nc)
        object.__setattr__(self, '_pfx', pfx)

    def sbuf_tensor(self, name, *a, **k):
        return self._nc.sbuf_tensor(self._pfx + name, *a, **k)

    def psum_tensor(self, name, *a, **k):
        return self._nc.psum_tensor(self._pfx + name, *a, **k)

    def dram_tensor(self, name, *a, **k):
        return self._nc.dram_tensor(self._pfx + name, *a, **k)

    def __getattr__(self, n):
        return getattr(self._nc, n)


def stage_barrier(block, sems, skip=()):
    vals = [(sems.hdl(n), sems.val(n)) for n in sems.c if n not in skip]
    for eng in ('tensor', 'vector', 'scalar', 'gpsimd', 'sync'):
        def _(e, vals=vals):
            for h, v in vals:
                e.wait_ge(h, v)
        getattr(block, eng)(_)


def open_common(nc, ctx, extra=()):
    sems = Sems(nc, ctx, ['ld', 'tv', 'vv', 'sv', 'st', 'gd', 'ldk', 'ldv'] + list(extra))
    cm = {}
    cm['ident_b'] = ctx.enter_context(nc.sbuf_tensor('c_ident_b', [P, P], BF16))
    cm['ident_f'] = ctx.enter_context(nc.sbuf_tensor('c_ident_f', [P, P], F32))
    cm['onesrow_b'] = ctx.enter_context(nc.sbuf_tensor('c_onesrow_b', [1, P], BF16))
    cm['onesrow_f'] = ctx.enter_context(nc.sbuf_tensor('c_onesrow_f', [1, P], F32))
    cm['onescol_b'] = ctx.enter_context(nc.sbuf_tensor('c_onescol_b', [P, 1], BF16))
    cm['onescol_f'] = ctx.enter_context(nc.sbuf_tensor('c_onescol_f', [P, 1], F32))
    cm['cols'] = ctx.enter_context(nc.sbuf_tensor('c_cols', [P, NCOLS], F32))
    cm['d_ident'] = nc.dram_tensor('c_ident', [P, P], F32, kind='ExternalInput').ap()
    cm['d_ones_r'] = nc.dram_tensor('c_onesr', [1, P], F32, kind='ExternalInput').ap()
    cm['d_ones_c'] = nc.dram_tensor('c_onesc', [P, 1], F32, kind='ExternalInput').ap()
    cm['d_cols'] = nc.dram_tensor('c_colmat', [P, NCOLS], F32, kind='ExternalInput').ap()
    return sems, cm


def emit_common_loads(block, sems, cm):
    def _(sync):
        sync.wait_ge(sems.hdl('ld'), sems.val('ld'))
        sync.dma_start(out=cm['ident_f'][:], in_=cm['d_ident']).then_inc(*sems.inc('ld', 16))
        sync.wait_ge(sems.hdl('ld'), sems.val('ld'))
        sync.dma_start(out=cm['cols'][:], in_=cm['d_cols']).then_inc(*sems.inc('ld', 16))
        sync.wait_ge(sems.hdl('ld'), sems.val('ld'))
        sync.dma_start(out=cm['onescol_f'][:], in_=cm['d_ones_c']).then_inc(*sems.inc('ld', 16))
        sync.wait_ge(sems.hdl('ld'), sems.val('ld'))
        sync.dma_start(out=cm['onesrow_f'][:], in_=cm['d_ones_r']).then_inc(*sems.inc('ld', 16))
    block.sync(_)

    def _(g):
        g.wait_ge(sems.hdl('gd'), sems.val('gd'))
        g.dma_start(out=cm['ident_b'][:], in_=cm['d_ident']).then_inc(*sems.inc('gd', 16))
        g.wait_ge(sems.hdl('gd'), sems.val('gd'))
        g.dma_start(out=cm['onesrow_b'][:], in_=cm['d_ones_r']).then_inc(*sems.inc('gd', 16))
        g.wait_ge(sems.hdl('gd'), sems.val('gd'))
        g.dma_start(out=cm['onescol_b'][:], in_=cm['d_ones_c']).then_inc(*sems.inc('gd', 16))
    block.gpsimd(_)


def common_inputs():
    return {
        'c_ident': IDENT_NP, 'c_onesr': ONESROW_NP, 'c_onesc': ONESCOL_NP,
        'c_colmat': COLMAT_NP,
    }


def col(cm, name):
    i = COLIDX[name]
    return cm['cols'][:, i:i + 1]


def shard_x(x):
    xt = np.ascontiguousarray(np.asarray(x).reshape(B, 64, P, D))
    return [np.ascontiguousarray(xt[:, c::NCORE]) for c in range(NCORE)]


def unshard_x(parts):
    out = np.empty((B, 64, P, D), np.float32)
    for c in range(NCORE):
        out[:, c::NCORE] = np.asarray(parts[c])
    return out.reshape(B, 8192, D)



def build_prep(layer, F=None):
    nc = bass.Bass('TRN2', target_bir_lowering=False, num_devices=NCORE) if F is None else F['nc']
    with ExitStack() as ctx:
        E = ctx.enter_context
        x_in = _dio(nc, F, 'x', [B, NT, P, D], F32, 'ExternalInput')
        wqt = _dio(nc, F, 'wqt', [D, D], F32, 'ExternalInput')
        wkt = _dio(nc, F, 'wkt', [D, D], F32, 'ExternalInput')
        wvt = _dio(nc, F, 'wvt', [D, D], F32, 'ExternalInput')
        if layer == 0:
            bq_in = _dio(nc, F, 'bq', [P, DC], F32, 'ExternalInput')
            bk_in = _dio(nc, F, 'bk', [P, DC], F32, 'ExternalInput')
            bv_in = _dio(nc, F, 'bv', [1, D], F32, 'ExternalInput')
            gw_in = _dio(nc, F, 'gw', [D, 8], F32, 'ExternalInput')
            gb_in = _dio(nc, F, 'gb', [1, 8], F32, 'ExternalInput')
            w8t_in = _dio(nc, F, 'w8tail', [P, 8], F32, 'ExternalInput')
        else:
            rw_in = _dio(nc, F, 'rw', [P, DC], F32, 'ExternalInput')
            rb_in = _dio(nc, F, 'rb', [1, 1], F32, 'ExternalInput')

        xnfm_o = _dio(nc, F, 'xnfm', [B, DC, P, L], BF16, 'ExternalOutput')
        qfm_o = _dio(nc, F, 'qfm', [B, DC, P, L], BF16, 'ExternalOutput')
        kfm_o = _dio(nc, F, 'kfm', [B, NT, P, DC, P], BF16, 'ExternalOutput')
        vtm_o = _dio(nc, F, 'vtm', [B, NT, P, D], BF16, 'ExternalOutput')
        if layer == 0:
            w8_o = _dio(nc, F, 'w8', [B, NT, P, 16], F32, 'ExternalOutput')
        else:
            xntm_o = _dio(nc, F, 'xntm', [B, NT, P, D], BF16, 'ExternalOutput')
            xnf32_o = _dio(nc, F, 'xnf32', [B, NT, P, D], F32, 'ExternalOutput')
            wrow_o = _dio(nc, F, 'wrow', [B, 1, L], F32, 'ExternalOutput')

        if F is None:
            sems, cm = open_common(nc, ctx)
            block = E(nc.Block())
            emit_common_loads(block, sems, cm)
        else:
            sems, cm, block = F['sems'], F['cm'], F['block']

        s_x = [E(nc.sbuf_tensor('s_x%d' % p, [P, D], F32)) for p in range(2)]
        s_sq = E(nc.sbuf_tensor('s_sq', [P, D], F32))
        s_xnb = [E(nc.sbuf_tensor('s_xnb%d' % p, [P, D], BF16)) for p in range(2)]
        s_xnf = [E(nc.sbuf_tensor('s_xnf%d' % p, [P, D], F32)) for p in range(2)]
        s_stat = [E(nc.sbuf_tensor('s_stat%d' % p, [P, 8], F32)) for p in range(2)]
        s_xnfm = E(nc.sbuf_tensor('s_xnfm', [P, DC * L], BF16))
        s_wq = E(nc.sbuf_tensor('s_wq', [P, DC * D], BF16))
        s_wk = E(nc.sbuf_tensor('s_wk', [P, DC * D], BF16))
        s_wv = E(nc.sbuf_tensor('s_wv', [P, DC * D], BF16))
        s_out = [E(nc.sbuf_tensor('s_out%d' % p, [P, D], BF16)) for p in range(2)]
        if layer == 0:
            s_bq = E(nc.sbuf_tensor('s_bq', [P, DC], F32))
            s_bk = E(nc.sbuf_tensor('s_bk', [P, DC], F32))
            s_bvb = E(nc.sbuf_tensor('s_bvb', [P, D], F32))
            s_bv1 = E(nc.sbuf_tensor('s_bv1', [1, D], BF16))
            s_gw = E(nc.sbuf_tensor('s_gw', [P, DC * 8], BF16))
            s_gb1 = E(nc.sbuf_tensor('s_gb1', [1, 8], BF16))
            s_gbb = E(nc.sbuf_tensor('s_gbb', [P, 8], F32))
            s_gsc = E(nc.sbuf_tensor('s_gsc', [P, 8], F32))
            s_top8 = E(nc.sbuf_tensor('s_top8', [P, 8], F32))
            s_w8 = E(nc.sbuf_tensor('s_w8', [P, 16], F32))
            s_w8t = E(nc.sbuf_tensor('s_w8t', [P, 8], F32))
        else:
            s_rw = E(nc.sbuf_tensor('s_rw', [P, DC], BF16))
            s_rb = E(nc.sbuf_tensor('s_rb', [1, 1], F32))
            s_wrow = E(nc.sbuf_tensor('s_wrow', [1, L], F32))

        p_tr = [E(nc.psum_tensor('p_tr%d' % p, [P, DC * P], BF16)) for p in range(2)]
        p_mm = [E(nc.psum_tensor('p_mm%d' % p, [P, 1024], F32)) for p in range(2)]
        p_sm = E(nc.psum_tensor('p_sm', [P, 512], F32))

        def _(g):
            g.wait_ge(sems.hdl('gd'), sems.val('gd'))
            g.dma_start(out=s_wq[:].rearrange('p (c f) -> p c f', c=DC),
                        in_=wqt.rearrange('(c p) f -> p c f', p=P)).then_inc(*sems.inc('gd', 16))
            g.wait_ge(sems.hdl('gd'), sems.val('gd'))
            g.dma_start(out=s_wk[:].rearrange('p (c f) -> p c f', c=DC),
                        in_=wkt.rearrange('(c p) f -> p c f', p=P)).then_inc(*sems.inc('gd', 16))
            g.wait_ge(sems.hdl('gd'), sems.val('gd'))
            g.dma_start(out=s_wv[:].rearrange('p (c f) -> p c f', c=DC),
                        in_=wvt.rearrange('(c p) f -> p c f', p=P)).then_inc(*sems.inc('gd', 16))
            if layer == 0:
                g.wait_ge(sems.hdl('gd'), sems.val('gd'))
                g.dma_start(out=s_gw[:].rearrange('p (c e) -> p c e', e=8),
                            in_=gw_in.rearrange('(c p) e -> p c e', p=P)).then_inc(*sems.inc('gd', 16))
                g.wait_ge(sems.hdl('gd'), sems.val('gd'))
                g.dma_start(out=s_bv1[:], in_=bv_in).then_inc(*sems.inc('gd', 16))
                g.wait_ge(sems.hdl('gd'), sems.val('gd'))
                g.dma_start(out=s_gb1[:], in_=gb_in).then_inc(*sems.inc('gd', 16))
            else:
                g.wait_ge(sems.hdl('gd'), sems.val('gd'))
                g.dma_start(out=s_rw[:], in_=rw_in).then_inc(*sems.inc('gd', 16))
        block.gpsimd(_)

        def _(sync):
            if layer == 0:
                sync.wait_ge(sems.hdl('ld'), sems.val('ld'))
                sync.dma_start(out=s_bq[:], in_=bq_in).then_inc(*sems.inc('ld', 16))
                sync.wait_ge(sems.hdl('ld'), sems.val('ld'))
                sync.dma_start(out=s_w8t[:], in_=w8t_in).then_inc(*sems.inc('ld', 16))
                sync.wait_ge(sems.hdl('ld'), sems.val('ld'))
                sync.dma_start(out=s_bk[:], in_=bk_in).then_inc(*sems.inc('ld', 16))
            else:
                sync.wait_ge(sems.hdl('ld'), sems.val('ld'))
                sync.dma_start(out=s_rb[:], in_=rb_in).then_inc(*sems.inc('ld', 16))
        block.sync(_)
        ld_all = sems.val('ld')
        gd_all = sems.val('gd')

        for eng in ('tensor', 'vector', 'scalar'):
            def _(e):
                e.wait_ge(sems.hdl('ld'), ld_all)
                e.wait_ge(sems.hdl('gd'), gd_all)
            getattr(block, eng)(_)

        if layer == 0:
            def _(t):
                T = EngOps(t, sems, 'tv')
                t.ldweights(cm['onesrow_b'][:])
                t.matmul(p_mm[0][:, 0:512], cm['onesrow_b'][:], s_bv1[0:1, 0:512], start=True, stop=True)
                t.matmul(p_mm[0][:, 512:1024], cm['onesrow_b'][:], s_bv1[0:1, 512:1024], start=True, stop=True)
                T(lambda: t.matmul(p_sm[:, 0:8], cm['onesrow_b'][:], s_gb1[0:1, :], start=True, stop=True))
            block.tensor(_)
            tv_bc = sems.val('tv')

            def _(v, tv_bc=tv_bc):
                V = EngOps(v, sems, 'vv')
                v.wait_ge(sems.hdl('tv'), tv_bc)
                V(lambda: v.tensor_copy(out=s_bvb[:], in_=p_mm[0][:, 0:1024]))
                V(lambda: v.tensor_copy(out=s_gbb[:], in_=p_sm[:, 0:8]))
            block.vector(_)

        mkp = {'xfree': [0, 0], 'xfree_sv': [0, 0], 'trfree': [sems.val('vv')] * 2,
               'mmfree': [sems.val('vv')] * 2, 'mmfree_sv': [sems.val('sv')] * 2,
               'outfree': [0, 0]}

        for b in range(B):
            for i in range(NT):
                pr = i % 2
                def _(sync, b=b, i=i, pr=pr, gx=mkp['xfree'][pr], gs=mkp['xfree_sv'][pr]):
                    sync.wait_ge(sems.hdl('vv'), gx)
                    sync.wait_ge(sems.hdl('sv'), gs)
                    sync.wait_ge(sems.hdl('ld'), sems.val('ld'))
                    sync.dma_start(out=s_x[pr][:], in_=x_in[b, i]).then_inc(*sems.inc('ld', 16))
                block.sync(_)
                ld_now = sems.val('ld')

                def _(v, pr=pr, ld_now=ld_now):
                    V = EngOps(v, sems, 'vv')
                    v.wait_ge(sems.hdl('ld'), ld_now)
                    V(lambda: v.reduce_sum(out=s_stat[pr][:, 0:1], in_=s_x[pr][:], axis=AX.X))
                    V(lambda: v.tensor_scalar(out=s_stat[pr][:, 1:2], in0=s_stat[pr][:, 0:1],
                                              scalar1=col(cm, 'neg_recip_d'), scalar2=None,
                                              op0=ALU.mult))
                block.vector(_)
                vv_m = sems.val('vv')

                def _(s, pr=pr, vv_m=vv_m):
                    S = EngOps(s, sems, 'sv')
                    s.wait_ge(sems.hdl('vv'), vv_m)
                    S(lambda: s.activation(out=s_sq[:], in_=s_x[pr][:], func=AF.Square,
                                           bias=s_stat[pr][:, 1:2], scale=1.0,
                                           accum_out=s_stat[pr][:, 2:3]))
                    S(lambda: s.activation(out=s_stat[pr][:, 3:4], in_=s_stat[pr][:, 2:3], func=AF.Sqrt,
                                           bias=col(cm, 'eps_ln'), scale=1.0 / D))
                block.scalar(_)
                sv_s = sems.val('sv')
                mkp['xfree_sv'][pr] = sv_s

                def _(v, pr=pr, sv_s=sv_s, layer=layer, gst=sems.val('st')):
                    V = EngOps(v, sems, 'vv')
                    v.wait_ge(sems.hdl('sv'), sv_s)
                    v.wait_ge(sems.hdl('st'), gst)
                    V(lambda: v.reciprocal(out=s_stat[pr][:, 4:5], in_=s_stat[pr][:, 3:4]))
                    V(lambda: v.tensor_scalar(out=s_xnb[pr][:], in0=s_x[pr][:],
                                              scalar1=s_stat[pr][:, 1:2], scalar2=s_stat[pr][:, 4:5],
                                              op0=ALU.add, op1=ALU.mult))
                    if layer == 1:
                        V(lambda: v.tensor_scalar(out=s_xnf[pr][:], in0=s_x[pr][:],
                                                  scalar1=s_stat[pr][:, 1:2], scalar2=s_stat[pr][:, 4:5],
                                                  op0=ALU.add, op1=ALU.mult))
                block.vector(_)
                vv_ln = sems.val('vv')
                mkp['xfree'][pr] = vv_ln

                if layer == 1:
                    def _(sync, b=b, i=i, pr=pr, vv_ln=vv_ln):
                        sync.wait_ge(sems.hdl('vv'), vv_ln)
                        sync.wait_ge(sems.hdl('st'), sems.val('st'))
                        sync.dma_start(out=xntm_o[b, i], in_=s_xnb[pr][:]).then_inc(*sems.inc('st', 16))
                        sync.wait_ge(sems.hdl('st'), sems.val('st'))
                        sync.dma_start(out=xnf32_o[b, i], in_=s_xnf[pr][:]).then_inc(*sems.inc('st', 16))
                    block.sync(_)

                def _(t, pr=pr, vv_ln=vv_ln, gtr=mkp['trfree'][pr]):
                    T = EngOps(t, sems, 'tv')
                    t.wait_ge(sems.hdl('vv'), vv_ln)
                    t.wait_ge(sems.hdl('vv'), gtr)
                    for dc in range(DC):
                        op = lambda dc=dc: t.transpose(p_tr[pr][:, dc * P:(dc + 1) * P],
                                                       s_xnb[pr][:, dc * P:(dc + 1) * P],
                                                       cm['ident_b'][:])
                        if dc == DC - 1:
                            T(op)
                        else:
                            op()
                block.tensor(_)
                tv_tr = sems.val('tv')

                def _(v, i=i, pr=pr, tv_tr=tv_tr):
                    V = EngOps(v, sems, 'vv')
                    v.wait_ge(sems.hdl('tv'), tv_tr)
                    V(lambda: v.tensor_copy(
                        out=s_xnfm[:].rearrange('p (c l) -> p c l', c=DC)[:, :, i * P:(i + 1) * P],
                        in_=p_tr[pr][:].rearrange('p (c q) -> p c q', c=DC)))
                block.vector(_)
                mkp['trfree'][pr] = sems.val('vv')

            vv_fm = sems.val('vv')

            def _(sync, b=b, vv_fm=vv_fm):
                sync.wait_ge(sems.hdl('vv'), vv_fm)
                sync.wait_ge(sems.hdl('st'), sems.val('st'))
                sync.dma_start(out=xnfm_o[b].rearrange('c p l -> p c l'),
                               in_=s_xnfm[:].rearrange('p (c l) -> p c l', c=DC)
                               ).then_inc(*sems.inc('st', 16))
            block.sync(_)

            for wi, (wsb, out_t, bname) in enumerate(((s_wq, qfm_o, 'bq'), (s_wk, kfm_o, 'bk'))):
                for ec in range(DC):
                    u = wi * DC + ec
                    pu = u % 2
                    def _(t, wsb=wsb, ec=ec, pu=pu, vv_fm=vv_fm, gmm=mkp['mmfree'][pu]):
                        T = EngOps(t, sems, 'tv')
                        t.wait_ge(sems.hdl('vv'), vv_fm)
                        t.wait_ge(sems.hdl('vv'), gmm)
                        for dc in range(DC):
                            wtile = wsb[:, dc * D + ec * P: dc * D + ec * P + P]
                            t.ldweights(wtile)
                            t.matmul(p_mm[pu][:, 0:512], wtile, s_xnfm[:, dc * L: dc * L + 512],
                                     start=(dc == 0), stop=(dc == DC - 1))
                            op = lambda dc=dc, wtile=wtile: t.matmul(
                                p_mm[pu][:, 512:1024], wtile, s_xnfm[:, dc * L + 512: dc * L + 1024],
                                start=(dc == 0), stop=(dc == DC - 1))
                            if dc == DC - 1:
                                T(op)
                            else:
                                op()
                    block.tensor(_)
                    tv_p = sems.val('tv')

                    def _(v, ec=ec, pu=pu, bname=bname, tv_p=tv_p, layer=layer,
                          gout=mkp['outfree'][pu]):
                        V = EngOps(v, sems, 'vv')
                        v.wait_ge(sems.hdl('tv'), tv_p)
                        v.wait_ge(sems.hdl('st'), gout)
                        if layer == 0:
                            bap = (s_bq if bname == 'bq' else s_bk)[:, ec:ec + 1]
                            V(lambda: v.tensor_scalar(out=s_out[pu][:, 0:1024], in0=p_mm[pu][:, 0:1024],
                                                      scalar1=bap, scalar2=None, op0=ALU.add))
                        else:
                            V(lambda: v.tensor_copy(out=s_out[pu][:, 0:1024], in_=p_mm[pu][:, 0:1024]))
                    block.vector(_)
                    mkp['mmfree'][pu] = sems.val('vv')
                    vv_p = sems.val('vv')

                    def _(sync, ec=ec, pu=pu, out_t=out_t, bname=bname, b=b, vv_p=vv_p):
                        sync.wait_ge(sems.hdl('vv'), vv_p)
                        sync.wait_ge(sems.hdl('st'), sems.val('st'))
                        if bname == 'bk':
                            sync.dma_start(out=out_t[b].rearrange('i p c q -> p c i q')[:, ec],
                                           in_=s_out[pu][:, 0:1024].rearrange('p (i q) -> p i q', i=NT)
                                           ).then_inc(*sems.inc('st', 16))
                        else:
                            sync.dma_start(out=out_t[b, ec],
                                           in_=s_out[pu][:, 0:1024]).then_inc(*sems.inc('st', 16))
                    block.sync(_)
                    mkp['outfree'][pu] = sems.val('st')

            for i in range(NT):
                pu = i % 2
                def _(t, i=i, pu=pu, gmm=mkp['mmfree'][pu]):
                    T = EngOps(t, sems, 'tv')
                    t.wait_ge(sems.hdl('vv'), gmm)
                    for dc in range(DC):
                        xt = s_xnfm[:, dc * L + i * P: dc * L + (i + 1) * P]
                        t.ldweights(xt)
                        t.matmul(p_mm[pu][:, 0:512], xt, s_wv[:, dc * D: dc * D + 512],
                                 start=(dc == 0), stop=(dc == DC - 1))
                        op = lambda dc=dc, xt=xt: t.matmul(
                            p_mm[pu][:, 512:1024], xt, s_wv[:, dc * D + 512: dc * D + 1024],
                            start=(dc == 0), stop=(dc == DC - 1))
                        if dc == DC - 1:
                            T(op)
                        else:
                            op()
                block.tensor(_)
                tv_v = sems.val('tv')

                def _(v, pu=pu, tv_v=tv_v, layer=layer, gout=mkp['outfree'][pu]):
                    V = EngOps(v, sems, 'vv')
                    v.wait_ge(sems.hdl('tv'), tv_v)
                    v.wait_ge(sems.hdl('st'), gout)
                    if layer == 0:
                        V(lambda: v.scalar_tensor_tensor(out=s_out[pu][:, 0:1024], in0=p_mm[pu][:, 0:1024],
                                                         scalar=col(cm, 'one'), in1=s_bvb[:],
                                                         op0=ALU.mult, op1=ALU.add))
                    else:
                        V(lambda: v.tensor_copy(out=s_out[pu][:, 0:1024], in_=p_mm[pu][:, 0:1024]))
                block.vector(_)
                mkp['mmfree'][pu] = sems.val('vv')
                vv_v = sems.val('vv')

                def _(sync, i=i, pu=pu, b=b, vv_v=vv_v):
                    sync.wait_ge(sems.hdl('vv'), vv_v)
                    sync.wait_ge(sems.hdl('st'), sems.val('st'))
                    sync.dma_start(out=vtm_o[b, i], in_=s_out[pu][:, 0:1024]).then_inc(*sems.inc('st', 16))
                block.sync(_)
                mkp['outfree'][pu] = sems.val('st')

            if layer == 0:
                for i in range(NT):
                    def _(t, i=i):
                        T = EngOps(t, sems, 'tv')
                        t.wait_ge(sems.hdl('vv'), sems.val('vv'))
                        for dc in range(DC):
                            xt = s_xnfm[:, dc * L + i * P: dc * L + (i + 1) * P]
                            t.ldweights(xt)
                            op = lambda dc=dc, xt=xt: t.matmul(
                                p_sm[:, 0:8], xt, s_gw[:, dc * 8:(dc + 1) * 8],
                                start=(dc == 0), stop=(dc == DC - 1))
                            if dc == DC - 1:
                                T(op)
                            else:
                                op()
                    block.tensor(_)
                    tv_g = sems.val('tv')

                    def _(v, tv_g=tv_g):
                        V = EngOps(v, sems, 'vv')
                        v.wait_ge(sems.hdl('tv'), tv_g)
                        v.wait_ge(sems.hdl('st'), sems.val('st'))
                        V(lambda: v.tensor_tensor(out=s_gsc[:], in0=p_sm[:, 0:8], in1=s_gbb[:], op=ALU.add))
                        V(lambda: v.reduce_max(out=s_stat[0][:, 5:6], in_=s_gsc[:], axis=AX.X, negate=True))
                    block.vector(_)
                    vv_g1 = sems.val('vv')

                    def _(s, vv_g1=vv_g1):
                        S = EngOps(s, sems, 'sv')
                        s.wait_ge(sems.hdl('vv'), vv_g1)
                        S(lambda: s.activation(out=s_gsc[:], in_=s_gsc[:], func=AF.Exp,
                                               bias=s_stat[0][:, 5:6], scale=1.0,
                                               accum_out=s_stat[0][:, 6:7]))
                    block.scalar(_)
                    sv_g = sems.val('sv')

                    def _(v, sv_g=sv_g):
                        V = EngOps(v, sems, 'vv')
                        v.wait_ge(sems.hdl('sv'), sv_g)
                        V(lambda: v.reciprocal(out=s_stat[0][:, 7:8], in_=s_stat[0][:, 6:7]))
                        V(lambda: v.tensor_scalar(out=s_gsc[:], in0=s_gsc[:],
                                                  scalar1=s_stat[0][:, 7:8], scalar2=None, op0=ALU.mult))
                        V(lambda: v.max(out=s_top8[:], in_=s_gsc[:]))
                        V(lambda: v.tensor_scalar(out=s_w8[:, 0:8], in0=s_gsc[:],
                                                  scalar1=s_top8[:, 1:2], scalar2=None, op0=ALU.is_ge))
                        V(lambda: v.tensor_tensor(out=s_w8[:, 0:8], in0=s_w8[:, 0:8], in1=s_gsc[:],
                                                  op=ALU.mult))
                        V(lambda: v.tensor_copy(out=s_w8[:, 8:16], in_=s_w8t[:]))
                    block.vector(_)
                    vv_g2 = sems.val('vv')

                    def _(sync, i=i, b=b, vv_g2=vv_g2):
                        sync.wait_ge(sems.hdl('vv'), vv_g2)
                        sync.wait_ge(sems.hdl('st'), sems.val('st'))
                        sync.dma_start(out=w8_o[b, i], in_=s_w8[:]).then_inc(*sems.inc('st', 16))
                    block.sync(_)
            else:
                def _(t):
                    T = EngOps(t, sems, 'tv')
                    t.wait_ge(sems.hdl('vv'), sems.val('vv'))
                    t.wait_ge(sems.hdl('sv'), sems.val('sv'))
                    for dc in range(DC):
                        t.ldweights(s_rw[:, dc:dc + 1])
                        op = lambda dc=dc: t.matmul(
                            p_sm[0:1, 0:512], s_rw[:, dc:dc + 1],
                            s_xnfm[:, dc * L: dc * L + 512],
                            start=(dc == 0), stop=(dc == DC - 1))
                        if dc == DC - 1:
                            T(op)
                        else:
                            op()
                    for dc in range(DC):
                        t.ldweights(s_rw[:, dc:dc + 1])
                        op = lambda dc=dc: t.matmul(
                            p_mm[0][0:1, 0:512], s_rw[:, dc:dc + 1],
                            s_xnfm[:, dc * L + 512: dc * L + 1024],
                            start=(dc == 0), stop=(dc == DC - 1))
                        if dc == DC - 1:
                            T(op)
                        else:
                            op()
                block.tensor(_)
                tv_r = sems.val('tv')

                def _(v, tv_r=tv_r):
                    V = EngOps(v, sems, 'vv')
                    v.wait_ge(sems.hdl('tv'), tv_r)
                    v.wait_ge(sems.hdl('st'), sems.val('st'))
                    V(lambda: v.tensor_scalar(out=s_wrow[0:1, 0:512], in0=p_sm[0:1, 0:512],
                                              scalar1=s_rb[0:1, 0:1], scalar2=None, op0=ALU.add))
                    V(lambda: v.tensor_scalar(out=s_wrow[0:1, 512:1024], in0=p_mm[0][0:1, 0:512],
                                              scalar1=s_rb[0:1, 0:1], scalar2=None, op0=ALU.add))
                block.vector(_)
                vv_r = sems.val('vv')

                def _(sync, b=b, vv_r=vv_r):
                    sync.wait_ge(sems.hdl('vv'), vv_r)
                    sync.wait_ge(sems.hdl('st'), sems.val('st'))
                    sync.dma_start(out=wrow_o[b], in_=s_wrow[:]).then_inc(*sems.inc('st', 16))
                block.sync(_)
                mkp['mmfree'][0] = sems.val('vv')

            if F is not None and 'after_batch' in F:
                F['after_batch'](b)

        def _(sync):
            sync.wait_ge(sems.hdl('st'), sems.val('st'))
        block.sync(_)
    return nc


def prep_inputs(layer, x_parts, I):
    base = common_inputs()
    if layer == 0:
        base.update({
            'wqt': np.ascontiguousarray(I['L0_attn_Wq'].T),
            'wkt': np.ascontiguousarray(I['L0_attn_Wk'].T),
            'wvt': np.ascontiguousarray(I['L0_attn_Wv'].T),
            'bq': np.ascontiguousarray(I['L0_attn_bq'].reshape(DC, P).T),
            'bk': np.ascontiguousarray(I['L0_attn_bk'].reshape(DC, P).T),
            'bv': I['L0_attn_bv'].reshape(1, D),
            'gw': np.ascontiguousarray(I['L0_ffn_gate_w']),
            'gb': I['L0_ffn_gate_b'].reshape(1, 8),
            'w8tail': np.tile(np.array([0.5, 0.5, 0, 0, 0, 0, 0, 0], np.float32), (P, 1)),
        })
    else:
        base.update({
            'wqt': np.ascontiguousarray(I['L1_attn_Wq'].T),
            'wkt': np.ascontiguousarray(I['L1_attn_Wk'].T),
            'wvt': np.ascontiguousarray(I['L1_attn_Wv'].T),
            'rw': np.ascontiguousarray(I['L1_ffn_router_w'].reshape(DC, P).T),
            'rb': I['L1_ffn_router_b'].reshape(1, 1),
        })
    return [dict(base, x=np.ascontiguousarray(x_parts[c])) for c in range(NCORE)]


def kernel(**inputs):
    raise NotImplementedError('under construction')



def build_attn(layer, F=None):
    nc = bass.Bass('TRN2', target_bir_lowering=False, num_devices=NCORE) if F is None else F['nc']
    with ExitStack() as ctx:
        E = ctx.enter_context
        q_in = _dio(nc, F, 'qfm', [B, DC, P, L], BF16, 'ExternalInput')
        k_in = _dio(nc, F, 'kag', [NCORE, B, NT, P, DC, P], BF16, 'ExternalInput')
        v_in = _dio(nc, F, 'vfull', [NCORE, B, NT, P, D], BF16, 'ExternalInput')
        dmask_in = _dio(nc, F, 'dmask', [P, 1024], F32, 'ExternalInput')
        if layer == 1:
            wot_in = _dio(nc, F, 'wot', [D, D], F32, 'ExternalInput')
        attn_o = _dio(nc, F, 'attn', [B, NT, P, D], F32, 'ExternalOutput')

        if F is None:
            sems, cm = open_common(nc, ctx)
            block = E(nc.Block())
            emit_common_loads(block, sems, cm)
        else:
            sems, cm, block = F['sems'], F['cm'], F['block']

        s_q = E(nc.sbuf_tensor('s_q', [P, DC * L], BF16))
        s_kb = E(nc.sbuf_tensor('s_kb', [P, 2 * DC * 1024], BF16))
        s_vb = E(nc.sbuf_tensor('s_vb', [P, 2 * NCORE * D], BF16))
        s_S = E(nc.sbuf_tensor('s_S', [P, 1024], F32))
        s_P = E(nc.sbuf_tensor('s_P', [P, 1024], BF16))
        s_PT = E(nc.sbuf_tensor('s_PT', [P, 1024], BF16))
        s_O = E(nc.sbuf_tensor('s_O', [P, D], F32))
        s_fst = E(nc.sbuf_tensor('s_fst', [P, 8], F32))
        s_mask = E(nc.sbuf_tensor('s_mask', [P, 1024], F32))
        s_attn = E(nc.sbuf_tensor('s_attn', [P, D], F32))
        if layer == 1:
            s_wo = E(nc.sbuf_tensor('s_wo', [P, DC * D], BF16))
            s_ob = E(nc.sbuf_tensor('s_ob', [P, D], BF16))
            s_ot = E(nc.sbuf_tensor('s_ot', [P, D], BF16))

        p_S0 = E(nc.psum_tensor('p_S0', [P, 1024], F32))
        p_S1 = E(nc.psum_tensor('p_S1', [P, 1024], F32))
        p_Ss = [p_S0, p_S1]
        p_PT = E(nc.psum_tensor('p_PT', [P, 1024], BF16))
        p_O = E(nc.psum_tensor('p_O', [P, 1024], F32))

        def _(sync):
            sync.wait_ge(sems.hdl('ld'), sems.val('ld'))
            sync.dma_start(out=s_mask[:], in_=dmask_in).then_inc(*sems.inc('ld', 16))
        block.sync(_)

        if layer == 1:
            def _(g):
                g.wait_ge(sems.hdl('gd'), sems.val('gd'))
                g.dma_start(out=s_wo[:].rearrange('p (c f) -> p c f', c=DC),
                            in_=wot_in.rearrange('(c p) f -> p c f', p=P)).then_inc(*sems.inc('gd', 16))
            block.gpsimd(_)
        ld_base = sems.val('ld')
        gd_base = sems.val('gd')

        for eng in ('tensor', 'vector', 'scalar'):
            def _(e):
                e.wait_ge(sems.hdl('ld'), ld_base)
                e.wait_ge(sems.hdl('gd'), gd_base)
            getattr(block, eng)(_)

        mk = {
            'kslot_pe': [0, 0],
            'vslot_pe': [0, 0],
            'evac_sv': 0,
            'evac_sv_slot': [0, 0],
            'exp_sv': 0,
            'max_vv': 0,
            'ptevac_vv': 0,
            'oupd_vv': 0,
            'S_tv': {},
            'ld_kv': {},
        }

        ccw = (F or {}).get('ccw', {})

        def emit_kv_dma(b, j, sl):
            def _(sync, b=b, j=j, sl=sl, gK=mk['kslot_pe'][sl]):
                sync.wait_ge(sems.hdl('tv'), gK)
                if b in ccw:
                    sync.wait_ge(sems.hdl('cc'), ccw[b])
                sync.wait_ge(sems.hdl('ldk'), sems.val('ldk'))
                sync.dma_start(
                    out=s_kb[:, sl * DC * 1024:(sl + 1) * DC * 1024]
                        .rearrange('p (r c q) -> p r c q', r=NCORE, c=DC),
                    in_=k_in[:, b, j].rearrange('r p c q -> p r c q')
                ).then_inc(*sems.inc('ldk', 16))
            block.sync(_)

            def _(g, b=b, j=j, sl=sl, gV=mk['vslot_pe'][sl]):
                g.wait_ge(sems.hdl('tv'), gV)
                if b in ccw:
                    g.wait_ge(sems.hdl('cc'), ccw[b])
                g.wait_ge(sems.hdl('ldv'), sems.val('ldv'))
                g.dma_start(
                    out=s_vb[:, sl * NCORE * D:(sl + 1) * NCORE * D]
                        .rearrange('p (r d) -> p r d', r=NCORE),
                    in_=v_in[:, b, j].rearrange('r p d -> p r d')
                ).then_inc(*sems.inc('ldv', 16))
            block.gpsimd(_)
            mk['ld_kv'][j] = (sems.val('ldk'), sems.val('ldv'))

        def emit_S(i, j, sl):
            def _(t, i=i, sl=sl, ldw=mk['ld_kv'][j], ev=mk['evac_sv_slot'][sl]):
                T = EngOps(t, sems, 'tv')
                t.wait_ge(sems.hdl('ldk'), ldw[0])
                t.wait_ge(sems.hdl('sv'), ev)
                kview = s_kb[:, sl * DC * 1024:(sl + 1) * DC * 1024].rearrange(
                    'p (r c q) -> p r c q', r=NCORE, c=DC)
                for dc in range(DC):
                    qt = s_q[:, dc * L + i * P: dc * L + (i + 1) * P]
                    t.ldweights(qt)
                    t.matmul(p_Ss[sl][:, 0:512], qt, kview[:, 0:4, dc, :],
                             start=(dc == 0), stop=(dc == DC - 1))
                    op = lambda dc=dc, qt=qt: t.matmul(
                        p_Ss[sl][:, 512:1024], qt, kview[:, 4:8, dc, :],
                        start=(dc == 0), stop=(dc == DC - 1))
                    if dc == DC - 1:
                        T(op)
                    else:
                        op()
            block.tensor(_)
            mk['kslot_pe'][sl] = sems.val('tv')
            mk['S_tv'][j] = sems.val('tv')

        for b in range(B):
            def _(sync, b=b):
                sync.wait_ge(sems.hdl('tv'), sems.val('tv'))
                sync.wait_ge(sems.hdl('ld'), sems.val('ld'))
                sync.dma_start(out=s_q[:].rearrange('p (c l) -> p c l', c=DC),
                               in_=q_in[b].rearrange('c p l -> p c l')).then_inc(*sems.inc('ld', 16))
            block.sync(_)
            ld_q = sems.val('ld')

            def _(t, ld_q=ld_q):
                t.wait_ge(sems.hdl('ld'), ld_q)
            block.tensor(_)

            for i in range(NT):
                nblk = i + 1
                emit_kv_dma(b, 0, 0)
                emit_S(i, 0, 0)
                for j in range(nblk):
                    sl = j % 2
                    if j + 1 < nblk:
                        emit_kv_dma(b, j + 1, (j + 1) % 2)
                        emit_S(i, j + 1, (j + 1) % 2)

                    def _(v, i=i, j=j, sl=sl, stv=mk['S_tv'][j]):
                        V = EngOps(v, sems, 'vv')
                        v.wait_ge(sems.hdl('tv'), stv)
                        if j == i:
                            V(lambda: v.tensor_tensor(out=p_Ss[sl][:], in0=p_Ss[sl][:],
                                                      in1=s_mask[:], op=ALU.add))
                        V(lambda: v.reduce_max(out=s_fst[:, 0:1], in_=p_Ss[sl][:], axis=AX.X,
                                               negate=True))
                        if j == 0:
                            V(lambda: v.tensor_scalar(out=s_fst[:, 2:3], in0=s_fst[:, 0:1],
                                                      scalar1=col(cm, 'inv32'), scalar2=None,
                                                      op0=ALU.mult))
                        else:
                            V(lambda: v.tensor_scalar(out=s_fst[:, 0:1], in0=s_fst[:, 0:1],
                                                      scalar1=col(cm, 'inv32'), scalar2=None,
                                                      op0=ALU.mult))
                            V(lambda: v.tensor_tensor(out=s_fst[:, 2:3], in0=s_fst[:, 1:2],
                                                      in1=s_fst[:, 0:1], op=ALU.min))
                            V(lambda: v.tensor_tensor(out=s_fst[:, 3:4], in0=s_fst[:, 2:3],
                                                      in1=s_fst[:, 1:2], op=ALU.subtract))
                    block.vector(_)
                    mk['max_vv'] = sems.val('vv')
                    vv_st = sems.val('vv')

                    def _(s, j=j, sl=sl, vv_st=vv_st, ptv=mk.get('pt_tv', 0)):
                        S = EngOps(s, sems, 'sv')
                        s.wait_ge(sems.hdl('vv'), vv_st)
                        s.wait_ge(sems.hdl('tv'), ptv)
                        S(lambda: s.activation(out=s_P[:], in_=p_Ss[sl][:], func=AF.Exp,
                                               bias=s_fst[:, 2:3], scale=0.03125,
                                               accum_out=s_fst[:, 4:5]))
                        if j > 0:
                            S(lambda: s.activation(out=s_fst[:, 5:6], in_=s_fst[:, 3:4],
                                                   func=AF.Exp, bias=col(cm, 'zero'), scale=1.0))
                    block.scalar(_)
                    mk['evac_sv_slot'][sl] = sems.val('sv')
                    mk['exp_sv'] = sems.val('sv')
                    sv_P = sems.val('sv')

                    def _(v, j=j, sv_P=sv_P):
                        V = EngOps(v, sems, 'vv')
                        v.wait_ge(sems.hdl('sv'), sv_P)
                        if j == 0:
                            V(lambda: v.tensor_copy(out=s_fst[:, 6:7], in_=s_fst[:, 4:5]))
                        else:
                            V(lambda: v.scalar_tensor_tensor(out=s_fst[:, 6:7], in0=s_fst[:, 6:7],
                                                             scalar=s_fst[:, 5:6], in1=s_fst[:, 4:5],
                                                             op0=ALU.mult, op1=ALU.add))
                        V(lambda: v.tensor_copy(out=s_fst[:, 1:2], in_=s_fst[:, 2:3]))
                    block.vector(_)

                    def _(t, sv_P=sv_P, pev=mk['ptevac_vv']):
                        T = EngOps(t, sems, 'tv')
                        t.wait_ge(sems.hdl('sv'), sv_P)
                        t.wait_ge(sems.hdl('vv'), pev)
                        for kc in range(DC):
                            op = lambda kc=kc: t.transpose(p_PT[:, kc * P:(kc + 1) * P],
                                                           s_P[:, kc * P:(kc + 1) * P],
                                                           cm['ident_b'][:])
                            if kc == DC - 1:
                                T(op)
                            else:
                                op()
                    block.tensor(_)
                    mk['pt_tv'] = sems.val('tv')
                    tv_PT = sems.val('tv')

                    def _(v, tv_PT=tv_PT):
                        V = EngOps(v, sems, 'vv')
                        v.wait_ge(sems.hdl('tv'), tv_PT)
                        V(lambda: v.tensor_copy(out=s_PT[:], in_=p_PT[:]))
                    block.vector(_)
                    mk['ptevac_vv'] = sems.val('vv')
                    vv_PT = sems.val('vv')

                    def _(t, sl=sl, vv_PT=vv_PT, ou=mk['oupd_vv'], ldv=mk['ld_kv'][j][1]):
                        T = EngOps(t, sems, 'tv')
                        t.wait_ge(sems.hdl('vv'), vv_PT)
                        t.wait_ge(sems.hdl('ldv'), ldv)
                        t.wait_ge(sems.hdl('vv'), ou)
                        for kc in range(DC):
                            pt = s_PT[:, kc * P:(kc + 1) * P]
                            vbase = sl * NCORE * D + kc * D
                            t.ldweights(pt)
                            t.matmul(p_O[:, 0:512], pt, s_vb[:, vbase: vbase + 512],
                                     start=(kc == 0), stop=(kc == DC - 1))
                            op = lambda kc=kc, pt=pt, vbase=vbase: t.matmul(
                                p_O[:, 512:1024], pt, s_vb[:, vbase + 512: vbase + 1024],
                                start=(kc == 0), stop=(kc == DC - 1))
                            if kc == DC - 1:
                                T(op)
                            else:
                                op()
                    block.tensor(_)
                    mk['vslot_pe'][sl] = sems.val('tv')
                    tv_PV = sems.val('tv')

                    def _(v, j=j, tv_PV=tv_PV):
                        V = EngOps(v, sems, 'vv')
                        v.wait_ge(sems.hdl('tv'), tv_PV)
                        if j == 0:
                            V(lambda: v.tensor_copy(out=s_O[:], in_=p_O[:]))
                        else:
                            V(lambda: v.scalar_tensor_tensor(out=s_O[:], in0=s_O[:],
                                                             scalar=s_fst[:, 5:6], in1=p_O[:],
                                                             op0=ALU.mult, op1=ALU.add))
                    block.vector(_)
                    mk['oupd_vv'] = sems.val('vv')

                    if F is not None and 'ileave' in F:
                        F['ileave']()

                def _(v):
                    V = EngOps(v, sems, 'vv')
                    v.wait_ge(sems.hdl('st'), sems.val('st'))
                    V(lambda: v.reciprocal(out=s_fst[:, 7:8], in_=s_fst[:, 6:7]))
                    if layer == 0:
                        V(lambda: v.tensor_scalar(out=s_attn[:], in0=s_O[:],
                                                  scalar1=s_fst[:, 7:8], scalar2=None, op0=ALU.mult))
                    else:
                        V(lambda: v.tensor_scalar(out=s_ob[:], in0=s_O[:],
                                                  scalar1=s_fst[:, 7:8], scalar2=None, op0=ALU.mult))
                block.vector(_)
                vv_fin = sems.val('vv')

                if layer == 0:
                    def _(sync, b=b, i=i, vv_fin=vv_fin):
                        sync.wait_ge(sems.hdl('vv'), vv_fin)
                        sync.wait_ge(sems.hdl('ld'), sems.val('ld'))
                        sync.dma_start(out=attn_o[b, i], in_=s_attn[:]).then_inc(*sems.inc('st', 16))
                    block.sync(_)
                else:
                    def _(t, vv_fin=vv_fin, pev=mk['ptevac_vv']):
                        T = EngOps(t, sems, 'tv')
                        t.wait_ge(sems.hdl('vv'), vv_fin)
                        t.wait_ge(sems.hdl('vv'), pev)
                        for dc in range(DC):
                            op = lambda dc=dc: t.transpose(p_PT[:, dc * P:(dc + 1) * P],
                                                           s_ob[:, dc * P:(dc + 1) * P],
                                                           cm['ident_b'][:])
                            if dc == DC - 1:
                                T(op)
                            else:
                                op()
                    block.tensor(_)
                    mk['pt_tv'] = sems.val('tv')
                    tv_otr = sems.val('tv')

                    def _(v, tv_otr=tv_otr):
                        V = EngOps(v, sems, 'vv')
                        v.wait_ge(sems.hdl('tv'), tv_otr)
                        V(lambda: v.tensor_copy(out=s_ot[:], in_=p_PT[:]))
                    block.vector(_)
                    mk['ptevac_vv'] = sems.val('vv')
                    vv_otr = sems.val('vv')

                    def _(t, vv_otr=vv_otr, ou=mk['oupd_vv']):
                        T = EngOps(t, sems, 'tv')
                        t.wait_ge(sems.hdl('vv'), vv_otr)
                        t.wait_ge(sems.hdl('vv'), ou)
                        t.wait_ge(sems.hdl('gd'), gd_base)
                        for dc in range(DC):
                            ot = s_ot[:, dc * P:(dc + 1) * P]
                            t.ldweights(ot)
                            t.matmul(p_O[:, 0:512], ot, s_wo[:, dc * D: dc * D + 512],
                                     start=(dc == 0), stop=(dc == DC - 1))
                            op = lambda dc=dc, ot=ot: t.matmul(
                                p_O[:, 512:1024], ot, s_wo[:, dc * D + 512: dc * D + 1024],
                                start=(dc == 0), stop=(dc == DC - 1))
                            if dc == DC - 1:
                                T(op)
                            else:
                                op()
                    block.tensor(_)
                    mk['vslot_pe'][0] = sems.val('tv')
                    mk['vslot_pe'][1] = sems.val('tv')
                    tv_wo = sems.val('tv')

                    def _(v, tv_wo=tv_wo):
                        V = EngOps(v, sems, 'vv')
                        v.wait_ge(sems.hdl('tv'), tv_wo)
                        v.wait_ge(sems.hdl('st'), sems.val('st'))
                        V(lambda: v.tensor_copy(out=s_attn[:], in_=p_O[:]))
                    block.vector(_)
                    mk['oupd_vv'] = sems.val('vv')
                    vv_wo = sems.val('vv')

                    def _(sync, b=b, i=i, vv_wo=vv_wo):
                        sync.wait_ge(sems.hdl('vv'), vv_wo)
                        sync.dma_start(out=attn_o[b, i], in_=s_attn[:]).then_inc(*sems.inc('st', 16))
                    block.sync(_)

        def _(sync):
            sync.wait_ge(sems.hdl('st'), sems.val('st'))
        block.sync(_)
    return nc


def attn_inputs(layer, prep_results, I, core):
    kag = np.ascontiguousarray(np.stack([np.asarray(r['kfm']) for r in prep_results]))
    vfull = np.ascontiguousarray(np.stack([np.asarray(r['vtm']) for r in prep_results]))
    maps = []
    for c in range(NCORE):
        m = dict(common_inputs())
        m['qfm'] = np.ascontiguousarray(prep_results[c]['qfm'])
        m['kag'] = kag
        m['vfull'] = vfull
        o = np.arange(1024)[None, :]
        p = np.arange(P)[:, None]
        m['dmask'] = np.where(o > 128 * c + p, -1e9, 0.0).astype(np.float32)
        if layer == 1:
            m['wot'] = np.ascontiguousarray(I['L1_attn_Wo'].T)
        maps.append(m)
    return maps



NEXP = 10
H = 2048
HC = H // P


def build_moe(F=None):
    nc = bass.Bass('TRN2', target_bir_lowering=False, num_devices=NCORE) if F is None else F['nc']
    with ExitStack() as ctx:
        E = ctx.enter_context
        x_in = _dio(nc, F, 'x', [B, NT, P, D], F32, 'ExternalInput')
        attn_in = _dio(nc, F, 'attn', [B, NT, P, D], F32, 'ExternalInput')
        xnfm_in = _dio(nc, F, 'xnfm', [B, DC, P, L], BF16, 'ExternalInput')
        w8_in = _dio(nc, F, 'w8e', [B, NT, P, 16], F32, 'ExternalInput')
        w1_in = _dio(nc, F, 'w1all', [NEXP, D, H], F32, 'ExternalInput')
        w2_in = _dio(nc, F, 'w2all', [NEXP, D, H], F32, 'ExternalInput')
        w3_in = _dio(nc, F, 'w3all', [NEXP, H, D], F32, 'ExternalInput')
        b1_in = _dio(nc, F, 'b1fm', [P, NEXP * HC], F32, 'ExternalInput')
        b2_in = _dio(nc, F, 'b2fm', [P, NEXP * HC], F32, 'ExternalInput')
        b3_in = _dio(nc, F, 'b3all', [16, D], F32, 'ExternalInput')
        x1_o = _dio(nc, F, 'x1', [B, NT, P, D], F32, 'ExternalOutput')

        if F is None:
            sems, cm = open_common(nc, ctx)
            block = E(nc.Block())
            emit_common_loads(block, sems, cm)
        else:
            sems, cm, block = F['sems'], F['cm'], F['block']

        s_xnfm = E(nc.sbuf_tensor('s_xnfm', [P, B * DC * L], BF16))
        s_acc = E(nc.sbuf_tensor('s_acc', [P, B * NT * D], F32))
        s_g = E(nc.sbuf_tensor('s_g', [P, HC * L], BF16))
        s_w3 = E(nc.sbuf_tensor('s_w3', [P, HC * D], BF16))
        s_w12 = E(nc.sbuf_tensor('s_w12', [P, 4 * DC * P], BF16))
        s_h1 = E(nc.sbuf_tensor('s_h1', [P, L], F32))
        s_b1 = E(nc.sbuf_tensor('s_b1', [P, NEXP * HC], F32))
        s_b2 = E(nc.sbuf_tensor('s_b2', [P, NEXP * HC], F32))
        s_b3 = E(nc.sbuf_tensor('s_b3', [16, D], F32))
        s_w8 = E(nc.sbuf_tensor('s_w8', [P, B * NT * 16], F32))
        s_w8t = E(nc.sbuf_tensor('s_w8t', [16, B * NT * P], F32))
        s_xt = E(nc.sbuf_tensor('s_xt', [P, D], F32))
        s_at = E(nc.sbuf_tensor('s_at', [P, D], F32))
        s_x1 = E(nc.sbuf_tensor('s_x1', [P, D], F32))

        p_A0 = E(nc.psum_tensor('p_A0', [P, 1024], F32))
        p_A1 = E(nc.psum_tensor('p_A1', [P, 1024], F32))
        p_B0 = E(nc.psum_tensor('p_B0', [P, 1024], F32))
        p_T16 = E(nc.psum_tensor('p_T16', [16, P], F32))
        pAs = [p_A0, p_A1]
        ypair = [[p_A0[:, 0:512], p_A0[:, 512:1024]], [p_A1[:, 0:512], p_A1[:, 512:1024]]]

        def _(sync):
            sync.wait_ge(sems.hdl('ld'), sems.val('ld'))
            sync.dma_start(out=s_xnfm[:].rearrange('p (b c l) -> p b c l', b=B, c=DC),
                           in_=xnfm_in.rearrange('b c p l -> p b c l')).then_inc(*sems.inc('ld', 16))
            sync.wait_ge(sems.hdl('ld'), sems.val('ld'))
            sync.dma_start(out=s_b1[:], in_=b1_in).then_inc(*sems.inc('ld', 16))
            sync.wait_ge(sems.hdl('ld'), sems.val('ld'))
            sync.dma_start(out=s_b2[:], in_=b2_in).then_inc(*sems.inc('ld', 16))
            sync.wait_ge(sems.hdl('ld'), sems.val('ld'))
            sync.dma_start(out=s_w8[:].rearrange('p (b i e) -> p b i e', b=B, i=NT),
                           in_=w8_in.rearrange('b i p e -> p b i e')).then_inc(*sems.inc('ld', 16))
            sync.wait_ge(sems.hdl('ld'), sems.val('ld'))
            sync.dma_start(out=s_b3[:], in_=b3_in).then_inc(*sems.inc('ld', 16))
        block.sync(_)
        ld0 = sems.val('ld')
        gd0 = sems.val('gd')

        for eng in ('tensor', 'vector', 'scalar'):
            def _(e):
                e.wait_ge(sems.hdl('ld'), ld0)
                e.wait_ge(sems.hdl('gd'), gd0)
            getattr(block, eng)(_)

        for b in range(B):
            for i in range(NT):
                def _(t, b=b, i=i):
                    T = EngOps(t, sems, 'tv')
                    t.wait_ge(sems.hdl('vv'), sems.val('vv'))
                    T(lambda: t.transpose(p_T16[:], s_w8[:, (b * NT + i) * 16:(b * NT + i + 1) * 16],
                                          cm['ident_f'][:]))
                block.tensor(_)
                tv_t = sems.val('tv')

                def _(v, b=b, i=i, tv_t=tv_t):
                    V = EngOps(v, sems, 'vv')
                    v.wait_ge(sems.hdl('tv'), tv_t)
                    V(lambda: v.tensor_copy(out=s_w8t[:, (b * NT + i) * P:(b * NT + i + 1) * P],
                                            in_=p_T16[:]))
                block.vector(_)
        vv_w8t = sems.val('vv')

        for b in range(B):
            for i in range(NT):
                def _(t, b=b, i=i, vv_w8t=vv_w8t):
                    T = EngOps(t, sems, 'tv')
                    t.wait_ge(sems.hdl('vv'), vv_w8t)
                    t.wait_ge(sems.hdl('vv'), sems.val('vv'))
                    wt = s_w8t[:, (b * NT + i) * P:(b * NT + i + 1) * P]
                    t.matmul(p_B0[:, 0:512], wt, s_b3[:, 0:512], start=True, stop=True)
                    T(lambda wt=wt: t.matmul(p_B0[:, 512:1024], wt, s_b3[:, 512:1024],
                                             start=True, stop=True))
                block.tensor(_)
                tv_b3 = sems.val('tv')

                def _(v, b=b, i=i, tv_b3=tv_b3):
                    V = EngOps(v, sems, 'vv')
                    v.wait_ge(sems.hdl('tv'), tv_b3)
                    V(lambda: v.tensor_copy(out=s_acc[:, (b * NT + i) * D:(b * NT + i + 1) * D],
                                            in_=p_B0[:, 0:1024]))
                block.vector(_)

        w12_guard = [0, 0]
        mkA = {'silu_sv': [0, 0], 'g_vv': sems.val('vv'), 'y_vv': [0, 0], 'h1_tv': 0,
               'w3_free': 0}
        for e in range(NEXP):
            def _(g, e=e, w3f=mkA['w3_free']):
                g.wait_ge(sems.hdl('tv'), w3f)
                g.wait_ge(sems.hdl('ldv'), sems.val('ldv'))
                g.dma_start(out=s_w3[:].rearrange('p (c f) -> p c f', c=HC),
                            in_=w3_in[e].rearrange('(c p) f -> p c f', p=P)
                            ).then_inc(*sems.inc('ldv', 16))
            block.gpsimd(_)
            ldv_w3 = sems.val('ldv')
            for b in range(B):
                for hc in range(HC):
                    sl = hc % 2
                    def _(g, e=e, hc=hc, sl=sl, guard=w12_guard[sl]):
                        g.wait_ge(sems.hdl('tv'), guard)
                        g.wait_ge(sems.hdl('gd'), sems.val('gd'))
                        g.dma_start(out=s_w12[:, sl * 2 * D: sl * 2 * D + D]
                                    .rearrange('p (c q) -> p c q', c=DC),
                                    in_=w1_in[e].rearrange('(c p) f -> p c f', p=P)[:, :, hc * P:(hc + 1) * P]
                                    ).then_inc(*sems.inc('gd', 16))
                        g.wait_ge(sems.hdl('gd'), sems.val('gd'))
                        g.dma_start(out=s_w12[:, sl * 2 * D + D: sl * 2 * D + 2 * D]
                                    .rearrange('p (c q) -> p c q', c=DC),
                                    in_=w2_in[e].rearrange('(c p) f -> p c f', p=P)[:, :, hc * P:(hc + 1) * P]
                                    ).then_inc(*sems.inc('gd', 16))
                    block.gpsimd(_)
                    gd_c = sems.val('gd')

                    def _(t, b=b, sl=sl, gd_c=gd_c, sa=mkA['silu_sv'][sl], gv=mkA['g_vv'],
                          yv=max(mkA['y_vv'])):
                        T = EngOps(t, sems, 'tv')
                        t.wait_ge(sems.hdl('gd'), gd_c)
                        t.wait_ge(sems.hdl('sv'), sa)
                        t.wait_ge(sems.hdl('vv'), yv)
                        for dc in range(DC):
                            w1t = s_w12[:, sl * 2 * D + dc * P: sl * 2 * D + (dc + 1) * P]
                            xb = s_xnfm[:, (b * DC + dc) * L: (b * DC + dc) * L + L]
                            t.ldweights(w1t)
                            t.matmul(pAs[sl][:, 0:512], w1t, xb[:, 0:512],
                                     start=(dc == 0), stop=(dc == DC - 1))
                            op = lambda dc=dc, w1t=w1t, xb=xb: t.matmul(
                                pAs[sl][:, 512:1024], w1t, xb[:, 512:1024],
                                start=(dc == 0), stop=(dc == DC - 1))
                            if dc == DC - 1:
                                T(op)
                            else:
                                op()
                        mkA['h1_tv'] = sems.val('tv')
                        t.wait_ge(sems.hdl('vv'), gv)
                        for dc in range(DC):
                            w2t = s_w12[:, sl * 2 * D + D + dc * P: sl * 2 * D + D + (dc + 1) * P]
                            xb = s_xnfm[:, (b * DC + dc) * L: (b * DC + dc) * L + L]
                            t.ldweights(w2t)
                            t.matmul(p_B0[:, 0:512], w2t, xb[:, 0:512],
                                     start=(dc == 0), stop=(dc == DC - 1))
                            op = lambda dc=dc, w2t=w2t, xb=xb: t.matmul(
                                p_B0[:, 512:1024], w2t, xb[:, 512:1024],
                                start=(dc == 0), stop=(dc == DC - 1))
                            if dc == DC - 1:
                                T(op)
                            else:
                                op()
                    block.tensor(_)
                    tv_h = sems.val('tv')
                    w12_guard[sl] = tv_h

                    def _(s, e=e, hc=hc, sl=sl, h1tv=mkA['h1_tv'], gprev=mkA['g_vv']):
                        S = EngOps(s, sems, 'sv')
                        s.wait_ge(sems.hdl('tv'), h1tv)
                        s.wait_ge(sems.hdl('vv'), gprev)
                        S(lambda: s.activation(out=s_h1[:], in_=pAs[sl][:], func=AF.Silu,
                                               bias=s_b1[:, e * HC + hc: e * HC + hc + 1], scale=1.0))
                    block.scalar(_)
                    mkA['silu_sv'][sl] = sems.val('sv')
                    sv_h = sems.val('sv')

                    def _(v, e=e, hc=hc, sv_h=sv_h, h2tv=tv_h):
                        V = EngOps(v, sems, 'vv')
                        v.wait_ge(sems.hdl('sv'), sv_h)
                        v.wait_ge(sems.hdl('tv'), h2tv)
                        V(lambda: v.scalar_tensor_tensor(
                            out=s_g[:, hc * L:(hc + 1) * L], in0=p_B0[:],
                            scalar=s_b2[:, e * HC + hc: e * HC + hc + 1], in1=s_h1[:],
                            op0=ALU.add, op1=ALU.mult))
                    block.vector(_)
                    mkA['g_vv'] = sems.val('vv')

                vv_gall = sems.val('vv')

                for i in range(NT):
                    yp = i % 2
                    def _(t, b=b, i=i, yp=yp, ldv_w3=ldv_w3, vv_gall=vv_gall,
                          yv=mkA['y_vv'][yp], sa=mkA['silu_sv'][yp]):
                        T = EngOps(t, sems, 'tv')
                        t.wait_ge(sems.hdl('ldv'), ldv_w3)
                        t.wait_ge(sems.hdl('vv'), vv_gall)
                        t.wait_ge(sems.hdl('vv'), yv)
                        t.wait_ge(sems.hdl('sv'), sa)
                        ys = ypair[yp]
                        for hcx in range(HC):
                            gt = s_g[:, hcx * L + i * P: hcx * L + (i + 1) * P]
                            t.ldweights(gt)
                            t.matmul(ys[0], gt, s_w3[:, hcx * D: hcx * D + 512],
                                     start=(hcx == 0), stop=(hcx == HC - 1))
                            op = lambda hcx=hcx, gt=gt, ys=ys: t.matmul(
                                ys[1], gt, s_w3[:, hcx * D + 512: hcx * D + 1024],
                                start=(hcx == 0), stop=(hcx == HC - 1))
                            if hcx == HC - 1:
                                T(op)
                            else:
                                op()
                    block.tensor(_)
                    tv_y = sems.val('tv')
                    if b == 1 and i == NT - 1:
                        mkA['w3_free'] = tv_y

                    def _(v, e=e, b=b, i=i, yp=yp, tv_y=tv_y):
                        V = EngOps(v, sems, 'vv')
                        v.wait_ge(sems.hdl('tv'), tv_y)
                        w8col = s_w8[:, (b * NT + i) * 16 + e: (b * NT + i) * 16 + e + 1]
                        ys = ypair[yp]
                        for hh in range(2):
                            accsl = s_acc[:, (b * NT + i) * D + hh * 512:
                                          (b * NT + i) * D + (hh + 1) * 512]
                            V(lambda accsl=accsl, hh=hh, ys=ys: v.scalar_tensor_tensor(
                                out=accsl, in0=ys[hh], scalar=w8col, in1=accsl,
                                op0=ALU.mult, op1=ALU.add))
                    block.vector(_)
                    mkA['y_vv'][yp] = sems.val('vv')

        for b in range(B):
            for i in range(NT):
                def _(sync, b=b, i=i):
                    sync.wait_ge(sems.hdl('vv'), sems.val('vv'))
                    sync.wait_ge(sems.hdl('ld'), sems.val('ld'))
                    sync.dma_start(out=s_xt[:], in_=x_in[b, i]).then_inc(*sems.inc('ld', 16))
                    sync.wait_ge(sems.hdl('ld'), sems.val('ld'))
                    sync.dma_start(out=s_at[:], in_=attn_in[b, i]).then_inc(*sems.inc('ld', 16))
                block.sync(_)
                ld_c = sems.val('ld')

                def _(v, b=b, i=i, ld_c=ld_c):
                    V = EngOps(v, sems, 'vv')
                    v.wait_ge(sems.hdl('ld'), ld_c)
                    v.wait_ge(sems.hdl('st'), sems.val('st'))
                    V(lambda: v.tensor_tensor(out=s_x1[:], in0=s_xt[:], in1=s_at[:], op=ALU.add))
                    V(lambda: v.tensor_tensor(out=s_x1[:], in0=s_x1[:],
                                              in1=s_acc[:, (b * NT + i) * D:(b * NT + i + 1) * D],
                                              op=ALU.add))
                block.vector(_)
                vv_c = sems.val('vv')

                def _(sync, b=b, i=i, vv_c=vv_c):
                    sync.wait_ge(sems.hdl('vv'), vv_c)
                    sync.dma_start(out=x1_o[b, i], in_=s_x1[:]).then_inc(*sems.inc('st', 16))
                block.sync(_)

        def _(sync):
            sync.wait_ge(sems.hdl('st'), sems.val('st'))
        block.sync(_)
    return nc


def moe_inputs(x_parts, prep_results, attn_results, I):
    w1all = np.ascontiguousarray(np.concatenate([I['L0_ffn_rw1'], I['L0_ffn_sw1']], 0))
    w2all = np.ascontiguousarray(np.concatenate([I['L0_ffn_rw2'], I['L0_ffn_sw2']], 0))
    w3all = np.ascontiguousarray(np.concatenate([I['L0_ffn_rw3'], I['L0_ffn_sw3']], 0))
    b1all = np.concatenate([I['L0_ffn_rb1'], I['L0_ffn_sb1']], 0)
    b2all = np.concatenate([I['L0_ffn_rb2'], I['L0_ffn_sb2']], 0)
    b3all = np.zeros((16, D), np.float32)
    b3all[:NEXP] = np.concatenate([I['L0_ffn_rb3'], I['L0_ffn_sb3']], 0)
    b1fm = np.ascontiguousarray(b1all.reshape(NEXP, HC, P).transpose(2, 0, 1).reshape(P, NEXP * HC))
    b2fm = np.ascontiguousarray(b2all.reshape(NEXP, HC, P).transpose(2, 0, 1).reshape(P, NEXP * HC))
    maps = []
    for c in range(NCORE):
        m = dict(common_inputs())
        m['x'] = np.ascontiguousarray(x_parts[c])
        m['attn'] = np.ascontiguousarray(np.asarray(attn_results[c]['attn']))
        m['xnfm'] = np.ascontiguousarray(np.asarray(prep_results[c]['xnfm']))
        w8 = np.asarray(prep_results[c]['w8'], np.float32)
        w8e = np.zeros((B, NT, P, 16), np.float32)
        w8e[..., :8] = w8
        w8e[..., 8:10] = 0.5
        m['w8e'] = w8e
        m.update(w1all=w1all, w2all=w2all, w3all=w3all, b1fm=b1fm, b2fm=b2fm, b3all=b3all)
        maps.append(m)
    return maps




def emit_router_bis(nc, E, sems, cm, block, wfull_ap, pr, cc_after=None):
    bb = {}
    wpos = nc.dram_tensor('rb_wpos', [B, 64, P], F32, kind='Internal')
    s_w64 = E(nc.sbuf_tensor('rb_w64', [64, P], F32))
    s_wqT = E(nc.sbuf_tensor('rb_wqT', [P, 128], F32))
    s_whi = E(nc.sbuf_tensor('rb_whi', [P, 128], BF16))
    s_wlo = E(nc.sbuf_tensor('rb_wlo', [P, 128], BF16))
    s_wlof = E(nc.sbuf_tensor('rb_wlof', [P, 64], F32))
    s_bp = E(nc.sbuf_tensor('rb_bp', [P, 8], F32))
    s_onespp = E(nc.sbuf_tensor('rb_onespp', [P, P], F32))
    s_mid2 = E(nc.sbuf_tensor('rb_mid2', [P, 2], F32))
    s_cnt2 = E(nc.sbuf_tensor('rb_cnt2', [P, 2], F32))
    s_cmp = E(nc.sbuf_tensor('rb_cmp', [P, 64], F32))
    s_cnt = E(nc.sbuf_tensor('rb_cnt', [P, 1], F32))
    s_bis = E(nc.sbuf_tensor('rb_bis', [1, 4], F32))
    bb.update(s_wqT=s_wqT, s_whi=s_whi, s_wlo=s_wlo, s_bp=s_bp)
    st = {'rnd': 0, 'tv': None}
    NRB = 50

    def loads():
        def _(sync):
            if cc_after is not None:
                sync.wait_ge(sems.hdl('cc'), cc_after)
            sync.wait_ge(sems.hdl('ld'), sems.val('ld'))
            sync.dma_start(out=wpos.ap().rearrange('b (i r) q -> b i r q', r=NCORE),
                           in_=wfull_ap.rearrange('r b o (i q) -> b i r (o q)', q=P)
                           ).then_inc(*sems.inc('ld', 16))
        block.sync(_)

        for b in range(B):
            def _(sync, b=b):
                sync.wait_ge(sems.hdl('vv'), sems.val('vv'))
                sync.wait_ge(sems.hdl('tv'), sems.val('tv'))
                sync.wait_ge(sems.hdl('ld'), sems.val('ld'))
                sync.dma_start(out=s_w64[:], in_=wpos.ap()[b]).then_inc(*sems.inc('ld', 16))
            block.sync(_)
            ld_w = sems.val('ld')

            def _(t, ld_w=ld_w):
                T = EngOps(t, sems, 'tv')
                t.wait_ge(sems.hdl('ld'), ld_w)
                t.wait_ge(sems.hdl('vv'), sems.val('vv'))
                T(lambda: t.transpose(pr[:, 0:64], s_w64[:], cm['ident_f'][0:64, 0:64]))
            block.tensor(_)
            tv_w = sems.val('tv')

            def _(v, b=b, tv_w=tv_w):
                V = EngOps(v, sems, 'vv')
                v.wait_ge(sems.hdl('tv'), tv_w)
                wq = s_wqT[:, 64 * b:64 * b + 64]
                wh = s_whi[:, 64 * b:64 * b + 64]
                wl = s_wlo[:, 64 * b:64 * b + 64]
                V(lambda: v.tensor_copy(out=wq, in_=pr[:, 0:64]))
                V(lambda: v.tensor_copy(out=wh, in_=wq))
                V(lambda: v.tensor_copy(out=s_wlof[:], in_=wh))
                V(lambda: v.tensor_tensor(out=s_wlof[:], in0=wq, in1=s_wlof[:], op=ALU.subtract))
                V(lambda: v.tensor_copy(out=wl, in_=s_wlof[:]))
                V(lambda: v.reduce_max(out=s_cnt[:], in_=wq, axis=AX.X))
            block.vector(_)
            vv_hi0 = sems.val('vv')

            def _(t, vv_hi0=vv_hi0):
                T = EngOps(t, sems, 'tv')
                t.wait_ge(sems.hdl('vv'), vv_hi0)
                T(lambda: t.transpose(pr[0:1, 64:192], s_cnt[:], cm['ident_f'][:]))
            block.tensor(_)
            tv_hi = sems.val('tv')

            def _(v, b=b, tv_hi=tv_hi):
                V = EngOps(v, sems, 'vv')
                v.wait_ge(sems.hdl('tv'), tv_hi)
                V(lambda: v.reduce_max(out=s_bis[:, 2 + b:3 + b], in_=pr[0:1, 64:192], axis=AX.X))
                V(lambda: v.reduce_max(out=s_cnt[:], in_=s_wqT[:, 64 * b:64 * b + 64],
                                       axis=AX.X, negate=True))
            block.vector(_)
            vv_lo0 = sems.val('vv')

            def _(t, vv_lo0=vv_lo0):
                T = EngOps(t, sems, 'tv')
                t.wait_ge(sems.hdl('vv'), vv_lo0)
                T(lambda: t.transpose(pr[0:1, 64:192], s_cnt[:], cm['ident_f'][:]))
            block.tensor(_)
            tv_lo = sems.val('tv')

            def _(v, b=b, tv_lo=tv_lo):
                V = EngOps(v, sems, 'vv')
                v.wait_ge(sems.hdl('tv'), tv_lo)
                lob = s_bis[:, b:b + 1]
                V(lambda: v.reduce_max(out=lob, in_=pr[0:1, 64:192], axis=AX.X))
                V(lambda: v.tensor_scalar(out=lob, in0=lob,
                                          scalar1=cm['cols'][0:1, COLIDX['neg_one']:COLIDX['neg_one'] + 1],
                                          scalar2=None, op0=ALU.mult))
                V(lambda: v.tensor_scalar(out=lob, in0=lob,
                                          scalar1=cm['cols'][0:1, COLIDX['neg_one']:COLIDX['neg_one'] + 1],
                                          scalar2=None, op0=ALU.add))
            block.vector(_)

        def _(t):
            T = EngOps(t, sems, 'tv')
            t.wait_ge(sems.hdl('vv'), sems.val('vv'))
            t.matmul(pr[:, 256:260], cm['onesrow_f'][:], s_bis[0:1, 0:4], start=True, stop=True)
            T(lambda: t.matmul(pr[:, 320:448], cm['onesrow_f'][:], cm['onesrow_f'][:],
                               start=True, stop=True))
        block.tensor(_)
        tv_b4 = sems.val('tv')

        def _(v, tv_b4=tv_b4):
            V = EngOps(v, sems, 'vv')
            v.wait_ge(sems.hdl('tv'), tv_b4)
            V(lambda: v.tensor_copy(out=s_bp[:, 0:4], in_=pr[:, 256:260]))
            V(lambda: v.tensor_copy(out=s_onespp[:], in_=pr[:, 320:448]))
        block.vector(_)

    def round():
        rnd = st['rnd']
        if rnd > NRB:
            return False
        last = rnd == NRB

        def _(v, rnd=rnd, last=last, tvt=st['tv']):
            V = EngOps(v, sems, 'vv')
            if tvt is not None:
                v.wait_ge(sems.hdl('tv'), tvt)
            ck = col(cm, 'count_k')
            if rnd > 0:
                V(lambda: v.tensor_scalar(out=s_bp[:, 6:8], in0=pr[:, 260:262],
                                          scalar1=ck, scalar2=None, op0=ALU.is_ge))
                V(lambda: v.tensor_tensor(out=s_mid2[:], in0=s_bp[:, 4:6],
                                          in1=s_bp[:, 0:2], op=ALU.subtract))
                for b in range(B):
                    V(lambda b=b: v.scalar_tensor_tensor(
                        out=s_bp[:, b:b + 1], in0=s_mid2[:, b:b + 1],
                        scalar=s_bp[:, 6 + b:7 + b], in1=s_bp[:, b:b + 1],
                        op0=ALU.mult, op1=ALU.add))
                V(lambda: v.tensor_scalar(out=s_bp[:, 6:8], in0=pr[:, 260:262],
                                          scalar1=ck, scalar2=None, op0=ALU.is_lt))
                V(lambda: v.tensor_tensor(out=s_mid2[:], in0=s_bp[:, 4:6],
                                          in1=s_bp[:, 2:4], op=ALU.subtract))
                for b in range(B):
                    V(lambda b=b: v.scalar_tensor_tensor(
                        out=s_bp[:, 2 + b:3 + b], in0=s_mid2[:, b:b + 1],
                        scalar=s_bp[:, 6 + b:7 + b], in1=s_bp[:, 2 + b:3 + b],
                        op0=ALU.mult, op1=ALU.add))
            if not last:
                V(lambda: v.tensor_tensor(out=s_bp[:, 4:6], in0=s_bp[:, 0:2],
                                          in1=s_bp[:, 2:4], op=ALU.add))
                V(lambda: v.tensor_scalar(out=s_bp[:, 4:6], in0=s_bp[:, 4:6],
                                          scalar1=col(cm, 'half'), scalar2=None, op0=ALU.mult))
                for b in range(B):
                    V(lambda b=b: v.tensor_scalar(
                        out=s_cmp[:], in0=s_wqT[:, 64 * b:64 * b + 64],
                        scalar1=s_bp[:, 4 + b:5 + b],
                        scalar2=col(cm, 'zero'), op0=ALU.is_gt, op1=ALU.add,
                        accum_out=s_cnt2[:, b:b + 1]))
        block.vector(_)
        vv_r = sems.val('vv')

        if not last:
            def _(t, vv_r=vv_r):
                T = EngOps(t, sems, 'tv')
                t.wait_ge(sems.hdl('vv'), vv_r)
                T(lambda: t.matmul(pr[:, 260:262], s_onespp[:], s_cnt2[:, 0:2],
                                   start=True, stop=True))
            block.tensor(_)
            st['tv'] = sems.val('tv')
        st['rnd'] += 1
        return True

    def finish():
        while round():
            pass

    bb['loads'] = loads
    bb['round'] = round
    bb['finish'] = finish
    return bb



def build_router(F=None):
    nc = bass.Bass('TRN2', target_bir_lowering=False, num_devices=NCORE) if F is None else F['nc']
    with ExitStack() as ctx:
        E = ctx.enter_context
        wfull_in = _dio(nc, F, 'wfull', [NCORE, B, 1, L], F32, 'ExternalInput')
        xntm_in = _dio(nc, F, 'xntmfull', [NCORE, B, NT, P, D], BF16, 'ExternalInput')
        pos_in = _dio(nc, F, 'posids', [4, 4096], mybir.dt.int32, 'ExternalInput')
        uincl_in = _dio(nc, F, 'uincl', [P, P], F32, 'ExternalInput')
        jrow_in = _dio(nc, F, 'jrowbc', [P, P], F32, 'ExternalInput')
        selc_in = _dio(nc, F, 'selcols', [P, 65], BF16, 'ExternalInput')
        ucols_in = _dio(nc, F, 'ucols', [P, 32], F32, 'ExternalInput')
        selmat_in = _dio(nc, F, 'selmat', [64, NT], F32, 'ExternalInput')

        selc_o = _dio(nc, F, 'selc', [B, 1, P], F32, 'ExternalOutput')
        wsel_o = _dio(nc, F, 'wsel', [B, 1, P], F32, 'ExternalOutput')
        possel_o = _dio(nc, F, 'possel', [B, 1, P], F32, 'ExternalOutput')
        xsel_o = _dio(nc, F, 'xsel', [B, P, D], BF16, 'ExternalOutput')
        down_o = _dio(nc, F, 'down', [B, NT, P], F32, 'ExternalOutput')
        dcols_o = _dio(nc, F, 'dcols', [B, P, NT], F32, 'ExternalOutput')

        if F is None:
            sems, cm = open_common(nc, ctx)
            block = E(nc.Block())
            emit_common_loads(block, sems, cm)
        else:
            sems, cm, block = F['sems'], F['cm'], F['block']

        s_uincl = E(nc.sbuf_tensor('s_uincl', [P, P], F32))
        s_jrow = E(nc.sbuf_tensor('s_jrow', [P, P], F32))
        s_selcols = E(nc.sbuf_tensor('s_selcols', [P, 65], BF16))
        s_ucols = E(nc.sbuf_tensor('s_ucols', [P, 32], F32))
        s_selmat = E(nc.sbuf_tensor('s_selmat', [64, NT], F32))
        s_pos32 = E(nc.sbuf_tensor('s_pos32', [32, P], mybir.dt.int32))
        s_posf = E(nc.sbuf_tensor('s_posf', [32, P], F32))
        s_posqT = E(nc.sbuf_tensor('s_posqT', [P, 32], F32))
        s_poshi = E(nc.sbuf_tensor('s_poshi', [P, 32], BF16))
        s_poslo = E(nc.sbuf_tensor('s_poslo', [P, 32], BF16))
        s_poslof = E(nc.sbuf_tensor('s_poslof', [P, 32], F32))
        s_cmp = E(nc.sbuf_tensor('s_cmp', [P, 64], F32))
        s_m = E(nc.sbuf_tensor('s_m', [P, 64], F32))
        s_inc = E(nc.sbuf_tensor('s_inc', [P, 64], F32))
        s_cs = E(nc.sbuf_tensor('s_cs', [1, 64], F32))
        s_cs64 = E(nc.sbuf_tensor('s_cs64', [64, 1], F32))
        s_css = E(nc.sbuf_tensor('s_css', [1, 64], F32))
        s_exbc = E(nc.sbuf_tensor('s_exbc', [P, 64], F32))
        s_d = E(nc.sbuf_tensor('s_d', [P, 64], F32))
        s_dT = E(nc.sbuf_tensor('s_dT', [64, P], F32))
        s_down = E(nc.sbuf_tensor('s_down', [NT, P], F32))
        s_ST = E(nc.sbuf_tensor('s_ST', [P, 64 * P], BF16))
        s_xtile = E(nc.sbuf_tensor('s_xtile', [P, 2 * D], BF16))
        s_selcol = E(nc.sbuf_tensor('s_selcol', [P, 1], F32))
        s_selcc = E(nc.sbuf_tensor('s_selcc', [P, 1], F32))
        s_selrow = E(nc.sbuf_tensor('s_selrow', [1, P], F32))
        s_selbc = E(nc.sbuf_tensor('s_selbc', [P, P], F32))
        s_gp = E(nc.sbuf_tensor('s_gp', [P, P], BF16))
        s_out1 = E(nc.sbuf_tensor('s_out1', [P, 1], F32))
        s_xsel = E(nc.sbuf_tensor('s_xsel', [P, D], BF16))

        p_w = E(nc.psum_tensor('p_w', [P, 512], F32))
        p_x = E(nc.psum_tensor('p_x', [P, 1024], F32))
        p_c0 = E(nc.psum_tensor('p_c0', [P, 1], F32))
        p_c1 = E(nc.psum_tensor('p_c1', [P, 1], F32))
        p_c2 = E(nc.psum_tensor('p_c2', [P, 1], F32))
        p_c3 = E(nc.psum_tensor('p_c3', [P, 1], F32))
        p_t64 = p_w[0:64, 128:256]
        p_t8 = p_w[0:NT, 256:384]
        p_t8b = p_w[:, 440:448]

        def _(sync):
            for ap_s, ap_d in ((s_uincl, uincl_in), (s_jrow, jrow_in), (s_ucols, ucols_in),
                               (s_selmat, selmat_in)):
                sync.wait_ge(sems.hdl('ld'), sems.val('ld'))
                sync.dma_start(out=ap_s[:], in_=ap_d).then_inc(*sems.inc('ld', 16))
        block.sync(_)

        if F is not None and 'bis' in F:
            bb = F['bis']
        else:
            bb = emit_router_bis(nc, E, sems, cm, block, wfull_in, p_w)
            bb['loads']()
            bb['finish']()
        s_wqT = bb['s_wqT']
        s_whi = bb['s_whi']
        s_wlo = bb['s_wlo']
        s_bp = bb['s_bp']

        def _(g):
            g.wait_ge(sems.hdl('gd'), sems.val('gd'))
            g.dma_start(out=s_selcols[:], in_=selc_in).then_inc(*sems.inc('gd', 16))
        block.gpsimd(_)
        ld0 = sems.val('ld')
        gd0 = sems.val('gd')

        for eng in ('tensor', 'vector', 'scalar'):
            def _(e):
                e.wait_ge(sems.hdl('ld'), ld0)
                e.wait_ge(sems.hdl('gd'), gd0)
            getattr(block, eng)(_)

        for b in range(B):
            def _(g, b=b):
                g.wait_ge(sems.hdl('vv'), sems.val('vv'))
                g.wait_ge(sems.hdl('gd'), sems.val('gd'))
                g.dma_start(out=s_pos32[:],
                            in_=pos_in[b].rearrange('(u q) -> u q', q=P)).then_inc(*sems.inc('gd', 16))
            block.gpsimd(_)
            gd_pos = sems.val('gd')

            def _(v, b=b):
                V = EngOps(v, sems, 'vv')
                v.wait_ge(sems.hdl('tv'), sems.val('tv'))
                V(lambda: v.tensor_scalar(out=s_m[:], in0=s_wqT[:, 64 * b:64 * b + 64],
                                          scalar1=s_bp[:, 2 + b:3 + b],
                                          scalar2=None, op0=ALU.is_gt))
            block.vector(_)
            vv_m = sems.val('vv')

            def _(t, vv_m=vv_m):
                T = EngOps(t, sems, 'tv')
                t.wait_ge(sems.hdl('vv'), vv_m)
                t.matmul(p_w[:, 0:64], s_uincl[:], s_m[:], start=True, stop=True)
                T(lambda: t.matmul(p_w[0:64, 64:65], s_m[:], cm['onescol_f'][:], start=True, stop=True))
            block.tensor(_)
            tv_cs = sems.val('tv')

            def _(v, tv_cs=tv_cs):
                V = EngOps(v, sems, 'vv')
                v.wait_ge(sems.hdl('tv'), tv_cs)
                V(lambda: v.tensor_copy(out=s_inc[:], in_=p_w[:, 0:64]))
            block.vector(_)
            vv_inc = sems.val('vv')

            def _(v, vv_inc=vv_inc):
                V = EngOps(v, sems, 'vv')
                v.wait_ge(sems.hdl('vv'), vv_inc)
                V(lambda: v.tensor_copy(out=s_cs64[:], in_=p_w[0:64, 64:65]))
            block.vector(_)
            vv_fix = sems.val('vv')

            def _(t, vv_fix=vv_fix):
                T = EngOps(t, sems, 'tv')
                t.wait_ge(sems.hdl('vv'), vv_fix)
                T(lambda: t.transpose(p_w[0:1, 128:192], s_cs64[:], cm['ident_f'][0:64, 0:64]))
            block.tensor(_)
            tv_cst = sems.val('tv')

            def _(v, tv_cst=tv_cst):
                V = EngOps(v, sems, 'vv')
                v.wait_ge(sems.hdl('tv'), tv_cst)
                V(lambda: v.tensor_copy(out=s_cs[0:1, 0:64], in_=p_w[0:1, 128:192]))
                V(lambda: v.tensor_tensor_scan(out=s_css[0:1, 0:64], data0=s_cs[0:1, 0:64],
                                               data1=s_cs[0:1, 0:64], initial=0.0,
                                               op0=ALU.add, op1=ALU.bypass))
                V(lambda: v.tensor_tensor(out=s_css[0:1, 0:64], in0=s_css[0:1, 0:64],
                                          in1=s_cs[0:1, 0:64], op=ALU.subtract))
            block.vector(_)
            vv_scan = sems.val('vv')

            def _(t, vv_scan=vv_scan):
                T = EngOps(t, sems, 'tv')
                t.wait_ge(sems.hdl('vv'), vv_scan)
                T(lambda: t.matmul(p_w[:, 192:256], cm['onesrow_f'][:], s_css[0:1, 0:64],
                                   start=True, stop=True))
            block.tensor(_)
            tv_ex = sems.val('tv')

            def _(v, tv_ex=tv_ex, b=b):
                V = EngOps(v, sems, 'vv')
                v.wait_ge(sems.hdl('tv'), tv_ex)
                V(lambda: v.tensor_copy(out=s_exbc[:], in_=p_w[:, 192:256]))
                V(lambda: v.tensor_tensor(out=s_d[:], in0=s_inc[:], in1=s_exbc[:], op=ALU.add))
                V(lambda: v.tensor_tensor(out=s_d[:], in0=s_d[:], in1=s_m[:], op=ALU.mult))
            block.vector(_)
            vv_d = sems.val('vv')

            def _(t, vv_d=vv_d):
                T = EngOps(t, sems, 'tv')
                t.wait_ge(sems.hdl('vv'), vv_d)
                T(lambda: t.transpose(p_t64, s_d[:], cm['ident_f'][:]))
            block.tensor(_)
            tv_dT = sems.val('tv')

            def _(v, tv_dT=tv_dT):
                V = EngOps(v, sems, 'vv')
                v.wait_ge(sems.hdl('tv'), tv_dT)
                V(lambda: v.tensor_copy(out=s_dT[:], in_=p_t64))
            block.vector(_)
            vv_dT = sems.val('vv')

            def _(t, vv_dT=vv_dT):
                T = EngOps(t, sems, 'tv')
                t.wait_ge(sems.hdl('vv'), vv_dT)
                T(lambda: t.matmul(p_t8, s_selmat[:], s_dT[:], start=True, stop=True))
            block.tensor(_)
            tv_d8 = sems.val('tv')

            def _(v, tv_d8=tv_d8):
                V = EngOps(v, sems, 'vv')
                v.wait_ge(sems.hdl('tv'), tv_d8)
                V(lambda: v.tensor_copy(out=s_down[:], in_=p_t8))
            block.vector(_)
            vv_d8 = sems.val('vv')

            def _(sync, b=b, vv_d8=vv_d8):
                sync.wait_ge(sems.hdl('vv'), vv_d8)
                sync.wait_ge(sems.hdl('st'), sems.val('st'))
                sync.dma_start(out=down_o[b], in_=s_down[:]).then_inc(*sems.inc('st', 16))
            block.sync(_)

            def _(t, vv_d8=vv_d8):
                T = EngOps(t, sems, 'tv')
                t.wait_ge(sems.hdl('vv'), vv_d8)
                T(lambda: t.transpose(p_t8b, s_down[:], cm['ident_f'][0:NT, 0:NT]))
            block.tensor(_)
            tv_dc = sems.val('tv')

            def _(v, tv_dc=tv_dc):
                V = EngOps(v, sems, 'vv')
                v.wait_ge(sems.hdl('tv'), tv_dc)
                V(lambda: v.tensor_copy(out=s_cmp[:, 0:8], in_=p_t8b))
            block.vector(_)
            vv_dc = sems.val('vv')

            def _(sync, b=b, vv_dc=vv_dc):
                sync.wait_ge(sems.hdl('vv'), vv_dc)
                sync.wait_ge(sems.hdl('st'), sems.val('st'))
                sync.dma_start(out=dcols_o[b], in_=s_cmp[:, 0:8]).then_inc(*sems.inc('st', 16))
            block.sync(_)

            def _(v, vv_d=vv_d):
                V = EngOps(v, sems, 'vv')
                v.wait_ge(sems.hdl('vv'), vv_d)
                for ch in range(64):
                    V(lambda ch=ch: v.tensor_scalar(out=s_ST[:, ch * P:(ch + 1) * P],
                                                    in0=s_jrow[:], scalar1=s_d[:, ch:ch + 1],
                                                    scalar2=None, op0=ALU.is_equal))
            block.vector(_)
            vv_ST = sems.val('vv')

            tv_ch = [0, 0]
            for ch in range(64):
                sl = ch % 2
                def _(g, b=b, ch=ch, sl=sl, guard=tv_ch[sl]):
                    g.wait_ge(sems.hdl('tv'), guard)
                    g.wait_ge(sems.hdl('gd'), sems.val('gd'))
                    g.dma_start(out=s_xtile[:, sl * D:(sl + 1) * D],
                                in_=xntm_in[ch % NCORE, b, ch // NCORE]).then_inc(*sems.inc('gd', 16))
                block.gpsimd(_)
                gd_x = sems.val('gd')

                def _(t, ch=ch, sl=sl, gd_x=gd_x, vv_ST=vv_ST):
                    T = EngOps(t, sems, 'tv')
                    t.wait_ge(sems.hdl('gd'), gd_x)
                    t.wait_ge(sems.hdl('vv'), vv_ST)
                    if ch == 0:
                        t.wait_ge(sems.hdl('vv'), sems.val('vv'))
                    st_t = s_ST[:, ch * P:(ch + 1) * P]
                    t.ldweights(st_t)
                    t.matmul(p_x[:, 0:512], st_t, s_xtile[:, sl * D: sl * D + 512],
                             start=(ch == 0), stop=(ch == 63))
                    t.matmul(p_x[:, 512:1024], st_t, s_xtile[:, sl * D + 512: sl * D + 1024],
                             start=(ch == 0), stop=(ch == 63))
                    t.matmul(p_c0[:], st_t, s_selcols[:, 0:1], start=(ch == 0), stop=(ch == 63))
                    t.matmul(p_c1[:], st_t, s_selcols[:, 1 + ch:2 + ch], start=(ch == 0), stop=(ch == 63))
                    t.matmul(p_c2[:], st_t, s_whi[:, 64 * b + ch:64 * b + ch + 1],
                             start=(ch == 0), stop=(ch == 63))
                    T(lambda st_t=st_t, ch=ch: t.matmul(p_c3[:], st_t,
                                                        s_wlo[:, 64 * b + ch:64 * b + ch + 1],
                                                        start=(ch == 0), stop=(ch == 63)))
                block.tensor(_)
                tv_ch[sl] = sems.val('tv')
            tv_gath = sems.val('tv')

            def _(v, b=b, tv_gath=tv_gath):
                V = EngOps(v, sems, 'vv')
                v.wait_ge(sems.hdl('tv'), tv_gath)
                v.wait_ge(sems.hdl('st'), sems.val('st'))
                V(lambda: v.tensor_copy(out=s_xsel[:], in_=p_x[:, 0:1024]))
                V(lambda: v.tensor_copy(out=s_selcol[:], in_=p_c0[:]))
                V(lambda: v.scalar_tensor_tensor(out=s_selcol[:], in0=p_c1[:],
                                                 scalar=col(cm, 'c64'), in1=s_selcol[:],
                                                 op0=ALU.mult, op1=ALU.add))
                V(lambda: v.tensor_copy(out=s_out1[:], in_=p_c2[:]))
                V(lambda: v.tensor_tensor(out=s_out1[:], in0=s_out1[:], in1=p_c3[:], op=ALU.add))
                V(lambda: v.tensor_scalar(out=s_selcc[:], in0=s_selcol[:],
                                          scalar1=col(cm, 'clamp4095'), scalar2=None, op0=ALU.min))
            block.vector(_)
            vv_gath = sems.val('vv')

            def _(sync, b=b, vv_gath=vv_gath):
                sync.wait_ge(sems.hdl('vv'), vv_gath)
                sync.wait_ge(sems.hdl('st'), sems.val('st'))
                sync.dma_start(out=xsel_o[b], in_=s_xsel[:]).then_inc(*sems.inc('st', 16))
                sync.wait_ge(sems.hdl('st'), sems.val('st'))
                sync.dma_start(out=wsel_o[b].rearrange('o p -> p o'), in_=s_out1[:]).then_inc(*sems.inc('st', 16))
            block.sync(_)

            def _(t, vv_gath=vv_gath):
                T = EngOps(t, sems, 'tv')
                t.wait_ge(sems.hdl('vv'), vv_gath)
                T(lambda: t.transpose(p_w[0:1, 256:384], s_selcc[:], cm['ident_f'][:]))
            block.tensor(_)
            tv_scr = sems.val('tv')

            def _(v, tv_scr=tv_scr):
                V = EngOps(v, sems, 'vv')
                v.wait_ge(sems.hdl('tv'), tv_scr)
                V(lambda: v.tensor_copy(out=s_selrow[:], in_=p_w[0:1, 256:384]))
            block.vector(_)
            vv_srow = sems.val('vv')

            def _(sync, b=b, vv_srow=vv_srow):
                sync.wait_ge(sems.hdl('vv'), vv_srow)
                sync.wait_ge(sems.hdl('st'), sems.val('st'))
                sync.dma_start(out=selc_o[b], in_=s_selrow[:]).then_inc(*sems.inc('st', 16))
            block.sync(_)

            def _(t, vv_srow=vv_srow):
                T = EngOps(t, sems, 'tv')
                t.wait_ge(sems.hdl('vv'), vv_srow)
                T(lambda: t.matmul(p_w[:, 256:384], cm['onesrow_f'][:], s_selrow[:],
                                   start=True, stop=True))
            block.tensor(_)
            tv_sbc = sems.val('tv')

            def _(v, b=b, tv_sbc=tv_sbc):
                V = EngOps(v, sems, 'vv')
                v.wait_ge(sems.hdl('tv'), tv_sbc)
                V(lambda: v.tensor_copy(out=s_selbc[:], in_=p_w[:, 256:384]))
                v.wait_ge(sems.hdl('gd'), gd_pos)
                V(lambda: v.tensor_copy(out=s_posf[:], in_=s_pos32[:]))
            block.vector(_)
            vv_sbc = sems.val('vv')


            def _(t, b=b, vv_sbc=vv_sbc):
                T = EngOps(t, sems, 'tv')
                t.wait_ge(sems.hdl('vv'), vv_sbc)
                T(lambda: t.transpose(p_w[:, 384:416], s_posf[0:32, :], cm['ident_f'][0:32, 0:32]))
            block.tensor(_)
            tv_pT = sems.val('tv')

            def _(v, tv_pT=tv_pT):
                V = EngOps(v, sems, 'vv')
                v.wait_ge(sems.hdl('tv'), tv_pT)
                V(lambda: v.tensor_copy(out=s_posqT[:], in_=p_w[:, 384:416]))
                V(lambda: v.tensor_copy(out=s_poshi[:], in_=s_posqT[:]))
                V(lambda: v.tensor_copy(out=s_poslof[:], in_=s_poshi[:]))
                V(lambda: v.tensor_tensor(out=s_poslof[:], in0=s_posqT[:], in1=s_poslof[:],
                                          op=ALU.subtract))
                V(lambda: v.tensor_copy(out=s_poslo[:], in_=s_poslof[:]))
            block.vector(_)
            vv_pos = sems.val('vv')

            for uc in range(32):
                def _(v, uc=uc, vv_pos=vv_pos, tv_prev=sems.val('tv')):
                    V = EngOps(v, sems, 'vv')
                    v.wait_ge(sems.hdl('vv'), vv_pos)
                    v.wait_ge(sems.hdl('tv'), tv_prev)
                    V(lambda: v.tensor_scalar(out=s_gp[:], in0=s_selbc[:],
                                              scalar1=s_ucols[:, uc:uc + 1], scalar2=None,
                                              op0=ALU.is_equal))
                block.vector(_)
                vv_gp = sems.val('vv')

                def _(t, uc=uc, vv_gp=vv_gp):
                    T = EngOps(t, sems, 'tv')
                    t.wait_ge(sems.hdl('vv'), vv_gp)
                    t.ldweights(s_gp[:])
                    t.matmul(p_c0[:], s_gp[:], s_poshi[:, uc:uc + 1],
                             start=(uc == 0), stop=(uc == 31))
                    T(lambda uc=uc: t.matmul(p_c1[:], s_gp[:], s_poslo[:, uc:uc + 1],
                                             start=(uc == 0), stop=(uc == 31)))
                block.tensor(_)
            tv_pos = sems.val('tv')

            def _(v, b=b, tv_pos=tv_pos):
                V = EngOps(v, sems, 'vv')
                v.wait_ge(sems.hdl('tv'), tv_pos)
                v.wait_ge(sems.hdl('st'), sems.val('st'))
                V(lambda: v.tensor_copy(out=s_out1[:], in_=p_c0[:]))
                V(lambda: v.tensor_tensor(out=s_out1[:], in0=s_out1[:], in1=p_c1[:], op=ALU.add))
            block.vector(_)
            vv_psel = sems.val('vv')

            def _(sync, b=b, vv_psel=vv_psel):
                sync.wait_ge(sems.hdl('vv'), vv_psel)
                sync.dma_start(out=possel_o[b].rearrange('o p -> p o'),
                               in_=s_out1[:]).then_inc(*sems.inc('st', 16))
            block.sync(_)

        def _(sync):
            sync.wait_ge(sems.hdl('st'), sems.val('st'))
        block.sync(_)
    return nc


def router_inputs(prep_results, I):
    wfull = np.ascontiguousarray(np.stack([np.asarray(r['wrow'], np.float32)
                                           for r in prep_results]))
    xntmfull = np.ascontiguousarray(np.stack([np.asarray(r['xntm']) for r in prep_results]))
    uincl = np.triu(np.ones((P, P), np.float32))
    ucols = np.ascontiguousarray(
        (np.arange(P)[:, None] + 128.0 * np.arange(32)[None, :]).astype(np.float32))
    import ml_dtypes
    selcols = np.zeros((P, 65), np.float32)
    selcols[:, 0] = np.arange(P)
    for ch in range(64):
        selcols[:, 1 + ch] = 2.0 * ch
    selcols = selcols.astype(ml_dtypes.bfloat16)
    maps = []
    for c in range(NCORE):
        m = dict(common_inputs())
        m['wfull'] = wfull
        m['xntmfull'] = xntmfull
        m['posids'] = np.ascontiguousarray(I['L1_ffn_position_ids'])
        m['uincl'] = uincl
        jbase = c * P
        m['jrowbc'] = np.tile(np.arange(jbase + 1, jbase + P + 1, dtype=np.float32), (P, 1))
        m['selcols'] = selcols
        m['ucols'] = ucols
        sel = np.zeros((64, NT), np.float32)
        for i in range(NT):
            sel[NCORE * i + c, i] = 1.0
        m['selmat'] = sel
        maps.append(m)
    return maps



def build_maskc(F=None):
    nc = bass.Bass('TRN2', target_bir_lowering=False, num_devices=NCORE) if F is None else F['nc']
    with ExitStack() as ctx:
        E = ctx.enter_context
        selc_in = _dio(nc, F, 'selcfull', [NCORE, B, 1, P], F32, 'ExternalInput')
        mvsl_in = _dio(nc, F, 'maskvsl', [B, 4096, 512], F32, 'ExternalInput')
        ucols_in = _dio(nc, F, 'ucols', [P, 32], F32, 'ExternalInput')
        cvsl_o = _dio(nc, F, 'cvsl', [B, 512, 8 * P], F32, 'ExternalOutput')

        if F is None:
            sems, cm = open_common(nc, ctx)
            block = E(nc.Block())
            emit_common_loads(block, sems, cm)
        else:
            sems, cm, block = F['sems'], F['cm'], F['block']

        JW = 8 * P
        s_ucols = E(nc.sbuf_tensor('s_ucols', [P, 32], F32))
        s_selrow = E(nc.sbuf_tensor('s_selrow', [1, JW], F32))
        s_selbc = E(nc.sbuf_tensor('s_selbc', [P, JW], F32))
        s_G = E(nc.sbuf_tensor('s_G', [P, JW], BF16))
        s_M = E(nc.sbuf_tensor('s_M', [P, 2 * 512], BF16))
        s_C = E(nc.sbuf_tensor('s_C', [P, JW], F32))

        p_c0 = E(nc.psum_tensor('p_c0', [P, 1024], F32))
        p_c1 = E(nc.psum_tensor('p_c1', [P, 1024], F32))
        p_c2 = E(nc.psum_tensor('p_c2', [P, 1024], F32))
        p_c3 = E(nc.psum_tensor('p_c3', [P, 1024], F32))
        pcs = [p_c0, p_c1, p_c2, p_c3]
        p_bc = p_c0

        def _(sync):
            sync.wait_ge(sems.hdl('ld'), sems.val('ld'))
            sync.dma_start(out=s_ucols[:], in_=ucols_in).then_inc(*sems.inc('ld', 16))
        block.sync(_)
        ld0 = sems.val('ld')
        gd0 = sems.val('gd')
        for eng in ('tensor', 'vector', 'scalar'):
            def _(e):
                e.wait_ge(sems.hdl('ld'), ld0)
                e.wait_ge(sems.hdl('gd'), gd0)
            getattr(block, eng)(_)

        tv_mch = [0, 0]
        ccw = (F or {}).get('ccw')
        for b in range(B):
            def _(sync, b=b):
                sync.wait_ge(sems.hdl('vv'), sems.val('vv'))
                if ccw is not None:
                    sync.wait_ge(sems.hdl('cc'), ccw)
                sync.wait_ge(sems.hdl('ld'), sems.val('ld'))
                sync.dma_start(out=s_selrow[:].rearrange('o (r p) -> o r p', r=NCORE),
                               in_=selc_in[:, b].rearrange('r o p -> o r p')
                               ).then_inc(*sems.inc('ld', 16))
            block.sync(_)
            ld_s = sems.val('ld')

            def _(t, ld_s=ld_s):
                T = EngOps(t, sems, 'tv')
                t.wait_ge(sems.hdl('ld'), ld_s)
                t.wait_ge(sems.hdl('vv'), sems.val('vv'))
                t.matmul(p_bc[:, 0:512], cm['onesrow_f'][:], s_selrow[0:1, 0:512],
                         start=True, stop=True)
                T(lambda: t.matmul(p_bc[:, 512:1024], cm['onesrow_f'][:], s_selrow[0:1, 512:1024],
                                   start=True, stop=True))
            block.tensor(_)
            tv_bc = sems.val('tv')

            def _(v, tv_bc=tv_bc):
                V = EngOps(v, sems, 'vv')
                v.wait_ge(sems.hdl('tv'), tv_bc)
                V(lambda: v.tensor_copy(out=s_selbc[:], in_=p_bc[:, 0:1024]))
            block.vector(_)
            vv_bc = sems.val('vv')

            for uc in range(32):
                sl = uc % 2
                def _(v, uc=uc, vv_bc=vv_bc, tv_prev=sems.val('tv')):
                    V = EngOps(v, sems, 'vv')
                    v.wait_ge(sems.hdl('vv'), vv_bc)
                    v.wait_ge(sems.hdl('tv'), tv_prev)
                    V(lambda: v.tensor_scalar(out=s_G[:], in0=s_selbc[:],
                                              scalar1=s_ucols[:, uc:uc + 1], scalar2=None,
                                              op0=ALU.is_equal))
                block.vector(_)
                vv_G = sems.val('vv')

                def _(g, b=b, uc=uc, sl=sl, guard=tv_mch[sl]):
                    g.wait_ge(sems.hdl('tv'), guard)
                    g.wait_ge(sems.hdl('gd'), sems.val('gd'))
                    g.dma_start(out=s_M[:, sl * 512:(sl + 1) * 512],
                                in_=mvsl_in[b, uc * P:(uc + 1) * P, :]).then_inc(*sems.inc('gd', 16))
                block.gpsimd(_)
                gd_M = sems.val('gd')

                def _(t, uc=uc, sl=sl, gd_M=gd_M, vv_G=vv_G):
                    T = EngOps(t, sems, 'tv')
                    t.wait_ge(sems.hdl('gd'), gd_M)
                    t.wait_ge(sems.hdl('vv'), vv_G)
                    if uc == 0:
                        t.wait_ge(sems.hdl('vv'), sems.val('vv'))
                    for vt in range(4):
                        mt = s_M[:, sl * 512 + vt * P: sl * 512 + (vt + 1) * P]
                        t.ldweights(mt)
                        t.matmul(pcs[vt][:, 0:512], mt, s_G[:, 0:512],
                                 start=(uc == 0), stop=(uc == 31))
                        op = lambda vt=vt, mt=mt, uc=uc: t.matmul(
                            pcs[vt][:, 512:1024], mt, s_G[:, 512:1024],
                            start=(uc == 0), stop=(uc == 31))
                        if vt == 3:
                            T(op)
                        else:
                            op()
                block.tensor(_)
                tv_mch[sl] = sems.val('tv')
            tv_C = sems.val('tv')

            for vt in range(4):
                def _(v, vt=vt, tv_C=tv_C):
                    V = EngOps(v, sems, 'vv')
                    v.wait_ge(sems.hdl('tv'), tv_C)
                    v.wait_ge(sems.hdl('st'), sems.val('st'))
                    V(lambda: v.tensor_copy(out=s_C[:], in_=pcs[vt][:, 0:1024]))
                block.vector(_)
                vv_C = sems.val('vv')

                def _(sync, b=b, vt=vt, vv_C=vv_C):
                    sync.wait_ge(sems.hdl('vv'), vv_C)
                    sync.dma_start(out=cvsl_o[b, vt * P:(vt + 1) * P, :],
                                   in_=s_C[:]).then_inc(*sems.inc('st', 16))
                block.sync(_)

        def _(sync):
            sync.wait_ge(sems.hdl('st'), sems.val('st'))
        block.sync(_)
    return nc


def maskc_inputs(router_results, I):
    selcfull = np.ascontiguousarray(
        np.concatenate([np.asarray(r['selc'], np.float32) for r in router_results], axis=2))
    ucols = np.ascontiguousarray(
        (np.arange(P)[:, None] + 128.0 * np.arange(32)[None, :]).astype(np.float32))
    mask = I['L1_ffn_attention_mask']
    maps = []
    for c in range(NCORE):
        m = dict(common_inputs())
        m['selcfull'] = selcfull
        m['maskvsl'] = np.ascontiguousarray(mask[0:B, 0, :, c * 512:(c + 1) * 512])
        m['ucols'] = ucols
        maps.append(m)
    return maps



def build_msel(F=None):
    nc = bass.Bass('TRN2', target_bir_lowering=False, num_devices=NCORE) if F is None else F['nc']
    with ExitStack() as ctx:
        E = ctx.enter_context
        JW = 8 * P
        cown_in = _dio(nc, F, 'cown', [B, 512, JW], F32, 'ExternalInput')
        selc_in = _dio(nc, F, 'selcfull', [NCORE, B, 1, P], F32, 'ExternalInput')
        vcols_in = _dio(nc, F, 'vcols', [P, 4], F32, 'ExternalInput')
        padm_in = _dio(nc, F, 'padmask', [P, JW], F32, 'ExternalInput')
        msel_o = _dio(nc, F, 'mselp', [B, JW, JW], F32, 'ExternalOutput')

        if F is None:
            sems, cm = open_common(nc, ctx)
            block = E(nc.Block())
            emit_common_loads(block, sems, cm)
        else:
            sems, cm, block = F['sems'], F['cm'], F['block']

        s_vcols = E(nc.sbuf_tensor('s_vcols', [P, 4], F32))
        s_selrow = E(nc.sbuf_tensor('s_selrow', [1, JW], F32))
        s_selbc = E(nc.sbuf_tensor('s_selbc', [P, JW], F32))
        s_G4 = E(nc.sbuf_tensor('s_G4', [P, 4 * JW], BF16))
        s_C4 = E(nc.sbuf_tensor('s_C4', [P, 4 * JW], BF16))
        s_pad = E(nc.sbuf_tensor('s_pad', [P, JW], F32))
        s_out = E(nc.sbuf_tensor('s_out', [P, JW], F32))

        p_bc = E(nc.psum_tensor('p_bc', [P, 1024], F32))
        p_m = E(nc.psum_tensor('p_m', [P, 1024], F32))

        def _(sync):
            sync.wait_ge(sems.hdl('ld'), sems.val('ld'))
            sync.dma_start(out=s_vcols[:], in_=vcols_in).then_inc(*sems.inc('ld', 16))
            sync.wait_ge(sems.hdl('ld'), sems.val('ld'))
            sync.dma_start(out=s_pad[:], in_=padm_in).then_inc(*sems.inc('ld', 16))
        block.sync(_)
        ld0 = sems.val('ld')
        for eng in ('tensor', 'vector', 'scalar'):
            def _(e):
                e.wait_ge(sems.hdl('ld'), ld0)
                e.wait_ge(sems.hdl('gd'), sems.val('gd'))
            getattr(block, eng)(_)

        for b in range(B):
            def _(sync, b=b):
                sync.wait_ge(sems.hdl('vv'), sems.val('vv'))
                sync.wait_ge(sems.hdl('tv'), sems.val('tv'))
                sync.wait_ge(sems.hdl('ld'), sems.val('ld'))
                sync.dma_start(out=s_selrow[:].rearrange('o (r p) -> o r p', r=NCORE),
                               in_=selc_in[:, b].rearrange('r o p -> o r p')
                               ).then_inc(*sems.inc('ld', 16))
            block.sync(_)
            ld_s = sems.val('ld')

            def _(g, b=b):
                g.wait_ge(sems.hdl('tv'), sems.val('tv'))
                g.wait_ge(sems.hdl('gd'), sems.val('gd'))
                g.dma_start(out=s_C4[:].rearrange('p (c j) -> p c j', c=4),
                            in_=cown_in[b].rearrange('(c p) j -> p c j', p=P)
                            ).then_inc(*sems.inc('gd', 16))
            block.gpsimd(_)
            gd_C = sems.val('gd')

            def _(t, ld_s=ld_s):
                T = EngOps(t, sems, 'tv')
                t.wait_ge(sems.hdl('ld'), ld_s)
                t.wait_ge(sems.hdl('vv'), sems.val('vv'))
                t.matmul(p_bc[:, 0:512], cm['onesrow_f'][:], s_selrow[0:1, 0:512],
                         start=True, stop=True)
                T(lambda: t.matmul(p_bc[:, 512:1024], cm['onesrow_f'][:], s_selrow[0:1, 512:1024],
                                   start=True, stop=True))
            block.tensor(_)
            tv_bc = sems.val('tv')

            def _(v, tv_bc=tv_bc, gd_C=gd_C):
                V = EngOps(v, sems, 'vv')
                v.wait_ge(sems.hdl('tv'), tv_bc)
                v.wait_ge(sems.hdl('gd'), gd_C)
                V(lambda: v.tensor_copy(out=s_selbc[:], in_=p_bc[:, 0:1024]))
                for vc in range(4):
                    V(lambda vc=vc: v.tensor_scalar(out=s_G4[:, vc * JW:(vc + 1) * JW],
                                                    in0=s_selbc[:], scalar1=s_vcols[:, vc:vc + 1],
                                                    scalar2=None, op0=ALU.is_equal))
            block.vector(_)
            vv_G = sems.val('vv')

            for jt in range(NT):
                def _(t, jt=jt, vv_G=vv_G):
                    T = EngOps(t, sems, 'tv')
                    t.wait_ge(sems.hdl('vv'), vv_G)
                    t.wait_ge(sems.hdl('vv'), sems.val('vv'))
                    for vc in range(4):
                        ct = s_C4[:, vc * JW + jt * P: vc * JW + (jt + 1) * P]
                        t.ldweights(ct)
                        t.matmul(p_m[:, 0:512], ct, s_G4[:, vc * JW: vc * JW + 512],
                                 start=(vc == 0), stop=(vc == 3))
                        op = lambda vc=vc, ct=ct: t.matmul(
                            p_m[:, 512:1024], ct, s_G4[:, vc * JW + 512: vc * JW + 1024],
                            start=(vc == 0), stop=(vc == 3))
                        if vc == 3:
                            T(op)
                        else:
                            op()
                block.tensor(_)
                tv_m = sems.val('tv')

                def _(v, tv_m=tv_m):
                    V = EngOps(v, sems, 'vv')
                    v.wait_ge(sems.hdl('tv'), tv_m)
                    v.wait_ge(sems.hdl('st'), sems.val('st'))
                    V(lambda: v.tensor_tensor(out=s_out[:], in0=p_m[:, 0:1024], in1=s_pad[:],
                                              op=ALU.add))
                block.vector(_)
                vv_m = sems.val('vv')

                def _(sync, b=b, jt=jt, vv_m=vv_m):
                    sync.wait_ge(sems.hdl('vv'), vv_m)
                    sync.dma_start(out=msel_o[b, jt * P:(jt + 1) * P, :],
                                   in_=s_out[:]).then_inc(*sems.inc('st', 16))
                block.sync(_)

        def _(sync):
            sync.wait_ge(sems.hdl('st'), sems.val('st'))
        block.sync(_)
    return nc


def msel_inputs(maskc_results, router_results, I):
    JW = 8 * P
    cfull = np.ascontiguousarray(
        np.concatenate([np.asarray(r['cvsl'], np.float32) for r in maskc_results], axis=1))
    selcfull = np.ascontiguousarray(
        np.concatenate([np.asarray(r['selc'], np.float32) for r in router_results], axis=2))
    ucols = np.ascontiguousarray(
        (np.arange(P)[:, None] + 128.0 * np.arange(32)[None, :]).astype(np.float32))
    padmask = np.zeros((P, JW), np.float32)
    padmask[:, JW - 1] = -1e9
    maps = []
    for c in range(NCORE):
        m = dict(common_inputs())
        m['cfull'] = np.ascontiguousarray(cfull[:, :, c * P:(c + 1) * P])
        m['selcfull'] = selcfull
        m['ucols'] = ucols
        m['jslice'] = np.zeros((1, 4), np.float32)
        m['padmask'] = padmask
        maps.append(m)
    return maps



def build_subattn(F=None):
    nc = bass.Bass('TRN2', target_bir_lowering=False, num_devices=NCORE) if F is None else F['nc']
    with ExitStack() as ctx:
        E = ctx.enter_context
        JW = 8 * P
        xs_in = _dio(nc, F, 'xself', [NCORE, B, P, D], BF16, 'ExternalInput')
        msel_in = _dio(nc, F, 'mselfull', [NCORE, B, P, JW], F32, 'ExternalInput')
        pos_in = _dio(nc, F, 'posfull', [NCORE, B, 1, P], F32, 'ExternalInput')
        wq_in = _dio(nc, F, 'wqsl', [D, P], F32, 'ExternalInput')
        wk_in = _dio(nc, F, 'wksl', [D, P], F32, 'ExternalInput')
        wv_in = _dio(nc, F, 'wvsl', [D, P], F32, 'ExternalInput')
        wo_in = _dio(nc, F, 'wosl', [P, D], F32, 'ExternalInput')
        ln1_in = _dio(nc, F, 'ln1', [1, D], F32, 'ExternalInput')
        inv_in = _dio(nc, F, 'invbc', [P, 32], F32, 'ExternalInput')
        part_o = _dio(nc, F, 'partial', [NT, B, P, D], F32, 'ExternalOutput')

        if F is None:
            sems, cm = open_common(nc, ctx)
            block = E(nc.Block())
            emit_common_loads(block, sems, cm)
        else:
            sems, cm, block = F['sems'], F['cm'], F['block']

        s_xs = E(nc.sbuf_tensor('s_xs', [P, NT * D], BF16))
        s_sq = E(nc.sbuf_tensor('s_sq', [P, D], F32))
        s_hn = [E(nc.sbuf_tensor('s_hn%d' % p, [P, D], BF16)) for p in range(2)]
        s_hnfm = E(nc.sbuf_tensor('s_hnfm', [P, DC * JW], BF16))
        s_ln1b = E(nc.sbuf_tensor('s_ln1b', [P, D], F32))
        s_ln1r = E(nc.sbuf_tensor('s_ln1r', [1, D], F32))
        s_inv = E(nc.sbuf_tensor('s_inv', [P, 32], F32))
        s_wq = E(nc.sbuf_tensor('s_wq', [P, DC * P], BF16))
        s_wk = E(nc.sbuf_tensor('s_wk', [P, DC * P], BF16))
        s_wv = E(nc.sbuf_tensor('s_wv', [P, DC * P], BF16))
        s_wo = E(nc.sbuf_tensor('s_wo', [P, D], BF16))
        s_q = E(nc.sbuf_tensor('s_q', [P, NT * P], F32))
        s_k = E(nc.sbuf_tensor('s_k', [P, NT * P], F32))
        s_v = E(nc.sbuf_tensor('s_v', [P, NT * P], BF16))
        s_qr = [E(nc.sbuf_tensor('s_qr%d' % p, [P, P], BF16)) for p in range(2)]
        s_kr = [E(nc.sbuf_tensor('s_kr%d' % p, [P, P], BF16)) for p in range(2)]
        s_qf = E(nc.sbuf_tensor('s_qf', [64, 2 * NT * P], BF16))
        s_kf = E(nc.sbuf_tensor('s_kf', [64, 2 * NT * P], BF16))
        s_pos8 = E(nc.sbuf_tensor('s_pos8', [NT, P], F32))
        s_posc = E(nc.sbuf_tensor('s_posc', [P, NT], F32))
        s_ang = E(nc.sbuf_tensor('s_ang', [P, 9 * 32], F32))
        s_S = E(nc.sbuf_tensor('s_S', [P, JW], F32))
        s_msel = E(nc.sbuf_tensor('s_msel', [P, JW], F32))
        s_l = E(nc.sbuf_tensor('s_l', [P, 4], F32))
        s_P = E(nc.sbuf_tensor('s_P', [P, JW], BF16))
        s_PT = E(nc.sbuf_tensor('s_PT', [P, 2 * NT * JW], BF16))
        s_of = E(nc.sbuf_tensor('s_of', [P, JW], F32))
        s_ofb = E(nc.sbuf_tensor('s_ofb', [P, JW], BF16))
        s_out = [E(nc.sbuf_tensor('s_outp%d' % p, [P, D], F32)) for p in range(2)]

        p_a = [E(nc.psum_tensor('p_a%d' % p, [P, 1024], F32)) for p in range(2)]
        p_PT = E(nc.psum_tensor('p_PT', [P, 1024], BF16))
        p_O = E(nc.psum_tensor('p_O', [64, 1024], F32))
        p_tr8 = E(nc.psum_tensor('p_tr8', [P, 1024], BF16))
        p_p8 = p_a[0][:, 0:NT]

        def _(g):
            for wsb, wdr in ((s_wq, wq_in), (s_wk, wk_in), (s_wv, wv_in)):
                g.wait_ge(sems.hdl('gd'), sems.val('gd'))
                g.dma_start(out=wsb[:].rearrange('p (c q) -> p c q', c=DC),
                            in_=wdr.rearrange('(c p) q -> p c q', p=P)).then_inc(*sems.inc('gd', 16))
            g.wait_ge(sems.hdl('gd'), sems.val('gd'))
            g.dma_start(out=s_wo[:], in_=wo_in).then_inc(*sems.inc('gd', 16))
        block.gpsimd(_)

        def _(sync):
            sync.wait_ge(sems.hdl('ld'), sems.val('ld'))
            sync.dma_start(out=s_ln1r[:], in_=ln1_in).then_inc(*sems.inc('ld', 16))
            sync.wait_ge(sems.hdl('ld'), sems.val('ld'))
            sync.dma_start(out=s_inv[:], in_=inv_in).then_inc(*sems.inc('ld', 16))
        block.sync(_)
        ld0 = sems.val('ld')
        gd0 = sems.val('gd')
        for eng in ('tensor', 'vector', 'scalar'):
            def _(e):
                e.wait_ge(sems.hdl('ld'), ld0)
                e.wait_ge(sems.hdl('gd'), gd0)
            getattr(block, eng)(_)

        def _(t):
            T = EngOps(t, sems, 'tv')
            t.matmul(p_a[0][:, 0:512], cm['onesrow_f'][:], s_ln1r[0:1, 0:512], start=True, stop=True)
            T(lambda: t.matmul(p_a[0][:, 512:1024], cm['onesrow_f'][:], s_ln1r[0:1, 512:1024],
                               start=True, stop=True))
        block.tensor(_)
        tv_ln = sems.val('tv')

        def _(v, tv_ln=tv_ln):
            V = EngOps(v, sems, 'vv')
            v.wait_ge(sems.hdl('tv'), tv_ln)
            V(lambda: v.tensor_copy(out=s_ln1b[:], in_=p_a[0][:, 0:1024]))
        block.vector(_)
        mkS = {'tr_vv': sems.val('vv'), 'hnfree': [sems.val('tv')] * 2,
               'qkv_vv': [sems.val('vv')] * 2, 'ropefree': [sems.val('tv')] * 2,
               'out_st': [sems.val('st')] * 2}

        for b in range(B):
            def _(sync, b=b):
                sync.wait_ge(sems.hdl('tv'), sems.val('tv'))
                sync.wait_ge(sems.hdl('sv'), sems.val('sv'))
                sync.wait_ge(sems.hdl('ld'), sems.val('ld'))
                sync.dma_start(out=s_xs[:].rearrange('p (t d) -> p t d', t=NT),
                               in_=xs_in[:, b].rearrange('t p d -> p t d')
                               ).then_inc(*sems.inc('ld', 16))
                sync.wait_ge(sems.hdl('ld'), sems.val('ld'))
                sync.dma_start(out=s_pos8[:], in_=pos_in[:, b, 0]
                               ).then_inc(*sems.inc('ld', 16))
            block.sync(_)
            ld_b = sems.val('ld')

            def _(t, ld_b=ld_b):
                T = EngOps(t, sems, 'tv')
                t.wait_ge(sems.hdl('ld'), ld_b)
                t.wait_ge(sems.hdl('vv'), sems.val('vv'))
                T(lambda: t.transpose(p_p8[:], s_pos8[:], cm['ident_f'][0:NT, 0:NT]))
            block.tensor(_)
            tv_p8 = sems.val('tv')

            def _(v, tv_p8=tv_p8):
                V = EngOps(v, sems, 'vv')
                v.wait_ge(sems.hdl('tv'), tv_p8)
                V(lambda: v.tensor_copy(out=s_posc[:], in_=p_p8[:]))
            block.vector(_)

            for jt in range(NT):
                def _(s, jt=jt, ld_b=ld_b):
                    S = EngOps(s, sems, 'sv')
                    s.wait_ge(sems.hdl('ld'), ld_b)
                    s.wait_ge(sems.hdl('vv'), sems.val('vv'))
                    S(lambda: s.activation(out=s_sq[:], in_=s_xs[:, jt * D:(jt + 1) * D],
                                           func=AF.Square, bias=col(cm, 'zero'), scale=1.0,
                                           accum_out=s_l[:, 0:1]))
                    S(lambda: s.activation(out=s_l[:, 1:2], in_=s_l[:, 0:1], func=AF.Sqrt,
                                           bias=col(cm, 'eps_rms'), scale=1.0 / D))
                block.scalar(_)
                sv_r = sems.val('sv')

                pr = jt % 2
                def _(v, jt=jt, pr=pr, sv_r=sv_r, hf=mkS['hnfree'][pr]):
                    V = EngOps(v, sems, 'vv')
                    v.wait_ge(sems.hdl('sv'), sv_r)
                    v.wait_ge(sems.hdl('tv'), hf)
                    V(lambda: v.reciprocal(out=s_l[:, 2:3], in_=s_l[:, 1:2]))
                    V(lambda: v.tensor_scalar(out=s_hn[pr][:], in0=s_xs[:, jt * D:(jt + 1) * D],
                                              scalar1=s_l[:, 2:3], scalar2=None, op0=ALU.mult))
                    V(lambda: v.tensor_tensor(out=s_hn[pr][:], in0=s_hn[pr][:], in1=s_ln1b[:],
                                              op=ALU.mult))
                block.vector(_)
                vv_hn = sems.val('vv')

                def _(t, jt=jt, pr=pr, vv_hn=vv_hn, gtr=mkS['tr_vv']):
                    T = EngOps(t, sems, 'tv')
                    t.wait_ge(sems.hdl('vv'), vv_hn)
                    t.wait_ge(sems.hdl('vv'), gtr)
                    for dc in range(DC):
                        op = lambda dc=dc: t.transpose(p_tr8[:, dc * P:(dc + 1) * P],
                                                       s_hn[pr][:, dc * P:(dc + 1) * P],
                                                       cm['ident_b'][:])
                        if dc == DC - 1:
                            T(op)
                        else:
                            op()
                block.tensor(_)
                mkS['hnfree'][pr] = sems.val('tv')
                tv_tr8 = sems.val('tv')

                def _(v, jt=jt, tv_tr8=tv_tr8):
                    V = EngOps(v, sems, 'vv')
                    v.wait_ge(sems.hdl('tv'), tv_tr8)
                    V(lambda: v.tensor_copy(
                        out=s_hnfm[:].rearrange('p (c t) -> p c t', c=DC)[:, :, jt * P:(jt + 1) * P],
                        in_=p_tr8[:].rearrange('p (c q) -> p c q', c=DC)))
                block.vector(_)
                mkS['tr_vv'] = sems.val('vv')
            vv_fm = sems.val('vv')

            for jt in range(NT):
                pr = jt % 2
                def _(t, jt=jt, pr=pr, vv_fm=vv_fm, gq=mkS['qkv_vv'][pr]):
                    T = EngOps(t, sems, 'tv')
                    t.wait_ge(sems.hdl('vv'), vv_fm)
                    t.wait_ge(sems.hdl('vv'), gq)
                    for wsb, off in ((s_wq, 0), (s_wk, 256), (s_wv, 512)):
                        for dc in range(DC):
                            hn_t = s_hnfm[:, dc * JW + jt * P: dc * JW + (jt + 1) * P]
                            t.ldweights(hn_t)
                            op = lambda dc=dc, wsb=wsb, off=off, hn_t=hn_t: t.matmul(
                                p_a[pr][:, off:off + P], hn_t, wsb[:, dc * P:(dc + 1) * P],
                                start=(dc == 0), stop=(dc == DC - 1))
                            if dc == DC - 1:
                                T(op)
                            else:
                                op()
                block.tensor(_)
                tv_qkv = sems.val('tv')

                def _(v, jt=jt, pr=pr, tv_qkv=tv_qkv):
                    V = EngOps(v, sems, 'vv')
                    v.wait_ge(sems.hdl('tv'), tv_qkv)
                    V(lambda: v.tensor_copy(out=s_q[:, jt * P:(jt + 1) * P], in_=p_a[pr][:, 0:P]))
                    V(lambda: v.tensor_copy(out=s_k[:, jt * P:(jt + 1) * P], in_=p_a[pr][:, 256:256 + P]))
                    V(lambda: v.tensor_copy(out=s_v[:, jt * P:(jt + 1) * P], in_=p_a[pr][:, 512:512 + P]))
                    V(lambda: v.tensor_scalar(out=s_ang[:, 0:32], in0=s_inv[:],
                                              scalar1=s_posc[:, jt:jt + 1], scalar2=None, op0=ALU.mult))
                    V(lambda: v.tensor_scalar(out=s_ang[:, 32:64], in0=s_ang[:, 0:32],
                                              scalar1=col(cm, 'inv_2pi'), scalar2=col(cm, 'magic'),
                                              op0=ALU.mult, op1=ALU.add))
                    V(lambda: v.tensor_scalar(out=s_ang[:, 32:64], in0=s_ang[:, 32:64],
                                              scalar1=col(cm, 'magic'), scalar2=None, op0=ALU.subtract))
                    V(lambda: v.scalar_tensor_tensor(out=s_ang[:, 64:96], in0=s_ang[:, 32:64],
                                                     scalar=col(cm, 'neg_chi'), in1=s_ang[:, 0:32],
                                                     op0=ALU.mult, op1=ALU.add))
                    V(lambda: v.scalar_tensor_tensor(out=s_ang[:, 64:96], in0=s_ang[:, 32:64],
                                                     scalar=col(cm, 'neg_cmid'), in1=s_ang[:, 64:96],
                                                     op0=ALU.mult, op1=ALU.add))
                    V(lambda: v.tensor_scalar(out=s_ang[:, 96:128], in0=s_ang[:, 0:32],
                                              scalar1=col(cm, 'half_pi'), scalar2=None, op0=ALU.add))
                    V(lambda: v.tensor_scalar(out=s_ang[:, 128:160], in0=s_ang[:, 96:128],
                                              scalar1=col(cm, 'inv_2pi'), scalar2=col(cm, 'magic'),
                                              op0=ALU.mult, op1=ALU.add))
                    V(lambda: v.tensor_scalar(out=s_ang[:, 128:160], in0=s_ang[:, 128:160],
                                              scalar1=col(cm, 'magic'), scalar2=None, op0=ALU.subtract))
                    V(lambda: v.scalar_tensor_tensor(out=s_ang[:, 160:192], in0=s_ang[:, 128:160],
                                                     scalar=col(cm, 'neg_chi'), in1=s_ang[:, 96:128],
                                                     op0=ALU.mult, op1=ALU.add))
                    V(lambda: v.scalar_tensor_tensor(out=s_ang[:, 160:192], in0=s_ang[:, 128:160],
                                                     scalar=col(cm, 'neg_cmid'), in1=s_ang[:, 160:192],
                                                     op0=ALU.mult, op1=ALU.add))
                block.vector(_)
                mkS['qkv_vv'][pr] = sems.val('vv')
                vv_ang = sems.val('vv')

                def _(s, vv_ang=vv_ang):
                    S = EngOps(s, sems, 'sv')
                    s.wait_ge(sems.hdl('vv'), vv_ang)
                    S(lambda: s.activation(out=s_ang[:, 192:224], in_=s_ang[:, 64:96],
                                           func=AF.Sin, bias=col(cm, 'zero'), scale=1.0))
                    S(lambda: s.activation(out=s_ang[:, 224:256], in_=s_ang[:, 160:192],
                                           func=AF.Sin, bias=col(cm, 'zero'), scale=1.0))
                block.scalar(_)
                sv_sc = sems.val('sv')

                def _(v, jt=jt, pr=pr, sv_sc=sv_sc, rf=mkS['ropefree'][pr]):
                    V = EngOps(v, sems, 'vv')
                    v.wait_ge(sems.hdl('sv'), sv_sc)
                    v.wait_ge(sems.hdl('tv'), rf)
                    sin = s_ang[:, 192:224]
                    cos = s_ang[:, 224:256]
                    tmp = s_ang[:, 256:288]
                    for (srct, dst) in ((s_q, s_qr[pr]), (s_k, s_kr[pr])):
                        for ho in (0, 64):
                            a1 = srct[:, jt * P + ho: jt * P + ho + 32]
                            a2 = srct[:, jt * P + ho + 32: jt * P + ho + 64]
                            o1 = dst[:, ho: ho + 32]
                            o2 = dst[:, ho + 32: ho + 64]
                            V(lambda a2=a2, o2=o2: v.tensor_tensor(out=o2, in0=a2, in1=cos, op=ALU.mult))
                            V(lambda a1=a1: v.tensor_tensor(out=tmp, in0=a1, in1=sin, op=ALU.mult))
                            V(lambda o2=o2: v.tensor_tensor(out=o2, in0=o2, in1=tmp, op=ALU.add))
                            V(lambda a1=a1, o1=o1: v.tensor_tensor(out=o1, in0=a1, in1=cos, op=ALU.mult))
                            V(lambda a2=a2: v.tensor_tensor(out=tmp, in0=a2, in1=sin, op=ALU.mult))
                            V(lambda o1=o1: v.tensor_tensor(out=o1, in0=o1, in1=tmp, op=ALU.subtract))
                block.vector(_)
                vv_rope = sems.val('vv')

                def _(t, jt=jt, pr=pr, vv_rope=vv_rope, gtr=mkS['tr_vv']):
                    T = EngOps(t, sems, 'tv')
                    t.wait_ge(sems.hdl('vv'), vv_rope)
                    t.wait_ge(sems.hdl('vv'), gtr)
                    sls = [(s_qr[pr], 0), (s_qr[pr], 1), (s_kr[pr], 2), (s_kr[pr], 3)]
                    for srct, sx in sls:
                        h = sx % 2
                        op = lambda srct=srct, sx=sx, h=h: t.transpose(
                            p_tr8[0:64, sx * P:(sx + 1) * P],
                            srct[:, h * 64:(h + 1) * 64], cm['ident_b'][:])
                        if sx == 3:
                            T(op)
                        else:
                            op()
                block.tensor(_)
                mkS['ropefree'][pr] = sems.val('tv')
                tv_rt = sems.val('tv')

                def _(v, jt=jt, tv_rt=tv_rt):
                    V = EngOps(v, sems, 'vv')
                    v.wait_ge(sems.hdl('tv'), tv_rt)
                    V(lambda: v.tensor_copy(
                        out=s_qf[0:64, :].rearrange('p (h t) -> p h t', h=2)[:, :, jt * P:(jt + 1) * P],
                        in_=p_tr8[0:64, 0:256].rearrange('p (h q) -> p h q', h=2)))
                    V(lambda: v.tensor_copy(
                        out=s_kf[0:64, :].rearrange('p (h t) -> p h t', h=2)[:, :, jt * P:(jt + 1) * P],
                        in_=p_tr8[0:64, 256:512].rearrange('p (h q) -> p h q', h=2)))
                block.vector(_)
                mkS['tr_vv'] = sems.val('vv')
            vv_qkf = sems.val('vv')

            ccwS = (F or {}).get('ccw')
            for jt in range(NT):
                def _(sync, b=b, jt=jt):
                    sync.wait_ge(sems.hdl('vv'), sems.val('vv'))
                    if ccwS is not None:
                        sync.wait_ge(sems.hdl('cc'), ccwS)
                    sync.wait_ge(sems.hdl('ld'), sems.val('ld'))
                    sync.dma_start(out=s_msel[:], in_=msel_in[jt, b]
                                   ).then_inc(*sems.inc('ld', 16))
                block.sync(_)
                ld_m = sems.val('ld')

                for h in range(2):
                    spr = (jt * 2 + h) % 2
                    def _(t, jt=jt, h=h, spr=spr, vv_qkf=vv_qkf, gsc=mkS['qkv_vv'][spr]):
                        T = EngOps(t, sems, 'tv')
                        t.wait_ge(sems.hdl('vv'), vv_qkf)
                        t.wait_ge(sems.hdl('sv'), sems.val('sv'))
                        t.wait_ge(sems.hdl('vv'), gsc)
                        qt = s_qf[0:64, h * NT * P + jt * P: h * NT * P + (jt + 1) * P]
                        t.ldweights(qt)
                        t.matmul(p_a[spr][:, 0:512], qt, s_kf[0:64, h * NT * P: h * NT * P + 512],
                                 start=True, stop=True)
                        T(lambda qt=qt, h=h, spr=spr: t.matmul(
                            p_a[spr][:, 512:1024], qt,
                            s_kf[0:64, h * NT * P + 512: h * NT * P + 1024],
                            start=True, stop=True))
                    block.tensor(_)
                    tv_sc = sems.val('tv')

                    def _(s, spr=spr, tv_sc=tv_sc):
                        S = EngOps(s, sems, 'sv')
                        s.wait_ge(sems.hdl('tv'), tv_sc)
                        s.wait_ge(sems.hdl('vv'), sems.val('vv'))
                        S(lambda: s.activation(out=s_S[:], in_=p_a[spr][:, 0:1024], func=AF.Copy,
                                               bias=0.0, scale=0.125))
                    block.scalar(_)
                    sv_ev = sems.val('sv')

                    def _(v, ld_m=ld_m, sv_ev=sv_ev):
                        V = EngOps(v, sems, 'vv')
                        v.wait_ge(sems.hdl('sv'), sv_ev)
                        v.wait_ge(sems.hdl('ld'), ld_m)
                        V(lambda: v.tensor_tensor(out=s_S[:], in0=s_S[:], in1=s_msel[:], op=ALU.add))
                        V(lambda: v.reduce_max(out=s_l[:, 0:1], in_=s_S[:], axis=AX.X, negate=True))
                    block.vector(_)
                    vv_mx = sems.val('vv')

                    def _(s, vv_mx=vv_mx):
                        S = EngOps(s, sems, 'sv')
                        s.wait_ge(sems.hdl('vv'), vv_mx)
                        S(lambda: s.activation(out=s_P[:], in_=s_S[:], func=AF.Exp,
                                               bias=s_l[:, 0:1], scale=1.0, accum_out=s_l[:, 1:2]))
                    block.scalar(_)
                    sv_P = sems.val('sv')

                    def _(v, sv_P=sv_P, tv_prev=sems.val('tv')):
                        V = EngOps(v, sems, 'vv')
                        v.wait_ge(sems.hdl('sv'), sv_P)
                        v.wait_ge(sems.hdl('tv'), tv_prev)
                        V(lambda: v.reciprocal(out=s_l[:, 2:3], in_=s_l[:, 1:2]))
                        V(lambda: v.tensor_scalar(out=s_P[:], in0=s_P[:], scalar1=s_l[:, 2:3],
                                                  scalar2=None, op0=ALU.mult))
                    block.vector(_)
                    mkS['qkv_vv'][spr] = sems.val('vv')
                    vv_Pn = sems.val('vv')

                    def _(t, vv_Pn=vv_Pn):
                        T = EngOps(t, sems, 'tv')
                        t.wait_ge(sems.hdl('vv'), vv_Pn)
                        for kc in range(NT):
                            op = lambda kc=kc: t.transpose(p_PT[:, kc * P:(kc + 1) * P],
                                                           s_P[:, kc * P:(kc + 1) * P],
                                                           cm['ident_b'][:])
                            if kc == NT - 1:
                                T(op)
                            else:
                                op()
                    block.tensor(_)
                    tv_PT = sems.val('tv')

                    def _(v, jt=jt, h=h, tv_PT=tv_PT):
                        V = EngOps(v, sems, 'vv')
                        v.wait_ge(sems.hdl('tv'), tv_PT)
                        reg = s_PT[:, h * NT * JW: (h + 1) * NT * JW]
                        V(lambda: v.tensor_copy(
                            out=reg.rearrange('p (k q) -> p k q', k=NT)[:, :, jt * P:(jt + 1) * P],
                            in_=p_PT[:].rearrange('p (k q) -> p k q', k=NT)))
                    block.vector(_)
            vv_PTall = sems.val('vv')

            for h in range(2):
                def _(t, h=h, vv_PTall=vv_PTall):
                    T = EngOps(t, sems, 'tv')
                    t.wait_ge(sems.hdl('vv'), vv_PTall)
                    t.wait_ge(sems.hdl('vv'), sems.val('vv'))
                    for kc in range(NT):
                        vt = s_v[:, kc * P + h * 64: kc * P + h * 64 + 64]
                        base = h * NT * JW + kc * JW
                        t.ldweights(vt)
                        t.matmul(p_O[:, 0:512], vt, s_PT[:, base: base + 512],
                                 start=(kc == 0), stop=(kc == NT - 1))
                        op = lambda kc=kc, vt=vt, base=base: t.matmul(
                            p_O[:, 512:1024], vt, s_PT[:, base + 512: base + 1024],
                            start=(kc == 0), stop=(kc == NT - 1))
                        if kc == NT - 1:
                            T(op)
                        else:
                            op()
                block.tensor(_)
                tv_O = sems.val('tv')

                def _(v, h=h, tv_O=tv_O):
                    V = EngOps(v, sems, 'vv')
                    v.wait_ge(sems.hdl('tv'), tv_O)
                    V(lambda: v.tensor_copy(out=s_of[h * 64:(h + 1) * 64, :], in_=p_O[:, 0:1024]))
                block.vector(_)
            vv_O = sems.val('vv')

            def _(v, vv_O=vv_O):
                V = EngOps(v, sems, 'vv')
                v.wait_ge(sems.hdl('vv'), vv_O)
                V(lambda: v.tensor_copy(out=s_ofb[:], in_=s_of[:]))
            block.vector(_)
            vv_ofb = sems.val('vv')

            for jt in range(NT):
                pr = jt % 2
                def _(t, jt=jt, pr=pr, vv_ofb=vv_ofb, gq=mkS['qkv_vv'][pr]):
                    T = EngOps(t, sems, 'tv')
                    t.wait_ge(sems.hdl('vv'), vv_ofb)
                    t.wait_ge(sems.hdl('vv'), gq)
                    ot = s_ofb[:, jt * P:(jt + 1) * P]
                    t.ldweights(ot)
                    t.matmul(p_a[pr][:, 0:512], ot, s_wo[:, 0:512], start=True, stop=True)
                    T(lambda ot=ot, pr=pr: t.matmul(p_a[pr][:, 512:1024], ot, s_wo[:, 512:1024],
                                                    start=True, stop=True))
                block.tensor(_)
                tv_op = sems.val('tv')

                def _(v, pr=pr, tv_op=tv_op, st_pr=mkS['out_st'][pr]):
                    V = EngOps(v, sems, 'vv')
                    v.wait_ge(sems.hdl('tv'), tv_op)
                    v.wait_ge(sems.hdl('st'), st_pr)
                    V(lambda: v.tensor_copy(out=s_out[pr][:], in_=p_a[pr][:, 0:1024]))
                block.vector(_)
                mkS['qkv_vv'][pr] = sems.val('vv')
                vv_op = sems.val('vv')

                def _(sync, b=b, jt=jt, pr=pr, vv_op=vv_op, st_c=sems.val('st')):
                    sync.wait_ge(sems.hdl('vv'), vv_op)
                    sync.wait_ge(sems.hdl('st'), st_c)
                    sync.dma_start(out=part_o[jt, b], in_=s_out[pr][:]).then_inc(*sems.inc('st', 16))
                block.sync(_)
                mkS['out_st'][pr] = sems.val('st')

        def _(sync):
            sync.wait_ge(sems.hdl('st'), sems.val('st'))
        block.sync(_)
    return nc


def subattn_inputs(router_results, msel_results, I):
    JW = 8 * P
    xself = np.ascontiguousarray(
        np.stack([np.asarray(r['xsel']) for r in router_results], axis=0))
    mselfull = np.ascontiguousarray(
        np.stack([np.asarray(r['mselsl'], np.float32) for r in msel_results], axis=0))
    posfull = np.ascontiguousarray(
        np.stack([np.asarray(r['possel'], np.float32) for r in router_results], axis=0))
    inv = (1.0 / 10000.0 ** (np.arange(0, 64, 2, dtype=np.float32) / 64)).astype(np.float32)
    invbc = np.tile(inv, (P, 1))
    maps = []
    for c in range(NCORE):
        m = dict(common_inputs())
        m['xself'] = xself
        m['mselfull'] = mselfull
        m['posfull'] = posfull
        hc0 = 2 * c * 64
        m['wqsl'] = np.ascontiguousarray(I['L1_ffn_wq'][:, hc0:hc0 + P])
        m['wksl'] = np.ascontiguousarray(I['L1_ffn_wk'][:, hc0:hc0 + P])
        m['wvsl'] = np.ascontiguousarray(I['L1_ffn_wv'][:, hc0:hc0 + P])
        m['wosl'] = np.ascontiguousarray(I['L1_ffn_wo'][hc0:hc0 + P, :])
        m['ln1'] = I['L1_ffn_ln1'].reshape(1, D)
        m['invbc'] = invbc
        maps.append(m)
    return maps



def build_subffn(F=None):
    nc = bass.Bass('TRN2', target_bir_lowering=False, num_devices=NCORE) if F is None else F['nc']
    with ExitStack() as ctx:
        E = ctx.enter_context
        F4 = 4096
        FC = F4 // 512
        part_in = _dio(nc, F, 'partsum', [B, P, D], F32, 'ExternalInput')
        xsl_in = _dio(nc, F, 'xselown', [B, P, D], BF16, 'ExternalInput')
        wsel_in = _dio(nc, F, 'wselown', [B, 1, P], F32, 'ExternalInput')
        ln2_in = _dio(nc, F, 'ln2', [1, D], F32, 'ExternalInput')
        w1_in = _dio(nc, F, 'w1', [D, F4], F32, 'ExternalInput')
        w2_in = _dio(nc, F, 'w2', [F4, D], F32, 'ExternalInput')
        proc_o = _dio(nc, F, 'procsl', [B, P, D], F32, 'ExternalOutput')

        if F is None:
            sems, cm = open_common(nc, ctx)
            block = E(nc.Block())
            emit_common_loads(block, sems, cm)
        else:
            sems, cm, block = F['sems'], F['cm'], F['block']

        s_w1 = E(nc.sbuf_tensor('s_w1', [P, DC * F4], BF16))
        s_w2 = E(nc.sbuf_tensor('s_w2', [P, 32 * D], BF16))
        s_part = E(nc.sbuf_tensor('s_part', [P, D], F32))
        s_as = E(nc.sbuf_tensor('s_as', [P, D], F32))
        s_x1 = E(nc.sbuf_tensor('s_x1', [P, D], F32))
        s_x1n = E(nc.sbuf_tensor('s_x1n', [P, D], BF16))
        s_x1fm = E(nc.sbuf_tensor('s_x1fm', [P, D], BF16))
        s_ln2b = E(nc.sbuf_tensor('s_ln2b', [P, D], F32))
        s_ln2r = E(nc.sbuf_tensor('s_ln2r', [1, D], F32))
        s_wselc = E(nc.sbuf_tensor('s_wselc', [P, 1], F32))
        s_l = E(nc.sbuf_tensor('s_l', [P, 4], F32))
        s_sq = E(nc.sbuf_tensor('s_sq', [P, 512], F32))
        s_g2 = E(nc.sbuf_tensor('s_g2', [P, F4], BF16))
        s_g2T = E(nc.sbuf_tensor('s_g2T', [P, 32 * P], BF16))
        s_x2 = E(nc.sbuf_tensor('s_x2', [P, D], F32))

        p_h = E(nc.psum_tensor('p_h', [P, 2048], F32))
        p_tr = E(nc.psum_tensor('p_tr', [P, P], BF16))

        def _(g):
            g.wait_ge(sems.hdl('gd'), sems.val('gd'))
            g.dma_start(out=s_w1[:].rearrange('p (c f) -> p c f', c=DC),
                        in_=w1_in.rearrange('(c p) f -> p c f', p=P)).then_inc(*sems.inc('gd', 16))
            g.wait_ge(sems.hdl('gd'), sems.val('gd'))
            g.dma_start(out=s_w2[:].rearrange('p (c f) -> p c f', c=32),
                        in_=w2_in.rearrange('(c p) f -> p c f', p=P)).then_inc(*sems.inc('gd', 16))
        block.gpsimd(_)

        def _(sync):
            sync.wait_ge(sems.hdl('ld'), sems.val('ld'))
            sync.dma_start(out=s_ln2r[:], in_=ln2_in).then_inc(*sems.inc('ld', 16))
        block.sync(_)
        ld0 = sems.val('ld')
        gd0 = sems.val('gd')
        for eng in ('tensor', 'vector', 'scalar'):
            def _(e):
                e.wait_ge(sems.hdl('ld'), ld0)
                e.wait_ge(sems.hdl('gd'), gd0)
            getattr(block, eng)(_)

        def _(t):
            T = EngOps(t, sems, 'tv')
            t.matmul(p_h[:, 0:512], cm['onesrow_f'][:], s_ln2r[0:1, 0:512], start=True, stop=True)
            T(lambda: t.matmul(p_h[:, 512:1024], cm['onesrow_f'][:], s_ln2r[0:1, 512:1024],
                               start=True, stop=True))
        block.tensor(_)
        tv_ln = sems.val('tv')

        def _(v, tv_ln=tv_ln):
            V = EngOps(v, sems, 'vv')
            v.wait_ge(sems.hdl('tv'), tv_ln)
            V(lambda: v.tensor_copy(out=s_ln2b[:], in_=p_h[:, 0:1024]))
        block.vector(_)

        ccwF = (F or {}).get('ccw')
        for b in range(B):
            def _(sync, b=b):
                sync.wait_ge(sems.hdl('vv'), sems.val('vv'))
                if ccwF is not None:
                    sync.wait_ge(sems.hdl('cc'), ccwF)
                sync.wait_ge(sems.hdl('ld'), sems.val('ld'))
                sync.dma_start(out=s_as[:], in_=part_in[b]).then_inc(*sems.inc('ld', 16))
            block.sync(_)

            def _(sync, b=b):
                sync.wait_ge(sems.hdl('vv'), sems.val('vv'))
                sync.wait_ge(sems.hdl('ld'), sems.val('ld'))
                sync.dma_start(out=s_x1n[:], in_=xsl_in[b]).then_inc(*sems.inc('ld', 16))
                sync.wait_ge(sems.hdl('ld'), sems.val('ld'))
                sync.dma_start(out=s_wselc[:], in_=wsel_in[b].rearrange('o p -> p o')
                               ).then_inc(*sems.inc('ld', 16))
            block.sync(_)
            ld_x = sems.val('ld')

            def _(v, ld_x=ld_x):
                V = EngOps(v, sems, 'vv')
                v.wait_ge(sems.hdl('ld'), ld_x)
                V(lambda: v.tensor_tensor(out=s_x1[:], in0=s_as[:], in1=s_x1n[:], op=ALU.add))
            block.vector(_)
            vv_x1 = sems.val('vv')

            def _(s, vv_x1=vv_x1):
                S = EngOps(s, sems, 'sv')
                s.wait_ge(sems.hdl('vv'), vv_x1)
                S(lambda: s.activation(out=s_sq[:], in_=s_x1[:, 0:512], func=AF.Square,
                                       bias=col(cm, 'zero'), scale=1.0, accum_out=s_l[:, 0:1]))
                S(lambda: s.activation(out=s_sq[:], in_=s_x1[:, 512:1024], func=AF.Square,
                                       bias=col(cm, 'zero'), scale=1.0, accum_out=s_l[:, 1:2]))
            block.scalar(_)
            sv_sq = sems.val('sv')

            def _(v, sv_sq=sv_sq):
                V = EngOps(v, sems, 'vv')
                v.wait_ge(sems.hdl('sv'), sv_sq)
                V(lambda: v.tensor_tensor(out=s_l[:, 0:1], in0=s_l[:, 0:1], in1=s_l[:, 1:2],
                                          op=ALU.add))
            block.vector(_)
            vv_ssq = sems.val('vv')

            def _(s, vv_ssq=vv_ssq):
                S = EngOps(s, sems, 'sv')
                s.wait_ge(sems.hdl('vv'), vv_ssq)
                S(lambda: s.activation(out=s_l[:, 2:3], in_=s_l[:, 0:1], func=AF.Sqrt,
                                       bias=col(cm, 'eps_rms'), scale=1.0 / D))
            block.scalar(_)
            sv_sd = sems.val('sv')

            def _(v, sv_sd=sv_sd, tv_prev=sems.val('tv')):
                V = EngOps(v, sems, 'vv')
                v.wait_ge(sems.hdl('sv'), sv_sd)
                v.wait_ge(sems.hdl('tv'), tv_prev)
                V(lambda: v.reciprocal(out=s_l[:, 3:4], in_=s_l[:, 2:3]))
                V(lambda: v.tensor_scalar(out=s_x1n[:], in0=s_x1[:], scalar1=s_l[:, 3:4],
                                          scalar2=None, op0=ALU.mult))
                V(lambda: v.tensor_tensor(out=s_x1n[:], in0=s_x1n[:], in1=s_ln2b[:], op=ALU.mult))
            block.vector(_)
            vv_x1n = sems.val('vv')

            for dc in range(DC):
                def _(t, dc=dc, vv_x1n=vv_x1n):
                    T = EngOps(t, sems, 'tv')
                    t.wait_ge(sems.hdl('vv'), vv_x1n)
                    t.wait_ge(sems.hdl('vv'), sems.val('vv'))
                    T(lambda: t.transpose(p_tr[:], s_x1n[:, dc * P:(dc + 1) * P], cm['ident_b'][:]))
                block.tensor(_)
                tv_tr = sems.val('tv')

                def _(v, dc=dc, tv_tr=tv_tr):
                    V = EngOps(v, sems, 'vv')
                    v.wait_ge(sems.hdl('tv'), tv_tr)
                    V(lambda: v.tensor_copy(out=s_x1fm[:, dc * P:(dc + 1) * P], in_=p_tr[:]))
                block.vector(_)
            vv_fm = sems.val('vv')

            for hf in range(2):
                def _(t, hf=hf, vv_fm=vv_fm):
                    T = EngOps(t, sems, 'tv')
                    t.wait_ge(sems.hdl('vv'), vv_fm)
                    t.wait_ge(sems.hdl('vv'), sems.val('vv'))
                    t.wait_ge(sems.hdl('sv'), sems.val('sv'))
                    for dc in range(DC):
                        xt = s_x1fm[:, dc * P:(dc + 1) * P]
                        t.ldweights(xt)
                        for fc in range(4):
                            fcg = hf * 4 + fc
                            op = lambda dc=dc, fc=fc, fcg=fcg, xt=xt: t.matmul(
                                p_h[:, fc * 512:(fc + 1) * 512], xt,
                                s_w1[:, dc * F4 + fcg * 512: dc * F4 + (fcg + 1) * 512],
                                start=(dc == 0), stop=(dc == DC - 1))
                            if dc == DC - 1 and fc == 3:
                                T(op)
                            else:
                                op()
                block.tensor(_)
                tv_h = sems.val('tv')

                for fc in range(4):
                    fcg = hf * 4 + fc
                    def _(s, fc=fc, tv_h=tv_h):
                        S = EngOps(s, sems, 'sv')
                        s.wait_ge(sems.hdl('tv'), tv_h)
                        s.wait_ge(sems.hdl('vv'), sems.val('vv'))
                        S(lambda: s.activation(out=s_sq[:], in_=p_h[:, fc * 512:(fc + 1) * 512],
                                               func=AF.Square, bias=col(cm, 'zero'), scale=1.0))
                    block.scalar(_)
                    sv_g1 = sems.val('sv')

                    def _(v, fc=fc, sv_g1=sv_g1):
                        V = EngOps(v, sems, 'vv')
                        v.wait_ge(sems.hdl('sv'), sv_g1)
                        hs = p_h[:, fc * 512:(fc + 1) * 512]
                        V(lambda: v.tensor_tensor(out=s_sq[:], in0=s_sq[:], in1=hs, op=ALU.mult))
                        V(lambda: v.scalar_tensor_tensor(out=s_sq[:], in0=s_sq[:],
                                                         scalar=col(cm, 'c_g1'), in1=hs,
                                                         op0=ALU.mult, op1=ALU.add))
                    block.vector(_)
                    vv_u = sems.val('vv')

                    def _(s, vv_u=vv_u):
                        S = EngOps(s, sems, 'sv')
                        s.wait_ge(sems.hdl('vv'), vv_u)
                        S(lambda: s.activation(out=s_sq[:], in_=s_sq[:], func=AF.Tanh,
                                               bias=col(cm, 'zero'), scale=0.7978845608028654))
                    block.scalar(_)
                    sv_t = sems.val('sv')

                    def _(v, fc=fc, fcg=fcg, sv_t=sv_t):
                        V = EngOps(v, sems, 'vv')
                        v.wait_ge(sems.hdl('sv'), sv_t)
                        hs = p_h[:, fc * 512:(fc + 1) * 512]
                        V(lambda: v.tensor_scalar(out=s_sq[:], in0=s_sq[:],
                                                  scalar1=col(cm, 'half'), scalar2=col(cm, 'half'),
                                                  op0=ALU.mult, op1=ALU.add))
                        V(lambda: v.tensor_tensor(out=s_g2[:, fcg * 512:(fcg + 1) * 512],
                                                  in0=s_sq[:], in1=hs, op=ALU.mult))
                    block.vector(_)
            vv_g2 = sems.val('vv')

            for hc in range(32):
                def _(t, hc=hc, vv_g2=vv_g2):
                    T = EngOps(t, sems, 'tv')
                    t.wait_ge(sems.hdl('vv'), vv_g2)
                    t.wait_ge(sems.hdl('vv'), sems.val('vv'))
                    T(lambda: t.transpose(p_tr[:], s_g2[:, hc * P:(hc + 1) * P], cm['ident_b'][:]))
                block.tensor(_)
                tv_g2t = sems.val('tv')

                def _(v, hc=hc, tv_g2t=tv_g2t):
                    V = EngOps(v, sems, 'vv')
                    v.wait_ge(sems.hdl('tv'), tv_g2t)
                    V(lambda: v.tensor_copy(out=s_g2T[:, hc * P:(hc + 1) * P], in_=p_tr[:]))
                block.vector(_)
            vv_g2T = sems.val('vv')

            def _(t, vv_g2T=vv_g2T):
                T = EngOps(t, sems, 'tv')
                t.wait_ge(sems.hdl('vv'), vv_g2T)
                t.wait_ge(sems.hdl('vv'), sems.val('vv'))
                for hc in range(32):
                    gt = s_g2T[:, hc * P:(hc + 1) * P]
                    t.ldweights(gt)
                    t.matmul(p_h[:, 0:512], gt, s_w2[:, hc * D: hc * D + 512],
                             start=(hc == 0), stop=(hc == 31))
                    op = lambda hc=hc, gt=gt: t.matmul(
                        p_h[:, 512:1024], gt, s_w2[:, hc * D + 512: hc * D + 1024],
                        start=(hc == 0), stop=(hc == 31))
                    if hc == 31:
                        T(op)
                    else:
                        op()
            block.tensor(_)
            tv_y = sems.val('tv')

            def _(v, b=b, tv_y=tv_y):
                V = EngOps(v, sems, 'vv')
                v.wait_ge(sems.hdl('tv'), tv_y)
                v.wait_ge(sems.hdl('st'), sems.val('st'))
                V(lambda: v.tensor_tensor(out=s_x2[:], in0=s_x1[:], in1=p_h[:, 0:1024], op=ALU.add))
                V(lambda: v.tensor_scalar(out=s_x2[:], in0=s_x2[:], scalar1=s_wselc[:],
                                          scalar2=None, op0=ALU.mult))
            block.vector(_)
            vv_p = sems.val('vv')

            def _(sync, b=b, vv_p=vv_p):
                sync.wait_ge(sems.hdl('vv'), vv_p)
                sync.dma_start(out=proc_o[b], in_=s_x2[:]).then_inc(*sems.inc('st', 16))
            block.sync(_)

        def _(sync):
            sync.wait_ge(sems.hdl('st'), sems.val('st'))
        block.sync(_)
    return nc


def subffn_inputs(subattn_results, router_results, I, core):
    maps = []
    partial = np.stack([np.asarray(r['partial'], np.float32) for r in subattn_results])
    for c in range(NCORE):
        m = dict(common_inputs())
        m['partown'] = np.ascontiguousarray(partial[:, c])
        m['xselown'] = np.ascontiguousarray(np.asarray(router_results[c]['xsel']))
        m['wselown'] = np.ascontiguousarray(np.asarray(router_results[c]['wsel'], np.float32))
        m['ln2'] = I['L1_ffn_ln2'].reshape(1, D)
        m['w1'] = np.ascontiguousarray(I['L1_ffn_w1'])
        m['w2'] = np.ascontiguousarray(I['L1_ffn_w2'])
        maps.append(m)
    return maps



def build_combine(F=None):
    nc = bass.Bass('TRN2', target_bir_lowering=False, num_devices=NCORE) if F is None else F['nc']
    with ExitStack() as ctx:
        E = ctx.enter_context
        proc_in = _dio(nc, F, 'procfull', [NCORE, B, P, D], F32, 'ExternalInput')
        down_in = _dio(nc, F, 'down', [B, NT, P], F32, 'ExternalInput')
        dcols_in = _dio(nc, F, 'dcols', [B, P, NT], F32, 'ExternalInput')
        xn_in = _dio(nc, F, 'xnf32', [B, NT, P, D], F32, 'ExternalInput')
        attn_in = _dio(nc, F, 'attn1', [B, NT, P, D], F32, 'ExternalInput')
        x1_in = _dio(nc, F, 'x1', [B, NT, P, D], F32, 'ExternalInput')
        jplus_in = _dio(nc, F, 'jplus', [P, NT], F32, 'ExternalInput')
        out_o = _dio(nc, F, 'out', [B, NT, P, D], F32, 'ExternalOutput')

        if F is None:
            sems, cm = open_common(nc, ctx)
            block = E(nc.Block())
            emit_common_loads(block, sems, cm)
        else:
            sems, cm, block = F['sems'], F['cm'], F['block']

        s_proc = E(nc.sbuf_tensor('s_proc', [P, NCORE * D], BF16))
        s_jplus = E(nc.sbuf_tensor('s_jplus', [P, NT], F32))
        s_drow = E(nc.sbuf_tensor('s_drow', [1, NT * P], F32))
        s_dcol = E(nc.sbuf_tensor('s_dcol', [P, NT], F32))
        s_dbc = E(nc.sbuf_tensor('s_dbc', [P, P], F32))
        s_DT = E(nc.sbuf_tensor('s_DT', [P, NCORE * P], BF16))
        s_m01 = E(nc.sbuf_tensor('s_m01', [P, 2], F32))
        s_xn = E(nc.sbuf_tensor('s_xn', [P, D], F32))
        s_at = E(nc.sbuf_tensor('s_at', [P, D], F32))
        s_x1 = E(nc.sbuf_tensor('s_x1', [P, D], F32))
        s_ffn = E(nc.sbuf_tensor('s_ffn', [P, D], F32))
        s_out = E(nc.sbuf_tensor('s_out', [P, D], F32))

        p_g = E(nc.psum_tensor('p_g', [P, 1024], F32))
        p_bc = E(nc.psum_tensor('p_bc', [P, P], F32))

        def _(sync):
            sync.wait_ge(sems.hdl('ld'), sems.val('ld'))
            sync.dma_start(out=s_jplus[:], in_=jplus_in).then_inc(*sems.inc('ld', 16))
        block.sync(_)
        ld0 = sems.val('ld')
        for eng in ('tensor', 'vector', 'scalar'):
            def _(e):
                e.wait_ge(sems.hdl('ld'), ld0)
                e.wait_ge(sems.hdl('gd'), sems.val('gd'))
            getattr(block, eng)(_)

        for b in range(B):
            def _(g, b=b):
                g.wait_ge(sems.hdl('tv'), sems.val('tv'))
                for r in range(NCORE):
                    if F is not None and 'ccw' in F:
                        g.wait_ge(sems.hdl('cc'), F['ccw'])
                    g.wait_ge(sems.hdl('gd'), sems.val('gd'))
                    g.dma_start(out=s_proc[:, r * D:(r + 1) * D],
                                in_=proc_in[r, b]).then_inc(*sems.inc('gd', 16))
            block.gpsimd(_)
            gd_p = sems.val('gd')

            def _(sync, b=b):
                sync.wait_ge(sems.hdl('tv'), sems.val('tv'))
                sync.wait_ge(sems.hdl('ld'), sems.val('ld'))
                sync.dma_start(out=s_drow[:].rearrange('o (i p) -> o i p', i=NT),
                               in_=down_in[b].rearrange('i p -> i p').rearrange(
                                   '(o i) p -> o i p', o=1)).then_inc(*sems.inc('ld', 16))
                sync.wait_ge(sems.hdl('ld'), sems.val('ld'))
                sync.dma_start(out=s_dcol[:], in_=dcols_in[b]).then_inc(*sems.inc('ld', 16))
            block.sync(_)
            ld_d = sems.val('ld')

            for i in range(NT):
                def _(t, b=b, i=i, ld_d=ld_d):
                    T = EngOps(t, sems, 'tv')
                    t.wait_ge(sems.hdl('ld'), ld_d)
                    t.wait_ge(sems.hdl('vv'), sems.val('vv'))
                    T(lambda: t.matmul(p_bc[:], cm['onesrow_f'][:],
                                       s_drow[0:1, i * P:(i + 1) * P],
                                       start=True, stop=True))
                block.tensor(_)
                tv_bc = sems.val('tv')

                def _(v, tv_bc=tv_bc):
                    V = EngOps(v, sems, 'vv')
                    v.wait_ge(sems.hdl('tv'), tv_bc)
                    V(lambda: v.tensor_copy(out=s_dbc[:], in_=p_bc[:]))
                    for jc in range(NCORE):
                        V(lambda jc=jc: v.tensor_scalar(out=s_DT[:, jc * P:(jc + 1) * P],
                                                        in0=s_dbc[:], scalar1=s_jplus[:, jc:jc + 1],
                                                        scalar2=None, op0=ALU.is_equal))
                block.vector(_)
                vv_DT = sems.val('vv')

                def _(t, i=i, gd_p=gd_p, vv_DT=vv_DT):
                    T = EngOps(t, sems, 'tv')
                    t.wait_ge(sems.hdl('gd'), gd_p)
                    t.wait_ge(sems.hdl('vv'), vv_DT)
                    t.wait_ge(sems.hdl('vv'), sems.val('vv'))
                    for jc in range(NCORE):
                        dt_t = s_DT[:, jc * P:(jc + 1) * P]
                        t.ldweights(dt_t)
                        t.matmul(p_g[:, 0:512], dt_t, s_proc[:, jc * D: jc * D + 512],
                                 start=(jc == 0), stop=(jc == NCORE - 1))
                        op = lambda jc=jc, dt_t=dt_t: t.matmul(
                            p_g[:, 512:1024], dt_t, s_proc[:, jc * D + 512: jc * D + 1024],
                            start=(jc == 0), stop=(jc == NCORE - 1))
                        if jc == NCORE - 1:
                            T(op)
                        else:
                            op()
                block.tensor(_)
                tv_g = sems.val('tv')

                def _(sync, b=b, i=i):
                    sync.wait_ge(sems.hdl('vv'), sems.val('vv'))
                    sync.wait_ge(sems.hdl('ld'), sems.val('ld'))
                    sync.dma_start(out=s_xn[:], in_=xn_in[b, i]).then_inc(*sems.inc('ld', 16))
                    sync.wait_ge(sems.hdl('ld'), sems.val('ld'))
                    sync.dma_start(out=s_at[:], in_=attn_in[b, i]).then_inc(*sems.inc('ld', 16))
                    sync.wait_ge(sems.hdl('ld'), sems.val('ld'))
                    sync.dma_start(out=s_x1[:], in_=x1_in[b, i]).then_inc(*sems.inc('ld', 16))
                block.sync(_)
                ld_t = sems.val('ld')

                def _(v, b=b, i=i, tv_g=tv_g, ld_t=ld_t):
                    V = EngOps(v, sems, 'vv')
                    v.wait_ge(sems.hdl('tv'), tv_g)
                    v.wait_ge(sems.hdl('ld'), ld_t)
                    v.wait_ge(sems.hdl('st'), sems.val('st'))
                    V(lambda: v.tensor_scalar(out=s_m01[:, 0:1], in0=s_dcol[:, i:i + 1],
                                              scalar1=col(cm, 'zero'), scalar2=None, op0=ALU.is_gt))
                    V(lambda: v.tensor_scalar(out=s_m01[:, 1:2], in0=s_m01[:, 0:1],
                                              scalar1=col(cm, 'neg_one'), scalar2=col(cm, 'one'),
                                              op0=ALU.mult, op1=ALU.add))
                    V(lambda: v.scalar_tensor_tensor(out=s_ffn[:], in0=s_xn[:],
                                                     scalar=s_m01[:, 1:2], in1=p_g[:, 0:1024],
                                                     op0=ALU.mult, op1=ALU.add))
                    V(lambda: v.tensor_tensor(out=s_out[:], in0=s_x1[:], in1=s_at[:], op=ALU.add))
                    V(lambda: v.tensor_tensor(out=s_out[:], in0=s_out[:], in1=s_ffn[:], op=ALU.add))
                block.vector(_)
                vv_o = sems.val('vv')

                def _(sync, b=b, i=i, vv_o=vv_o):
                    sync.wait_ge(sems.hdl('vv'), vv_o)
                    sync.dma_start(out=out_o[b, i], in_=s_out[:]).then_inc(*sems.inc('st', 16))
                block.sync(_)

        def _(sync):
            sync.wait_ge(sems.hdl('st'), sems.val('st'))
        block.sync(_)
    return nc


def combine_inputs(subffn_results, router_results, prep1_results, attn1_results, x1_parts):
    procfull = np.ascontiguousarray(
        np.stack([np.asarray(r['procsl'], np.float32) for r in subffn_results]))
    jplus = np.ascontiguousarray(
        (np.arange(P)[:, None] + 128.0 * np.arange(NT)[None, :] + 1.0).astype(np.float32))
    maps = []
    for c in range(NCORE):
        m = dict(common_inputs())
        m['procfull'] = procfull
        m['down'] = np.ascontiguousarray(np.asarray(router_results[c]['down'], np.float32))
        m['dcols'] = np.ascontiguousarray(np.asarray(router_results[c]['dcols'], np.float32))
        m['xnf32'] = np.ascontiguousarray(np.asarray(prep1_results[c]['xnf32'], np.float32))
        m['attn1'] = np.ascontiguousarray(np.asarray(attn1_results[c]['attn'], np.float32))
        m['x1'] = np.ascontiguousarray(x1_parts[c])
        m['jplus'] = jplus
        maps.append(m)
    return maps




def _flat2d(ap, rows_dims):
    nd = len(ap.shape)
    names = [chr(ord('a') + i) for i in range(nd)]
    lhs = ' '.join(names)
    rhs = '(' + ' '.join(names[:rows_dims]) + ') (' + ' '.join(names[rows_dims:]) + ')'
    return ap.rearrange(f'{lhs} -> {rhs}')


def build_fused(dbg=()):
    nc = bass.Bass('TRN2', target_bir_lowering=False, num_devices=NCORE)
    JW = 8 * P
    with ExitStack() as ctx:
        sems, cm = open_common(nc, ctx, extra=('cc',))
        block = ctx.enter_context(nc.Block())
        emit_common_loads(block, sems, cm)
        out_ext = nc.dram_tensor('out', [B, NT, P, D], F32, kind='ExternalOutput').ap()
        R = {}

        def F(pfx, io):
            return {'nc': _NcProxy(nc, pfx), 'sems': sems, 'cm': cm, 'block': block,
                    'reg': R, 'pfx': pfx, 'io': io, 'dbg': dbg}

        def ags(pairs, barrier=True):
            ccv = []
            def _(g):
                for n in sems.c:
                    g.wait_ge(sems.hdl(n), sems.val(n))
                for kind, op, i2, o2 in pairs:
                    g.collective_compute(kind, op, replica_groups=[list(range(NCORE))],
                                         ins=[i2.opt()], outs=[o2.opt()]
                                         ).then_inc(*sems.inc('cc'))
                    ccv.append(sems.val('cc'))
            block.gpsimd(_)
            if barrier:
                stage_barrier(block, sems)
            return ccv

        def dram(name, shape, dtype):
            return nc.dram_tensor(name, shape, dtype, kind='Internal').ap()

        def dbg_copy(name, ap):
            if name not in dbg:
                return
            d = nc.dram_tensor('dbg_' + name, list(ap.shape), ap.dtype, kind='ExternalOutput').ap()

            def _(sync):
                for n in sems.c:
                    sync.wait_ge(sems.hdl(n), sems.val(n))
                sync.dma_start(out=d, in_=ap).then_inc(*sems.inc('st', 16))
            block.sync(_)
            stage_barrier(block, sems)

        def kv_ab(pfx, kag, vag, ccv):
            def cb(b):
                v = ags([('AllGather', ALU.bypass,
                          _flat2d(R[pfx + 'kfm'][b], 2), _flat2d(kag[b], 3)),
                         ('AllGather', ALU.bypass,
                          _flat2d(R[pfx + 'vtm'][b], 2), _flat2d(vag[b], 3))],
                        barrier=False)
                ccv.append(v[-1])
            return cb

        kag0 = dram('kag0', [B, NCORE, NT, P, DC, P], BF16)
        vag0 = dram('vag0', [B, NCORE, NT, P, D], BF16)
        ccv0 = []
        fp0 = F('p0_', {})
        fp0['after_batch'] = kv_ab('p0_', kag0, vag0, ccv0)
        build_prep(0, fp0)
        stage_barrier(block, sems, skip=('cc',))
        f0 = F('a0_', {'qfm': R['p0_qfm'],
                       'kag': kag0.rearrange('b r i p c q -> r b i p c q'),
                       'vfull': vag0.rearrange('b r i p d -> r b i p d')})
        f0['ccw'] = {0: ccv0[0], 1: ccv0[1]}
        build_attn(0, f0)
        stage_barrier(block, sems)
        build_moe(F('m_', {'x': R['p0_x'], 'attn': R['a0_attn'],
                           'xnfm': R['p0_xnfm'], 'w8e': R['p0_w8']}))
        stage_barrier(block, sems)

        kag1 = dram('kag1', [B, NCORE, NT, P, DC, P], BF16)
        vag1 = dram('vag1', [B, NCORE, NT, P, D], BF16)
        wag = dram('wag', [NCORE, B, 1, L], F32)
        xntmag = dram('xntmag', [NCORE, B, NT, P, D], BF16)
        ccv1 = {}

        def ab1(b):
            pairs = []
            if b == 1:
                pairs.append(('AllGather', ALU.bypass, _flat2d(R['p1_wrow'], 2),
                              _flat2d(wag, 3)))
            pairs.append(('AllGather', ALU.bypass, _flat2d(R['p1_kfm'][b], 2),
                          _flat2d(kag1[b], 3)))
            pairs.append(('AllGather', ALU.bypass, _flat2d(R['p1_vtm'][b], 2),
                          _flat2d(vag1[b], 3)))
            if b == 1:
                pairs.append(('AllGather', ALU.bypass, _flat2d(R['p1_xntm'], 3),
                              _flat2d(xntmag, 4)))
            v = ags(pairs, barrier=False)
            if b == 0:
                ccv1[0] = v[1]
            else:
                ccv1['wag'] = v[0]
                ccv1[1] = v[2]

        fp1 = F('p1_', {'x': R['m_x1']})
        fp1['after_batch'] = ab1
        build_prep(1, fp1)
        stage_barrier(block, sems, skip=('cc',))
        with ExitStack() as rctx:
            p_rt = rctx.enter_context(nc.psum_tensor('rb_prt', [P, 512], F32))
            bis = emit_router_bis(nc, rctx.enter_context, sems, cm, block,
                                  wag, p_rt, cc_after=ccv1['wag'])
            bis['loads']()
            f1 = F('a1_', {'qfm': R['p1_qfm'],
                           'kag': kag1.rearrange('b r i p c q -> r b i p c q'),
                           'vfull': vag1.rearrange('b r i p d -> r b i p d')})
            f1['ccw'] = {0: ccv1[0], 1: ccv1[1]}
            f1['ileave'] = bis['round']
            build_attn(1, f1)
            bis['finish']()
            stage_barrier(block, sems)
            fr = F('r_', {'wfull': wag, 'xntmfull': xntmag})
            fr['bis'] = bis
            build_router(fr)
            stage_barrier(block, sems)
        selcag = dram('selcag', [NCORE, B, 1, P], F32)
        posag = dram('posag', [NCORE, B, 1, P], F32)
        xselag = dram('xselag', [NCORE, B, P, D], BF16)
        ccg1 = ags([('AllGather', ALU.bypass, _flat2d(R['r_selc'], 2), _flat2d(selcag, 3)),
                    ('AllGather', ALU.bypass, _flat2d(R['r_possel'], 2), _flat2d(posag, 3)),
                    ('AllGather', ALU.bypass, _flat2d(R['r_xsel'], 2), _flat2d(xselag, 3))],
                   barrier=False)
        fc = F('c_', {'selcfull': selcag})
        fc['ccw'] = ccg1[0]
        build_maskc(fc)
        stage_barrier(block, sems)
        build_msel(F('s_', {'cown': R['c_cvsl'], 'selcfull': selcag}))
        stage_barrier(block, sems)
        mselsum = dram('mselsum', [B, JW, JW], F32)
        ccar = ags([('AllReduce', ALU.add, _flat2d(R['s_mselp'], 2), _flat2d(mselsum, 2))],
                   barrier=False)
        fsa = F('sa_', {'xself': xselag,
                        'mselfull': mselsum.rearrange('b (t p) k -> t b p k', p=P),
                        'posfull': posag})
        fsa['ccw'] = ccar[0]
        build_subattn(fsa)
        stage_barrier(block, sems)
        partsum = dram('partsum', [B, P, D], F32)
        ccrs = ags([('ReduceScatter', ALU.add, _flat2d(R['sa_partial'], 3), _flat2d(partsum, 2))],
                   barrier=False)
        fsf = F('sf_', {'partsum': partsum, 'xselown': R['r_xsel'],
                        'wselown': R['r_wsel']})
        fsf['ccw'] = ccrs[0]
        build_subffn(fsf)
        stage_barrier(block, sems)
        procag = dram('procag', [NCORE, B, P, D], F32)
        ccpr = ags([('AllGather', ALU.bypass, _flat2d(R['sf_procsl'], 2), _flat2d(procag, 3))],
                   barrier=False)
        fcb = F('cb_', {'procfull': procag, 'down': R['r_down'],
                        'dcols': R['r_dcols'], 'xnf32': R['p1_xnf32'],
                        'attn1': R['a1_attn'], 'x1': R['m_x1'], 'out': out_ext})
        fcb['ccw'] = ccpr[0]
        build_combine(fcb)
        stage_barrier(block, sems)

        def _(sync):
            sync.wait_ge(sems.hdl('st'), sems.val('st'))
        block.sync(_)
    return nc


def fused_inputs(I):
    x_parts = shard_x(I['L0_attn_x'])
    inv = (1.0 / 10000.0 ** (np.arange(0, 64, 2, dtype=np.float32) / 64)).astype(np.float32)
    mask = I['L1_ffn_attention_mask']
    uincl = np.triu(np.ones((P, P), np.float32))
    ucols = np.ascontiguousarray(
        (np.arange(P)[:, None] + 128.0 * np.arange(32)[None, :]).astype(np.float32))
    import ml_dtypes
    selcols = np.zeros((P, 65), np.float32)
    selcols[:, 0] = np.arange(P)
    for ch in range(64):
        selcols[:, 1 + ch] = 2.0 * ch
    selcols = selcols.astype(ml_dtypes.bfloat16)
    padmask = np.zeros((P, 8 * P), np.float32)
    padmask[:, 8 * P - 1] = -1e9
    jplus = np.ascontiguousarray(
        (np.arange(P)[:, None] + 128.0 * np.arange(NT)[None, :] + 1.0).astype(np.float32))
    w8tail = np.tile(np.array([0.5, 0.5, 0, 0, 0, 0, 0, 0], np.float32), (P, 1))
    o = np.arange(1024)[None, :]
    pq = np.arange(P)[:, None]

    w1all = np.ascontiguousarray(np.concatenate([I['L0_ffn_rw1'], I['L0_ffn_sw1']], 0))
    w2all = np.ascontiguousarray(np.concatenate([I['L0_ffn_rw2'], I['L0_ffn_sw2']], 0))
    w3all = np.ascontiguousarray(np.concatenate([I['L0_ffn_rw3'], I['L0_ffn_sw3']], 0))
    b1all = np.concatenate([I['L0_ffn_rb1'], I['L0_ffn_sb1']], 0)
    b2all = np.concatenate([I['L0_ffn_rb2'], I['L0_ffn_sb2']], 0)
    b3all = np.zeros((16, D), np.float32)
    b3all[:NEXP] = np.concatenate([I['L0_ffn_rb3'], I['L0_ffn_sb3']], 0)
    b1fm = np.ascontiguousarray(b1all.reshape(NEXP, HC, P).transpose(2, 0, 1).reshape(P, NEXP * HC))
    b2fm = np.ascontiguousarray(b2all.reshape(NEXP, HC, P).transpose(2, 0, 1).reshape(P, NEXP * HC))

    maps = []
    for c in range(NCORE):
        m = dict(common_inputs())
        m['p0_x'] = np.ascontiguousarray(x_parts[c])
        m['p0_wqt'] = np.ascontiguousarray(I['L0_attn_Wq'].T)
        m['p0_wkt'] = np.ascontiguousarray(I['L0_attn_Wk'].T)
        m['p0_wvt'] = np.ascontiguousarray(I['L0_attn_Wv'].T)
        m['p0_bq'] = np.ascontiguousarray(I['L0_attn_bq'].reshape(DC, P).T)
        m['p0_bk'] = np.ascontiguousarray(I['L0_attn_bk'].reshape(DC, P).T)
        m['p0_bv'] = I['L0_attn_bv'].reshape(1, D)
        m['p0_gw'] = np.ascontiguousarray(I['L0_ffn_gate_w'])
        m['p0_gb'] = I['L0_ffn_gate_b'].reshape(1, 8)
        m['p0_w8tail'] = w8tail
        dmask = np.where(o > 128 * c + pq, -1e9, 0.0).astype(np.float32)
        m['a0_dmask'] = dmask
        m['a1_dmask'] = dmask
        m['a1_wot'] = np.ascontiguousarray(I['L1_attn_Wo'].T)
        m.update(m_w1all=w1all, m_w2all=w2all, m_w3all=w3all,
                 m_b1fm=b1fm, m_b2fm=b2fm, m_b3all=b3all)
        m['p1_wqt'] = np.ascontiguousarray(I['L1_attn_Wq'].T)
        m['p1_wkt'] = np.ascontiguousarray(I['L1_attn_Wk'].T)
        m['p1_wvt'] = np.ascontiguousarray(I['L1_attn_Wv'].T)
        m['p1_rw'] = np.ascontiguousarray(I['L1_ffn_router_w'].reshape(DC, P).T)
        m['p1_rb'] = I['L1_ffn_router_b'].reshape(1, 1)
        m['r_posids'] = np.ascontiguousarray(I['L1_ffn_position_ids'])
        m['r_uincl'] = uincl
        jbase = c * P
        m['r_jrowbc'] = np.tile(np.arange(jbase + 1, jbase + P + 1, dtype=np.float32), (P, 1))
        m['r_selcols'] = selcols
        m['r_ucols'] = ucols
        sel = np.zeros((64, NT), np.float32)
        for i in range(NT):
            sel[NCORE * i + c, i] = 1.0
        m['r_selmat'] = sel
        m['c_maskvsl'] = np.ascontiguousarray(mask[0:B, 0, :, c * 512:(c + 1) * 512])
        m['c_ucols'] = ucols
        m['s_vcols'] = np.ascontiguousarray(
            (np.arange(P)[:, None] + 512.0 * c + 128.0 * np.arange(4)[None, :]).astype(np.float32))
        m['s_padmask'] = padmask if c == 0 else np.zeros_like(padmask)
        hc0 = 2 * c * 64
        m['sa_wqsl'] = np.ascontiguousarray(I['L1_ffn_wq'][:, hc0:hc0 + P])
        m['sa_wksl'] = np.ascontiguousarray(I['L1_ffn_wk'][:, hc0:hc0 + P])
        m['sa_wvsl'] = np.ascontiguousarray(I['L1_ffn_wv'][:, hc0:hc0 + P])
        m['sa_wosl'] = np.ascontiguousarray(I['L1_ffn_wo'][hc0:hc0 + P, :])
        m['sa_ln1'] = I['L1_ffn_ln1'].reshape(1, D)
        m['sa_invbc'] = np.tile(inv, (P, 1))
        m['sf_ln2'] = I['L1_ffn_ln2'].reshape(1, D)
        m['sf_w1'] = np.ascontiguousarray(I['L1_ffn_w1'])
        m['sf_w2'] = np.ascontiguousarray(I['L1_ffn_w2'])
        m['cb_jplus'] = jplus
        maps.append(m)
    return maps


def kernel(**inputs):
    nc = build_fused()
    maps = fused_inputs(inputs)
    res = run_bass_kernel_spmd(nc, maps, core_ids=list(range(NCORE)))
    out_parts = [np.asarray(r['out'], np.float32) for r in res.results]
    return unshard_x(out_parts).astype(np.float32)


# version 17
# speedup vs baseline: 1.0006x; 1.0006x over previous
import numpy as np
from contextlib import ExitStack

import concourse.bass as bass
import concourse.mybir as mybir
from concourse.bass_utils import run_bass_kernel_spmd

F32 = mybir.dt.float32
BF16 = mybir.dt.bfloat16
AF = mybir.ActivationFunctionType
ALU = mybir.AluOpType
AX = mybir.AxisListType

NCORE = 8
B = 2
D = 1024
NT = 8
P = 128
L = NT * P
DC = D // P

TWO_PI = 6.283185307179586
C_HI = 6.28125
C_MID = TWO_PI - C_HI


def _consts_np():
    ident = np.eye(P, dtype=np.float32)
    onesrow = np.ones((1, P), np.float32)
    onescol = np.ones((P, 1), np.float32)
    cols = {
        'recip_d': 1.0 / D,
        'neg_recip_d': -1.0 / D,
        'eps_ln': 1e-5,
        'eps_rms': 1e-6,
        'one': 1.0,
        'neg_one': -1.0,
        'half': 0.5,
        'zero': 0.0,
        'magic': 8388608.0,
        'inv_2pi': 1.0 / TWO_PI,
        'neg_chi': -C_HI,
        'neg_cmid': -C_MID,
        'half_pi': np.pi / 2,
        'clamp4095': 4095.0,
        'c_g1': 0.044715,
        'c_g2': 0.7978845608028654,
        'count_k': 1024.0,
        'c64': 64.0,
        'inv32': 0.03125,
        'quarter': 0.25,
        'j1': 1.0,
        'j2': 2.0,
        'j3': 3.0,
    }
    names = sorted(cols)
    colmat = np.stack([np.full(P, cols[n], np.float32) for n in names], axis=1)
    colidx = {n: i for i, n in enumerate(names)}
    return ident, onesrow, onescol, colmat, colidx


IDENT_NP, ONESROW_NP, ONESCOL_NP, COLMAT_NP, COLIDX = _consts_np()
NCOLS = COLMAT_NP.shape[1]


class Sems:
    def __init__(self, nc, ctx, names):
        self.h = {}
        self.c = {}
        for n in names:
            self.h[n] = ctx.enter_context(nc.semaphore(n))
            self.c[n] = 0

    def inc(self, name, by=1):
        self.c[name] += by
        return self.h[name], by

    def val(self, name):
        return self.c[name]

    def hdl(self, name):
        return self.h[name]


class EngOps:
    def __init__(self, eng, sems, name):
        self.eng = eng
        self.sems = sems
        self.name = name

    def __call__(self, fn):
        self.eng.wait_ge(self.sems.hdl(self.name), self.sems.val(self.name))
        return fn().then_inc(*self.sems.inc(self.name))


def _dio(nc, F, name, shape, dtype, kind):
    if F is None:
        return nc.dram_tensor(name, shape, dtype, kind=kind).ap()
    if name in F['io']:
        return F['io'][name]
    k = 'Internal' if kind == 'ExternalOutput' else kind
    if kind == 'ExternalOutput' and (F['pfx'] + name) in F.get('dbg', ()):
        k = 'ExternalOutput'
    ap = nc.dram_tensor(name, shape, dtype, kind=k).ap()
    F['reg'][F['pfx'] + name] = ap
    return ap


class _NcProxy:
    def __init__(self, nc, pfx):
        object.__setattr__(self, '_nc', nc)
        object.__setattr__(self, '_pfx', pfx)

    def sbuf_tensor(self, name, *a, **k):
        return self._nc.sbuf_tensor(self._pfx + name, *a, **k)

    def psum_tensor(self, name, *a, **k):
        return self._nc.psum_tensor(self._pfx + name, *a, **k)

    def dram_tensor(self, name, *a, **k):
        return self._nc.dram_tensor(self._pfx + name, *a, **k)

    def __getattr__(self, n):
        return getattr(self._nc, n)


def stage_barrier(block, sems, skip=()):
    vals = [(sems.hdl(n), sems.val(n)) for n in sems.c if n not in skip]
    for eng in ('tensor', 'vector', 'scalar', 'gpsimd', 'sync'):
        def _(e, vals=vals):
            for h, v in vals:
                e.wait_ge(h, v)
        getattr(block, eng)(_)


def open_common(nc, ctx, extra=()):
    sems = Sems(nc, ctx, ['ld', 'tv', 'vv', 'sv', 'st', 'gd', 'ldk', 'ldv'] + list(extra))
    cm = {}
    cm['ident_b'] = ctx.enter_context(nc.sbuf_tensor('c_ident_b', [P, P], BF16))
    cm['ident_f'] = ctx.enter_context(nc.sbuf_tensor('c_ident_f', [P, P], F32))
    cm['onesrow_b'] = ctx.enter_context(nc.sbuf_tensor('c_onesrow_b', [1, P], BF16))
    cm['onesrow_f'] = ctx.enter_context(nc.sbuf_tensor('c_onesrow_f', [1, P], F32))
    cm['onescol_b'] = ctx.enter_context(nc.sbuf_tensor('c_onescol_b', [P, 1], BF16))
    cm['onescol_f'] = ctx.enter_context(nc.sbuf_tensor('c_onescol_f', [P, 1], F32))
    cm['cols'] = ctx.enter_context(nc.sbuf_tensor('c_cols', [P, NCOLS], F32))
    cm['d_ident'] = nc.dram_tensor('c_ident', [P, P], F32, kind='ExternalInput').ap()
    cm['d_ones_r'] = nc.dram_tensor('c_onesr', [1, P], F32, kind='ExternalInput').ap()
    cm['d_ones_c'] = nc.dram_tensor('c_onesc', [P, 1], F32, kind='ExternalInput').ap()
    cm['d_cols'] = nc.dram_tensor('c_colmat', [P, NCOLS], F32, kind='ExternalInput').ap()
    return sems, cm


def emit_common_loads(block, sems, cm):
    def _(sync):
        sync.wait_ge(sems.hdl('ld'), sems.val('ld'))
        sync.dma_start(out=cm['ident_f'][:], in_=cm['d_ident']).then_inc(*sems.inc('ld', 16))
        sync.wait_ge(sems.hdl('ld'), sems.val('ld'))
        sync.dma_start(out=cm['cols'][:], in_=cm['d_cols']).then_inc(*sems.inc('ld', 16))
        sync.wait_ge(sems.hdl('ld'), sems.val('ld'))
        sync.dma_start(out=cm['onescol_f'][:], in_=cm['d_ones_c']).then_inc(*sems.inc('ld', 16))
        sync.wait_ge(sems.hdl('ld'), sems.val('ld'))
        sync.dma_start(out=cm['onesrow_f'][:], in_=cm['d_ones_r']).then_inc(*sems.inc('ld', 16))
    block.sync(_)

    def _(g):
        g.wait_ge(sems.hdl('gd'), sems.val('gd'))
        g.dma_start(out=cm['ident_b'][:], in_=cm['d_ident']).then_inc(*sems.inc('gd', 16))
        g.wait_ge(sems.hdl('gd'), sems.val('gd'))
        g.dma_start(out=cm['onesrow_b'][:], in_=cm['d_ones_r']).then_inc(*sems.inc('gd', 16))
        g.wait_ge(sems.hdl('gd'), sems.val('gd'))
        g.dma_start(out=cm['onescol_b'][:], in_=cm['d_ones_c']).then_inc(*sems.inc('gd', 16))
    block.gpsimd(_)


def common_inputs():
    return {
        'c_ident': IDENT_NP, 'c_onesr': ONESROW_NP, 'c_onesc': ONESCOL_NP,
        'c_colmat': COLMAT_NP,
    }


def col(cm, name):
    i = COLIDX[name]
    return cm['cols'][:, i:i + 1]


def shard_x(x):
    xt = np.ascontiguousarray(np.asarray(x).reshape(B, 64, P, D))
    return [np.ascontiguousarray(xt[:, c::NCORE]) for c in range(NCORE)]


def unshard_x(parts):
    out = np.empty((B, 64, P, D), np.float32)
    for c in range(NCORE):
        out[:, c::NCORE] = np.asarray(parts[c])
    return out.reshape(B, 8192, D)



def build_prep(layer, F=None):
    nc = bass.Bass('TRN2', target_bir_lowering=False, num_devices=NCORE) if F is None else F['nc']
    with ExitStack() as ctx:
        E = ctx.enter_context
        x_in = _dio(nc, F, 'x', [B, NT, P, D], F32, 'ExternalInput')
        wqt = _dio(nc, F, 'wqt', [D, D], F32, 'ExternalInput')
        wkt = _dio(nc, F, 'wkt', [D, D], F32, 'ExternalInput')
        wvt = _dio(nc, F, 'wvt', [D, D], F32, 'ExternalInput')
        if layer == 0:
            bq_in = _dio(nc, F, 'bq', [P, DC], F32, 'ExternalInput')
            bk_in = _dio(nc, F, 'bk', [P, DC], F32, 'ExternalInput')
            bv_in = _dio(nc, F, 'bv', [1, D], F32, 'ExternalInput')
            gw_in = _dio(nc, F, 'gw', [D, 8], F32, 'ExternalInput')
            gb_in = _dio(nc, F, 'gb', [1, 8], F32, 'ExternalInput')
            w8t_in = _dio(nc, F, 'w8tail', [P, 8], F32, 'ExternalInput')
        else:
            rw_in = _dio(nc, F, 'rw', [P, DC], F32, 'ExternalInput')
            rb_in = _dio(nc, F, 'rb', [1, 1], F32, 'ExternalInput')

        xnfm_o = _dio(nc, F, 'xnfm', [B, DC, P, L], BF16, 'ExternalOutput')
        qfm_o = _dio(nc, F, 'qfm', [B, DC, P, L], BF16, 'ExternalOutput')
        kfm_o = _dio(nc, F, 'kfm', [B, NT, P, DC, P], BF16, 'ExternalOutput')
        vtm_o = _dio(nc, F, 'vtm', [B, NT, P, D], BF16, 'ExternalOutput')
        if layer == 0:
            w8_o = _dio(nc, F, 'w8', [B, NT, P, 16], F32, 'ExternalOutput')
        else:
            xntm_o = _dio(nc, F, 'xntm', [B, NT, P, D], BF16, 'ExternalOutput')
            xnf32_o = _dio(nc, F, 'xnf32', [B, NT, P, D], F32, 'ExternalOutput')
            wrow_o = _dio(nc, F, 'wrow', [B, 1, L], F32, 'ExternalOutput')

        if F is None:
            sems, cm = open_common(nc, ctx)
            block = E(nc.Block())
            emit_common_loads(block, sems, cm)
        else:
            sems, cm, block = F['sems'], F['cm'], F['block']

        s_x = [E(nc.sbuf_tensor('s_x%d' % p, [P, D], F32)) for p in range(2)]
        s_sq = E(nc.sbuf_tensor('s_sq', [P, D], F32))
        s_xnb = [E(nc.sbuf_tensor('s_xnb%d' % p, [P, D], BF16)) for p in range(2)]
        s_xnf = [E(nc.sbuf_tensor('s_xnf%d' % p, [P, D], F32)) for p in range(2)]
        s_stat = [E(nc.sbuf_tensor('s_stat%d' % p, [P, 8], F32)) for p in range(2)]
        s_xnfm = E(nc.sbuf_tensor('s_xnfm', [P, DC * L], BF16))
        s_wq = E(nc.sbuf_tensor('s_wq', [P, DC * D], BF16))
        s_wk = E(nc.sbuf_tensor('s_wk', [P, DC * D], BF16))
        s_wv = E(nc.sbuf_tensor('s_wv', [P, DC * D], BF16))
        s_out = [E(nc.sbuf_tensor('s_out%d' % p, [P, D], BF16)) for p in range(2)]
        if layer == 0:
            s_bq = E(nc.sbuf_tensor('s_bq', [P, DC], F32))
            s_bk = E(nc.sbuf_tensor('s_bk', [P, DC], F32))
            s_bvb = E(nc.sbuf_tensor('s_bvb', [P, D], F32))
            s_bv1 = E(nc.sbuf_tensor('s_bv1', [1, D], BF16))
            s_gw = E(nc.sbuf_tensor('s_gw', [P, DC * 8], BF16))
            s_gb1 = E(nc.sbuf_tensor('s_gb1', [1, 8], BF16))
            s_gbb = E(nc.sbuf_tensor('s_gbb', [P, 8], F32))
            s_gsc = E(nc.sbuf_tensor('s_gsc', [P, 8], F32))
            s_top8 = E(nc.sbuf_tensor('s_top8', [P, 8], F32))
            s_w8 = E(nc.sbuf_tensor('s_w8', [P, 16], F32))
            s_w8t = E(nc.sbuf_tensor('s_w8t', [P, 8], F32))
        else:
            s_rw = E(nc.sbuf_tensor('s_rw', [P, DC], BF16))
            s_rb = E(nc.sbuf_tensor('s_rb', [1, 1], F32))
            s_wrow = E(nc.sbuf_tensor('s_wrow', [1, L], F32))

        p_tr = [E(nc.psum_tensor('p_tr%d' % p, [P, DC * P], BF16)) for p in range(2)]
        p_mm = [E(nc.psum_tensor('p_mm%d' % p, [P, 1024], F32)) for p in range(2)]
        p_sm = E(nc.psum_tensor('p_sm', [P, 512], F32))

        def _(g):
            g.wait_ge(sems.hdl('gd'), sems.val('gd'))
            g.dma_start(out=s_wq[:].rearrange('p (c f) -> p c f', c=DC),
                        in_=wqt.rearrange('(c p) f -> p c f', p=P)).then_inc(*sems.inc('gd', 16))
            g.wait_ge(sems.hdl('gd'), sems.val('gd'))
            g.dma_start(out=s_wk[:].rearrange('p (c f) -> p c f', c=DC),
                        in_=wkt.rearrange('(c p) f -> p c f', p=P)).then_inc(*sems.inc('gd', 16))
            g.wait_ge(sems.hdl('gd'), sems.val('gd'))
            g.dma_start(out=s_wv[:].rearrange('p (c f) -> p c f', c=DC),
                        in_=wvt.rearrange('(c p) f -> p c f', p=P)).then_inc(*sems.inc('gd', 16))
            if layer == 0:
                g.wait_ge(sems.hdl('gd'), sems.val('gd'))
                g.dma_start(out=s_gw[:].rearrange('p (c e) -> p c e', e=8),
                            in_=gw_in.rearrange('(c p) e -> p c e', p=P)).then_inc(*sems.inc('gd', 16))
                g.wait_ge(sems.hdl('gd'), sems.val('gd'))
                g.dma_start(out=s_bv1[:], in_=bv_in).then_inc(*sems.inc('gd', 16))
                g.wait_ge(sems.hdl('gd'), sems.val('gd'))
                g.dma_start(out=s_gb1[:], in_=gb_in).then_inc(*sems.inc('gd', 16))
            else:
                g.wait_ge(sems.hdl('gd'), sems.val('gd'))
                g.dma_start(out=s_rw[:], in_=rw_in).then_inc(*sems.inc('gd', 16))
        block.gpsimd(_)

        def _(sync):
            if layer == 0:
                sync.wait_ge(sems.hdl('ld'), sems.val('ld'))
                sync.dma_start(out=s_bq[:], in_=bq_in).then_inc(*sems.inc('ld', 16))
                sync.wait_ge(sems.hdl('ld'), sems.val('ld'))
                sync.dma_start(out=s_w8t[:], in_=w8t_in).then_inc(*sems.inc('ld', 16))
                sync.wait_ge(sems.hdl('ld'), sems.val('ld'))
                sync.dma_start(out=s_bk[:], in_=bk_in).then_inc(*sems.inc('ld', 16))
            else:
                sync.wait_ge(sems.hdl('ld'), sems.val('ld'))
                sync.dma_start(out=s_rb[:], in_=rb_in).then_inc(*sems.inc('ld', 16))
        block.sync(_)
        ld_all = sems.val('ld')
        gd_all = sems.val('gd')

        for eng in ('tensor', 'vector', 'scalar'):
            def _(e):
                e.wait_ge(sems.hdl('ld'), ld_all)
                e.wait_ge(sems.hdl('gd'), gd_all)
            getattr(block, eng)(_)

        if layer == 0:
            def _(t):
                T = EngOps(t, sems, 'tv')
                t.ldweights(cm['onesrow_b'][:])
                t.matmul(p_mm[0][:, 0:512], cm['onesrow_b'][:], s_bv1[0:1, 0:512], start=True, stop=True)
                t.matmul(p_mm[0][:, 512:1024], cm['onesrow_b'][:], s_bv1[0:1, 512:1024], start=True, stop=True)
                T(lambda: t.matmul(p_sm[:, 0:8], cm['onesrow_b'][:], s_gb1[0:1, :], start=True, stop=True))
            block.tensor(_)
            tv_bc = sems.val('tv')

            def _(v, tv_bc=tv_bc):
                V = EngOps(v, sems, 'vv')
                v.wait_ge(sems.hdl('tv'), tv_bc)
                V(lambda: v.tensor_copy(out=s_bvb[:], in_=p_mm[0][:, 0:1024]))
                V(lambda: v.tensor_copy(out=s_gbb[:], in_=p_sm[:, 0:8]))
            block.vector(_)

        mkp = {'xfree': [0, 0], 'xfree_sv': [0, 0], 'trfree': [sems.val('vv')] * 2,
               'mmfree': [sems.val('vv')] * 2, 'mmfree_sv': [sems.val('sv')] * 2,
               'outfree': [0, 0]}

        for b in range(B):
            for i in range(NT):
                pr = i % 2
                def _(sync, b=b, i=i, pr=pr, gx=mkp['xfree'][pr], gs=mkp['xfree_sv'][pr]):
                    sync.wait_ge(sems.hdl('vv'), gx)
                    sync.wait_ge(sems.hdl('sv'), gs)
                    sync.wait_ge(sems.hdl('ld'), sems.val('ld'))
                    sync.dma_start(out=s_x[pr][:], in_=x_in[b, i]).then_inc(*sems.inc('ld', 16))
                block.sync(_)
                ld_now = sems.val('ld')

                def _(v, pr=pr, ld_now=ld_now):
                    V = EngOps(v, sems, 'vv')
                    v.wait_ge(sems.hdl('ld'), ld_now)
                    V(lambda: v.reduce_sum(out=s_stat[pr][:, 0:1], in_=s_x[pr][:], axis=AX.X))
                    V(lambda: v.tensor_scalar(out=s_stat[pr][:, 1:2], in0=s_stat[pr][:, 0:1],
                                              scalar1=col(cm, 'neg_recip_d'), scalar2=None,
                                              op0=ALU.mult))
                block.vector(_)
                vv_m = sems.val('vv')

                def _(s, pr=pr, vv_m=vv_m):
                    S = EngOps(s, sems, 'sv')
                    s.wait_ge(sems.hdl('vv'), vv_m)
                    S(lambda: s.activation(out=s_sq[:], in_=s_x[pr][:], func=AF.Square,
                                           bias=s_stat[pr][:, 1:2], scale=1.0,
                                           accum_out=s_stat[pr][:, 2:3]))
                    S(lambda: s.activation(out=s_stat[pr][:, 3:4], in_=s_stat[pr][:, 2:3], func=AF.Sqrt,
                                           bias=col(cm, 'eps_ln'), scale=1.0 / D))
                block.scalar(_)
                sv_s = sems.val('sv')
                mkp['xfree_sv'][pr] = sv_s

                def _(v, pr=pr, sv_s=sv_s, layer=layer, gst=sems.val('st')):
                    V = EngOps(v, sems, 'vv')
                    v.wait_ge(sems.hdl('sv'), sv_s)
                    v.wait_ge(sems.hdl('st'), gst)
                    V(lambda: v.reciprocal(out=s_stat[pr][:, 4:5], in_=s_stat[pr][:, 3:4]))
                    V(lambda: v.tensor_scalar(out=s_xnb[pr][:], in0=s_x[pr][:],
                                              scalar1=s_stat[pr][:, 1:2], scalar2=s_stat[pr][:, 4:5],
                                              op0=ALU.add, op1=ALU.mult))
                    if layer == 1:
                        V(lambda: v.tensor_scalar(out=s_xnf[pr][:], in0=s_x[pr][:],
                                                  scalar1=s_stat[pr][:, 1:2], scalar2=s_stat[pr][:, 4:5],
                                                  op0=ALU.add, op1=ALU.mult))
                block.vector(_)
                vv_ln = sems.val('vv')
                mkp['xfree'][pr] = vv_ln

                if layer == 1:
                    def _(sync, b=b, i=i, pr=pr, vv_ln=vv_ln):
                        sync.wait_ge(sems.hdl('vv'), vv_ln)
                        sync.wait_ge(sems.hdl('st'), sems.val('st'))
                        sync.dma_start(out=xntm_o[b, i], in_=s_xnb[pr][:]).then_inc(*sems.inc('st', 16))
                        sync.wait_ge(sems.hdl('st'), sems.val('st'))
                        sync.dma_start(out=xnf32_o[b, i], in_=s_xnf[pr][:]).then_inc(*sems.inc('st', 16))
                    block.sync(_)

                def _(t, pr=pr, vv_ln=vv_ln, gtr=mkp['trfree'][pr]):
                    T = EngOps(t, sems, 'tv')
                    t.wait_ge(sems.hdl('vv'), vv_ln)
                    t.wait_ge(sems.hdl('vv'), gtr)
                    for dc in range(DC):
                        op = lambda dc=dc: t.transpose(p_tr[pr][:, dc * P:(dc + 1) * P],
                                                       s_xnb[pr][:, dc * P:(dc + 1) * P],
                                                       cm['ident_b'][:])
                        if dc == DC - 1:
                            T(op)
                        else:
                            op()
                block.tensor(_)
                tv_tr = sems.val('tv')

                def _(v, i=i, pr=pr, tv_tr=tv_tr):
                    V = EngOps(v, sems, 'vv')
                    v.wait_ge(sems.hdl('tv'), tv_tr)
                    V(lambda: v.tensor_copy(
                        out=s_xnfm[:].rearrange('p (c l) -> p c l', c=DC)[:, :, i * P:(i + 1) * P],
                        in_=p_tr[pr][:].rearrange('p (c q) -> p c q', c=DC)))
                block.vector(_)
                mkp['trfree'][pr] = sems.val('vv')

            vv_fm = sems.val('vv')

            def _(sync, b=b, vv_fm=vv_fm):
                sync.wait_ge(sems.hdl('vv'), vv_fm)
                sync.wait_ge(sems.hdl('st'), sems.val('st'))
                sync.dma_start(out=xnfm_o[b].rearrange('c p l -> p c l'),
                               in_=s_xnfm[:].rearrange('p (c l) -> p c l', c=DC)
                               ).then_inc(*sems.inc('st', 16))
            block.sync(_)

            for wi, (wsb, out_t, bname) in enumerate(((s_wq, qfm_o, 'bq'), (s_wk, kfm_o, 'bk'))):
                for ec in range(DC):
                    u = wi * DC + ec
                    pu = u % 2
                    def _(t, wsb=wsb, ec=ec, pu=pu, vv_fm=vv_fm, gmm=mkp['mmfree'][pu]):
                        T = EngOps(t, sems, 'tv')
                        t.wait_ge(sems.hdl('vv'), vv_fm)
                        t.wait_ge(sems.hdl('vv'), gmm)
                        for dc in range(DC):
                            wtile = wsb[:, dc * D + ec * P: dc * D + ec * P + P]
                            t.ldweights(wtile)
                            t.matmul(p_mm[pu][:, 0:512], wtile, s_xnfm[:, dc * L: dc * L + 512],
                                     start=(dc == 0), stop=(dc == DC - 1))
                            op = lambda dc=dc, wtile=wtile: t.matmul(
                                p_mm[pu][:, 512:1024], wtile, s_xnfm[:, dc * L + 512: dc * L + 1024],
                                start=(dc == 0), stop=(dc == DC - 1))
                            if dc == DC - 1:
                                T(op)
                            else:
                                op()
                    block.tensor(_)
                    tv_p = sems.val('tv')

                    def _(v, ec=ec, pu=pu, bname=bname, tv_p=tv_p, layer=layer,
                          gout=mkp['outfree'][pu]):
                        V = EngOps(v, sems, 'vv')
                        v.wait_ge(sems.hdl('tv'), tv_p)
                        v.wait_ge(sems.hdl('st'), gout)
                        if layer == 0:
                            bap = (s_bq if bname == 'bq' else s_bk)[:, ec:ec + 1]
                            V(lambda: v.tensor_scalar(out=s_out[pu][:, 0:1024], in0=p_mm[pu][:, 0:1024],
                                                      scalar1=bap, scalar2=None, op0=ALU.add))
                        else:
                            V(lambda: v.tensor_copy(out=s_out[pu][:, 0:1024], in_=p_mm[pu][:, 0:1024]))
                    block.vector(_)
                    mkp['mmfree'][pu] = sems.val('vv')
                    vv_p = sems.val('vv')

                    def _(sync, ec=ec, pu=pu, out_t=out_t, bname=bname, b=b, vv_p=vv_p):
                        sync.wait_ge(sems.hdl('vv'), vv_p)
                        sync.wait_ge(sems.hdl('st'), sems.val('st'))
                        if bname == 'bk':
                            sync.dma_start(out=out_t[b].rearrange('i p c q -> p c i q')[:, ec],
                                           in_=s_out[pu][:, 0:1024].rearrange('p (i q) -> p i q', i=NT)
                                           ).then_inc(*sems.inc('st', 16))
                        else:
                            sync.dma_start(out=out_t[b, ec],
                                           in_=s_out[pu][:, 0:1024]).then_inc(*sems.inc('st', 16))
                    block.sync(_)
                    mkp['outfree'][pu] = sems.val('st')

            for i in range(NT):
                pu = i % 2
                def _(t, i=i, pu=pu, gmm=mkp['mmfree'][pu]):
                    T = EngOps(t, sems, 'tv')
                    t.wait_ge(sems.hdl('vv'), gmm)
                    for dc in range(DC):
                        xt = s_xnfm[:, dc * L + i * P: dc * L + (i + 1) * P]
                        t.ldweights(xt)
                        t.matmul(p_mm[pu][:, 0:512], xt, s_wv[:, dc * D: dc * D + 512],
                                 start=(dc == 0), stop=(dc == DC - 1))
                        op = lambda dc=dc, xt=xt: t.matmul(
                            p_mm[pu][:, 512:1024], xt, s_wv[:, dc * D + 512: dc * D + 1024],
                            start=(dc == 0), stop=(dc == DC - 1))
                        if dc == DC - 1:
                            T(op)
                        else:
                            op()
                block.tensor(_)
                tv_v = sems.val('tv')

                def _(v, pu=pu, tv_v=tv_v, layer=layer, gout=mkp['outfree'][pu]):
                    V = EngOps(v, sems, 'vv')
                    v.wait_ge(sems.hdl('tv'), tv_v)
                    v.wait_ge(sems.hdl('st'), gout)
                    if layer == 0:
                        V(lambda: v.scalar_tensor_tensor(out=s_out[pu][:, 0:1024], in0=p_mm[pu][:, 0:1024],
                                                         scalar=col(cm, 'one'), in1=s_bvb[:],
                                                         op0=ALU.mult, op1=ALU.add))
                    else:
                        V(lambda: v.tensor_copy(out=s_out[pu][:, 0:1024], in_=p_mm[pu][:, 0:1024]))
                block.vector(_)
                mkp['mmfree'][pu] = sems.val('vv')
                vv_v = sems.val('vv')

                def _(sync, i=i, pu=pu, b=b, vv_v=vv_v):
                    sync.wait_ge(sems.hdl('vv'), vv_v)
                    sync.wait_ge(sems.hdl('st'), sems.val('st'))
                    sync.dma_start(out=vtm_o[b, i], in_=s_out[pu][:, 0:1024]).then_inc(*sems.inc('st', 16))
                block.sync(_)
                mkp['outfree'][pu] = sems.val('st')

            if layer == 0:
                for i in range(NT):
                    def _(t, i=i):
                        T = EngOps(t, sems, 'tv')
                        t.wait_ge(sems.hdl('vv'), sems.val('vv'))
                        for dc in range(DC):
                            xt = s_xnfm[:, dc * L + i * P: dc * L + (i + 1) * P]
                            t.ldweights(xt)
                            op = lambda dc=dc, xt=xt: t.matmul(
                                p_sm[:, 0:8], xt, s_gw[:, dc * 8:(dc + 1) * 8],
                                start=(dc == 0), stop=(dc == DC - 1))
                            if dc == DC - 1:
                                T(op)
                            else:
                                op()
                    block.tensor(_)
                    tv_g = sems.val('tv')

                    def _(v, tv_g=tv_g):
                        V = EngOps(v, sems, 'vv')
                        v.wait_ge(sems.hdl('tv'), tv_g)
                        v.wait_ge(sems.hdl('st'), sems.val('st'))
                        V(lambda: v.tensor_tensor(out=s_gsc[:], in0=p_sm[:, 0:8], in1=s_gbb[:], op=ALU.add))
                        V(lambda: v.reduce_max(out=s_stat[0][:, 5:6], in_=s_gsc[:], axis=AX.X, negate=True))
                    block.vector(_)
                    vv_g1 = sems.val('vv')

                    def _(s, vv_g1=vv_g1):
                        S = EngOps(s, sems, 'sv')
                        s.wait_ge(sems.hdl('vv'), vv_g1)
                        S(lambda: s.activation(out=s_gsc[:], in_=s_gsc[:], func=AF.Exp,
                                               bias=s_stat[0][:, 5:6], scale=1.0,
                                               accum_out=s_stat[0][:, 6:7]))
                    block.scalar(_)
                    sv_g = sems.val('sv')

                    def _(v, sv_g=sv_g):
                        V = EngOps(v, sems, 'vv')
                        v.wait_ge(sems.hdl('sv'), sv_g)
                        V(lambda: v.reciprocal(out=s_stat[0][:, 7:8], in_=s_stat[0][:, 6:7]))
                        V(lambda: v.tensor_scalar(out=s_gsc[:], in0=s_gsc[:],
                                                  scalar1=s_stat[0][:, 7:8], scalar2=None, op0=ALU.mult))
                        V(lambda: v.max(out=s_top8[:], in_=s_gsc[:]))
                        V(lambda: v.tensor_scalar(out=s_w8[:, 0:8], in0=s_gsc[:],
                                                  scalar1=s_top8[:, 1:2], scalar2=None, op0=ALU.is_ge))
                        V(lambda: v.tensor_tensor(out=s_w8[:, 0:8], in0=s_w8[:, 0:8], in1=s_gsc[:],
                                                  op=ALU.mult))
                        V(lambda: v.tensor_copy(out=s_w8[:, 8:16], in_=s_w8t[:]))
                    block.vector(_)
                    vv_g2 = sems.val('vv')

                    def _(sync, i=i, b=b, vv_g2=vv_g2):
                        sync.wait_ge(sems.hdl('vv'), vv_g2)
                        sync.wait_ge(sems.hdl('st'), sems.val('st'))
                        sync.dma_start(out=w8_o[b, i], in_=s_w8[:]).then_inc(*sems.inc('st', 16))
                    block.sync(_)
            else:
                def _(t):
                    T = EngOps(t, sems, 'tv')
                    t.wait_ge(sems.hdl('vv'), sems.val('vv'))
                    t.wait_ge(sems.hdl('sv'), sems.val('sv'))
                    for dc in range(DC):
                        t.ldweights(s_rw[:, dc:dc + 1])
                        op = lambda dc=dc: t.matmul(
                            p_sm[0:1, 0:512], s_rw[:, dc:dc + 1],
                            s_xnfm[:, dc * L: dc * L + 512],
                            start=(dc == 0), stop=(dc == DC - 1))
                        if dc == DC - 1:
                            T(op)
                        else:
                            op()
                    for dc in range(DC):
                        t.ldweights(s_rw[:, dc:dc + 1])
                        op = lambda dc=dc: t.matmul(
                            p_mm[0][0:1, 0:512], s_rw[:, dc:dc + 1],
                            s_xnfm[:, dc * L + 512: dc * L + 1024],
                            start=(dc == 0), stop=(dc == DC - 1))
                        if dc == DC - 1:
                            T(op)
                        else:
                            op()
                block.tensor(_)
                tv_r = sems.val('tv')

                def _(v, tv_r=tv_r):
                    V = EngOps(v, sems, 'vv')
                    v.wait_ge(sems.hdl('tv'), tv_r)
                    v.wait_ge(sems.hdl('st'), sems.val('st'))
                    V(lambda: v.tensor_scalar(out=s_wrow[0:1, 0:512], in0=p_sm[0:1, 0:512],
                                              scalar1=s_rb[0:1, 0:1], scalar2=None, op0=ALU.add))
                    V(lambda: v.tensor_scalar(out=s_wrow[0:1, 512:1024], in0=p_mm[0][0:1, 0:512],
                                              scalar1=s_rb[0:1, 0:1], scalar2=None, op0=ALU.add))
                block.vector(_)
                vv_r = sems.val('vv')

                def _(sync, b=b, vv_r=vv_r):
                    sync.wait_ge(sems.hdl('vv'), vv_r)
                    sync.wait_ge(sems.hdl('st'), sems.val('st'))
                    sync.dma_start(out=wrow_o[b], in_=s_wrow[:]).then_inc(*sems.inc('st', 16))
                block.sync(_)
                mkp['mmfree'][0] = sems.val('vv')

            if F is not None and 'after_batch' in F:
                F['after_batch'](b)

        def _(sync):
            sync.wait_ge(sems.hdl('st'), sems.val('st'))
        block.sync(_)
    return nc


def prep_inputs(layer, x_parts, I):
    base = common_inputs()
    if layer == 0:
        base.update({
            'wqt': np.ascontiguousarray(I['L0_attn_Wq'].T),
            'wkt': np.ascontiguousarray(I['L0_attn_Wk'].T),
            'wvt': np.ascontiguousarray(I['L0_attn_Wv'].T),
            'bq': np.ascontiguousarray(I['L0_attn_bq'].reshape(DC, P).T),
            'bk': np.ascontiguousarray(I['L0_attn_bk'].reshape(DC, P).T),
            'bv': I['L0_attn_bv'].reshape(1, D),
            'gw': np.ascontiguousarray(I['L0_ffn_gate_w']),
            'gb': I['L0_ffn_gate_b'].reshape(1, 8),
            'w8tail': np.tile(np.array([0.5, 0.5, 0, 0, 0, 0, 0, 0], np.float32), (P, 1)),
        })
    else:
        base.update({
            'wqt': np.ascontiguousarray(I['L1_attn_Wq'].T),
            'wkt': np.ascontiguousarray(I['L1_attn_Wk'].T),
            'wvt': np.ascontiguousarray(I['L1_attn_Wv'].T),
            'rw': np.ascontiguousarray(I['L1_ffn_router_w'].reshape(DC, P).T),
            'rb': I['L1_ffn_router_b'].reshape(1, 1),
        })
    return [dict(base, x=np.ascontiguousarray(x_parts[c])) for c in range(NCORE)]


def kernel(**inputs):
    raise NotImplementedError('under construction')



def build_attn(layer, F=None):
    nc = bass.Bass('TRN2', target_bir_lowering=False, num_devices=NCORE) if F is None else F['nc']
    with ExitStack() as ctx:
        E = ctx.enter_context
        q_in = _dio(nc, F, 'qfm', [B, DC, P, L], BF16, 'ExternalInput')
        k_in = _dio(nc, F, 'kag', [NCORE, B, NT, P, DC, P], BF16, 'ExternalInput')
        v_in = _dio(nc, F, 'vfull', [NCORE, B, NT, P, D], BF16, 'ExternalInput')
        dmask_in = _dio(nc, F, 'dmask', [P, 1024], F32, 'ExternalInput')
        if layer == 1:
            wot_in = _dio(nc, F, 'wot', [D, D], F32, 'ExternalInput')
        attn_o = _dio(nc, F, 'attn', [B, NT, P, D], F32, 'ExternalOutput')

        if F is None:
            sems, cm = open_common(nc, ctx)
            block = E(nc.Block())
            emit_common_loads(block, sems, cm)
        else:
            sems, cm, block = F['sems'], F['cm'], F['block']

        s_q = E(nc.sbuf_tensor('s_q', [P, DC * L], BF16))
        s_kb = E(nc.sbuf_tensor('s_kb', [P, 2 * DC * 1024], BF16))
        s_vb = E(nc.sbuf_tensor('s_vb', [P, 2 * NCORE * D], BF16))
        s_S = E(nc.sbuf_tensor('s_S', [P, 1024], F32))
        s_P = E(nc.sbuf_tensor('s_P', [P, 1024], BF16))
        s_PT = E(nc.sbuf_tensor('s_PT', [P, 1024], BF16))
        s_O = E(nc.sbuf_tensor('s_O', [P, D], F32))
        s_fst = E(nc.sbuf_tensor('s_fst', [P, 8], F32))
        s_mask = E(nc.sbuf_tensor('s_mask', [P, 1024], F32))
        s_attn = E(nc.sbuf_tensor('s_attn', [P, D], F32))
        if layer == 1:
            s_wo = E(nc.sbuf_tensor('s_wo', [P, DC * D], BF16))
            s_ob = E(nc.sbuf_tensor('s_ob', [P, D], BF16))
            s_ot = E(nc.sbuf_tensor('s_ot', [P, D], BF16))

        p_S0 = E(nc.psum_tensor('p_S0', [P, 1024], F32))
        p_S1 = E(nc.psum_tensor('p_S1', [P, 1024], F32))
        p_Ss = [p_S0, p_S1]
        p_PT = E(nc.psum_tensor('p_PT', [P, 1024], BF16))
        p_O = E(nc.psum_tensor('p_O', [P, 1024], F32))

        def _(sync):
            sync.wait_ge(sems.hdl('ld'), sems.val('ld'))
            sync.dma_start(out=s_mask[:], in_=dmask_in).then_inc(*sems.inc('ld', 16))
        block.sync(_)

        if layer == 1:
            def _(g):
                g.wait_ge(sems.hdl('gd'), sems.val('gd'))
                g.dma_start(out=s_wo[:].rearrange('p (c f) -> p c f', c=DC),
                            in_=wot_in.rearrange('(c p) f -> p c f', p=P)).then_inc(*sems.inc('gd', 16))
            block.gpsimd(_)
        ld_base = sems.val('ld')
        gd_base = sems.val('gd')

        for eng in ('tensor', 'vector', 'scalar'):
            def _(e):
                e.wait_ge(sems.hdl('ld'), ld_base)
                e.wait_ge(sems.hdl('gd'), gd_base)
            getattr(block, eng)(_)

        mk = {
            'kslot_pe': [0, 0],
            'vslot_pe': [0, 0],
            'evac_sv': 0,
            'evac_sv_slot': [0, 0],
            'exp_sv': 0,
            'max_vv': 0,
            'ptevac_vv': 0,
            'oupd_vv': 0,
            'S_tv': {},
            'ld_kv': {},
        }

        ccw = (F or {}).get('ccw', {})

        def emit_kv_dma(b, j, sl):
            def _(sync, b=b, j=j, sl=sl, gK=mk['kslot_pe'][sl]):
                sync.wait_ge(sems.hdl('tv'), gK)
                if b in ccw:
                    sync.wait_ge(sems.hdl('cc'), ccw[b])
                sync.wait_ge(sems.hdl('ldk'), sems.val('ldk'))
                sync.dma_start(
                    out=s_kb[:, sl * DC * 1024:(sl + 1) * DC * 1024]
                        .rearrange('p (r c q) -> p r c q', r=NCORE, c=DC),
                    in_=k_in[:, b, j].rearrange('r p c q -> p r c q')
                ).then_inc(*sems.inc('ldk', 16))
            block.sync(_)

            def _(g, b=b, j=j, sl=sl, gV=mk['vslot_pe'][sl]):
                g.wait_ge(sems.hdl('tv'), gV)
                if b in ccw:
                    g.wait_ge(sems.hdl('cc'), ccw[b])
                g.wait_ge(sems.hdl('ldv'), sems.val('ldv'))
                g.dma_start(
                    out=s_vb[:, sl * NCORE * D:(sl + 1) * NCORE * D]
                        .rearrange('p (r d) -> p r d', r=NCORE),
                    in_=v_in[:, b, j].rearrange('r p d -> p r d')
                ).then_inc(*sems.inc('ldv', 16))
            block.gpsimd(_)
            mk['ld_kv'][j] = (sems.val('ldk'), sems.val('ldv'))

        def emit_S(i, j, sl):
            def _(t, i=i, sl=sl, ldw=mk['ld_kv'][j], ev=mk['evac_sv_slot'][sl]):
                T = EngOps(t, sems, 'tv')
                t.wait_ge(sems.hdl('ldk'), ldw[0])
                t.wait_ge(sems.hdl('sv'), ev)
                kview = s_kb[:, sl * DC * 1024:(sl + 1) * DC * 1024].rearrange(
                    'p (r c q) -> p r c q', r=NCORE, c=DC)
                for dc in range(DC):
                    qt = s_q[:, dc * L + i * P: dc * L + (i + 1) * P]
                    t.ldweights(qt)
                    t.matmul(p_Ss[sl][:, 0:512], qt, kview[:, 0:4, dc, :],
                             start=(dc == 0), stop=(dc == DC - 1))
                    op = lambda dc=dc, qt=qt: t.matmul(
                        p_Ss[sl][:, 512:1024], qt, kview[:, 4:8, dc, :],
                        start=(dc == 0), stop=(dc == DC - 1))
                    if dc == DC - 1:
                        T(op)
                    else:
                        op()
            block.tensor(_)
            mk['kslot_pe'][sl] = sems.val('tv')
            mk['S_tv'][j] = sems.val('tv')

        for b in range(B):
            def _(sync, b=b):
                sync.wait_ge(sems.hdl('tv'), sems.val('tv'))
                sync.wait_ge(sems.hdl('ld'), sems.val('ld'))
                sync.dma_start(out=s_q[:].rearrange('p (c l) -> p c l', c=DC),
                               in_=q_in[b].rearrange('c p l -> p c l')).then_inc(*sems.inc('ld', 16))
            block.sync(_)
            ld_q = sems.val('ld')

            def _(t, ld_q=ld_q):
                t.wait_ge(sems.hdl('ld'), ld_q)
            block.tensor(_)

            for i in range(NT):
                nblk = i + 1
                emit_kv_dma(b, 0, 0)
                emit_S(i, 0, 0)
                for j in range(nblk):
                    sl = j % 2
                    if j + 1 < nblk:
                        emit_kv_dma(b, j + 1, (j + 1) % 2)
                        emit_S(i, j + 1, (j + 1) % 2)

                    def _(v, i=i, j=j, sl=sl, stv=mk['S_tv'][j]):
                        V = EngOps(v, sems, 'vv')
                        v.wait_ge(sems.hdl('tv'), stv)
                        if j == i:
                            V(lambda: v.tensor_tensor(out=p_Ss[sl][:], in0=p_Ss[sl][:],
                                                      in1=s_mask[:], op=ALU.add))
                        V(lambda: v.reduce_max(out=s_fst[:, 0:1], in_=p_Ss[sl][:], axis=AX.X,
                                               negate=True))
                        if j == 0:
                            V(lambda: v.tensor_scalar(out=s_fst[:, 2:3], in0=s_fst[:, 0:1],
                                                      scalar1=col(cm, 'inv32'), scalar2=None,
                                                      op0=ALU.mult))
                        else:
                            V(lambda: v.tensor_scalar(out=s_fst[:, 0:1], in0=s_fst[:, 0:1],
                                                      scalar1=col(cm, 'inv32'), scalar2=None,
                                                      op0=ALU.mult))
                            V(lambda: v.tensor_tensor(out=s_fst[:, 2:3], in0=s_fst[:, 1:2],
                                                      in1=s_fst[:, 0:1], op=ALU.min))
                            V(lambda: v.tensor_tensor(out=s_fst[:, 3:4], in0=s_fst[:, 2:3],
                                                      in1=s_fst[:, 1:2], op=ALU.subtract))
                    block.vector(_)
                    mk['max_vv'] = sems.val('vv')
                    vv_st = sems.val('vv')

                    def _(s, j=j, sl=sl, vv_st=vv_st, ptv=mk.get('pt_tv', 0)):
                        S = EngOps(s, sems, 'sv')
                        s.wait_ge(sems.hdl('vv'), vv_st)
                        s.wait_ge(sems.hdl('tv'), ptv)
                        S(lambda: s.activation(out=s_P[:], in_=p_Ss[sl][:], func=AF.Exp,
                                               bias=s_fst[:, 2:3], scale=0.03125,
                                               accum_out=s_fst[:, 4:5]))
                        if j > 0:
                            S(lambda: s.activation(out=s_fst[:, 5:6], in_=s_fst[:, 3:4],
                                                   func=AF.Exp, bias=col(cm, 'zero'), scale=1.0))
                    block.scalar(_)
                    mk['evac_sv_slot'][sl] = sems.val('sv')
                    mk['exp_sv'] = sems.val('sv')
                    sv_P = sems.val('sv')

                    def _(v, j=j, sv_P=sv_P):
                        V = EngOps(v, sems, 'vv')
                        v.wait_ge(sems.hdl('sv'), sv_P)
                        if j == 0:
                            V(lambda: v.tensor_copy(out=s_fst[:, 6:7], in_=s_fst[:, 4:5]))
                        else:
                            V(lambda: v.scalar_tensor_tensor(out=s_fst[:, 6:7], in0=s_fst[:, 6:7],
                                                             scalar=s_fst[:, 5:6], in1=s_fst[:, 4:5],
                                                             op0=ALU.mult, op1=ALU.add))
                        V(lambda: v.tensor_copy(out=s_fst[:, 1:2], in_=s_fst[:, 2:3]))
                    block.vector(_)

                    def _(t, sv_P=sv_P, pev=mk['ptevac_vv']):
                        T = EngOps(t, sems, 'tv')
                        t.wait_ge(sems.hdl('sv'), sv_P)
                        t.wait_ge(sems.hdl('vv'), pev)
                        for kc in range(DC):
                            op = lambda kc=kc: t.transpose(p_PT[:, kc * P:(kc + 1) * P],
                                                           s_P[:, kc * P:(kc + 1) * P],
                                                           cm['ident_b'][:])
                            if kc == DC - 1:
                                T(op)
                            else:
                                op()
                    block.tensor(_)
                    mk['pt_tv'] = sems.val('tv')
                    tv_PT = sems.val('tv')

                    def _(v, tv_PT=tv_PT):
                        V = EngOps(v, sems, 'vv')
                        v.wait_ge(sems.hdl('tv'), tv_PT)
                        V(lambda: v.tensor_copy(out=s_PT[:], in_=p_PT[:]))
                    block.vector(_)
                    mk['ptevac_vv'] = sems.val('vv')
                    vv_PT = sems.val('vv')

                    def _(t, sl=sl, vv_PT=vv_PT, ou=mk['oupd_vv'], ldv=mk['ld_kv'][j][1]):
                        T = EngOps(t, sems, 'tv')
                        t.wait_ge(sems.hdl('vv'), vv_PT)
                        t.wait_ge(sems.hdl('ldv'), ldv)
                        t.wait_ge(sems.hdl('vv'), ou)
                        for kc in range(DC):
                            pt = s_PT[:, kc * P:(kc + 1) * P]
                            vbase = sl * NCORE * D + kc * D
                            t.ldweights(pt)
                            t.matmul(p_O[:, 0:512], pt, s_vb[:, vbase: vbase + 512],
                                     start=(kc == 0), stop=(kc == DC - 1))
                            op = lambda kc=kc, pt=pt, vbase=vbase: t.matmul(
                                p_O[:, 512:1024], pt, s_vb[:, vbase + 512: vbase + 1024],
                                start=(kc == 0), stop=(kc == DC - 1))
                            if kc == DC - 1:
                                T(op)
                            else:
                                op()
                    block.tensor(_)
                    mk['vslot_pe'][sl] = sems.val('tv')
                    tv_PV = sems.val('tv')

                    def _(v, j=j, tv_PV=tv_PV):
                        V = EngOps(v, sems, 'vv')
                        v.wait_ge(sems.hdl('tv'), tv_PV)
                        if j == 0:
                            V(lambda: v.tensor_copy(out=s_O[:], in_=p_O[:]))
                        else:
                            V(lambda: v.scalar_tensor_tensor(out=s_O[:], in0=s_O[:],
                                                             scalar=s_fst[:, 5:6], in1=p_O[:],
                                                             op0=ALU.mult, op1=ALU.add))
                    block.vector(_)
                    mk['oupd_vv'] = sems.val('vv')

                    if F is not None and 'ileave' in F:
                        F['ileave']()

                def _(v):
                    V = EngOps(v, sems, 'vv')
                    v.wait_ge(sems.hdl('st'), sems.val('st'))
                    V(lambda: v.reciprocal(out=s_fst[:, 7:8], in_=s_fst[:, 6:7]))
                    if layer == 0:
                        V(lambda: v.tensor_scalar(out=s_attn[:], in0=s_O[:],
                                                  scalar1=s_fst[:, 7:8], scalar2=None, op0=ALU.mult))
                    else:
                        V(lambda: v.tensor_scalar(out=s_ob[:], in0=s_O[:],
                                                  scalar1=s_fst[:, 7:8], scalar2=None, op0=ALU.mult))
                block.vector(_)
                vv_fin = sems.val('vv')

                if layer == 0:
                    def _(sync, b=b, i=i, vv_fin=vv_fin):
                        sync.wait_ge(sems.hdl('vv'), vv_fin)
                        sync.wait_ge(sems.hdl('ld'), sems.val('ld'))
                        sync.dma_start(out=attn_o[b, i], in_=s_attn[:]).then_inc(*sems.inc('st', 16))
                    block.sync(_)
                else:
                    def _(t, vv_fin=vv_fin, pev=mk['ptevac_vv']):
                        T = EngOps(t, sems, 'tv')
                        t.wait_ge(sems.hdl('vv'), vv_fin)
                        t.wait_ge(sems.hdl('vv'), pev)
                        for dc in range(DC):
                            op = lambda dc=dc: t.transpose(p_PT[:, dc * P:(dc + 1) * P],
                                                           s_ob[:, dc * P:(dc + 1) * P],
                                                           cm['ident_b'][:])
                            if dc == DC - 1:
                                T(op)
                            else:
                                op()
                    block.tensor(_)
                    mk['pt_tv'] = sems.val('tv')
                    tv_otr = sems.val('tv')

                    def _(v, tv_otr=tv_otr):
                        V = EngOps(v, sems, 'vv')
                        v.wait_ge(sems.hdl('tv'), tv_otr)
                        V(lambda: v.tensor_copy(out=s_ot[:], in_=p_PT[:]))
                    block.vector(_)
                    mk['ptevac_vv'] = sems.val('vv')
                    vv_otr = sems.val('vv')

                    def _(t, vv_otr=vv_otr, ou=mk['oupd_vv']):
                        T = EngOps(t, sems, 'tv')
                        t.wait_ge(sems.hdl('vv'), vv_otr)
                        t.wait_ge(sems.hdl('vv'), ou)
                        t.wait_ge(sems.hdl('gd'), gd_base)
                        for dc in range(DC):
                            ot = s_ot[:, dc * P:(dc + 1) * P]
                            t.ldweights(ot)
                            t.matmul(p_O[:, 0:512], ot, s_wo[:, dc * D: dc * D + 512],
                                     start=(dc == 0), stop=(dc == DC - 1))
                            op = lambda dc=dc, ot=ot: t.matmul(
                                p_O[:, 512:1024], ot, s_wo[:, dc * D + 512: dc * D + 1024],
                                start=(dc == 0), stop=(dc == DC - 1))
                            if dc == DC - 1:
                                T(op)
                            else:
                                op()
                    block.tensor(_)
                    mk['vslot_pe'][0] = sems.val('tv')
                    mk['vslot_pe'][1] = sems.val('tv')
                    tv_wo = sems.val('tv')

                    def _(v, tv_wo=tv_wo):
                        V = EngOps(v, sems, 'vv')
                        v.wait_ge(sems.hdl('tv'), tv_wo)
                        v.wait_ge(sems.hdl('st'), sems.val('st'))
                        V(lambda: v.tensor_copy(out=s_attn[:], in_=p_O[:]))
                    block.vector(_)
                    mk['oupd_vv'] = sems.val('vv')
                    vv_wo = sems.val('vv')

                    def _(sync, b=b, i=i, vv_wo=vv_wo):
                        sync.wait_ge(sems.hdl('vv'), vv_wo)
                        sync.dma_start(out=attn_o[b, i], in_=s_attn[:]).then_inc(*sems.inc('st', 16))
                    block.sync(_)

        def _(sync):
            sync.wait_ge(sems.hdl('st'), sems.val('st'))
        block.sync(_)
    return nc


def attn_inputs(layer, prep_results, I, core):
    kag = np.ascontiguousarray(np.stack([np.asarray(r['kfm']) for r in prep_results]))
    vfull = np.ascontiguousarray(np.stack([np.asarray(r['vtm']) for r in prep_results]))
    maps = []
    for c in range(NCORE):
        m = dict(common_inputs())
        m['qfm'] = np.ascontiguousarray(prep_results[c]['qfm'])
        m['kag'] = kag
        m['vfull'] = vfull
        o = np.arange(1024)[None, :]
        p = np.arange(P)[:, None]
        m['dmask'] = np.where(o > 128 * c + p, -1e9, 0.0).astype(np.float32)
        if layer == 1:
            m['wot'] = np.ascontiguousarray(I['L1_attn_Wo'].T)
        maps.append(m)
    return maps



NEXP = 10
H = 2048
HC = H // P


def build_moe(F=None):
    nc = bass.Bass('TRN2', target_bir_lowering=False, num_devices=NCORE) if F is None else F['nc']
    with ExitStack() as ctx:
        E = ctx.enter_context
        x_in = _dio(nc, F, 'x', [B, NT, P, D], F32, 'ExternalInput')
        attn_in = _dio(nc, F, 'attn', [B, NT, P, D], F32, 'ExternalInput')
        xnfm_in = _dio(nc, F, 'xnfm', [B, DC, P, L], BF16, 'ExternalInput')
        w8_in = _dio(nc, F, 'w8e', [B, NT, P, 16], F32, 'ExternalInput')
        w1_in = _dio(nc, F, 'w1all', [NEXP, D, H], F32, 'ExternalInput')
        w2_in = _dio(nc, F, 'w2all', [NEXP, D, H], F32, 'ExternalInput')
        w3_in = _dio(nc, F, 'w3all', [NEXP, H, D], F32, 'ExternalInput')
        b1_in = _dio(nc, F, 'b1fm', [P, NEXP * HC], F32, 'ExternalInput')
        b2_in = _dio(nc, F, 'b2fm', [P, NEXP * HC], F32, 'ExternalInput')
        b3_in = _dio(nc, F, 'b3all', [16, D], F32, 'ExternalInput')
        x1_o = _dio(nc, F, 'x1', [B, NT, P, D], F32, 'ExternalOutput')

        if F is None:
            sems, cm = open_common(nc, ctx)
            block = E(nc.Block())
            emit_common_loads(block, sems, cm)
        else:
            sems, cm, block = F['sems'], F['cm'], F['block']

        s_xnfm = E(nc.sbuf_tensor('s_xnfm', [P, B * DC * L], BF16))
        s_acc = E(nc.sbuf_tensor('s_acc', [P, B * NT * D], F32))
        s_g = E(nc.sbuf_tensor('s_g', [P, HC * L], BF16))
        s_w3 = E(nc.sbuf_tensor('s_w3', [P, HC * D], BF16))
        s_w12 = E(nc.sbuf_tensor('s_w12', [P, 4 * DC * P], BF16))
        s_h1 = E(nc.sbuf_tensor('s_h1', [P, L], F32))
        s_b1 = E(nc.sbuf_tensor('s_b1', [P, NEXP * HC], F32))
        s_b2 = E(nc.sbuf_tensor('s_b2', [P, NEXP * HC], F32))
        s_b3 = E(nc.sbuf_tensor('s_b3', [16, D], F32))
        s_w8 = E(nc.sbuf_tensor('s_w8', [P, B * NT * 16], F32))
        s_w8t = E(nc.sbuf_tensor('s_w8t', [16, B * NT * P], F32))
        s_xt = E(nc.sbuf_tensor('s_xt', [P, D], F32))
        s_at = E(nc.sbuf_tensor('s_at', [P, D], F32))
        s_x1 = E(nc.sbuf_tensor('s_x1', [P, D], F32))

        p_A0 = E(nc.psum_tensor('p_A0', [P, 1024], F32))
        p_A1 = E(nc.psum_tensor('p_A1', [P, 1024], F32))
        p_B0 = E(nc.psum_tensor('p_B0', [P, 1024], F32))
        p_T16 = E(nc.psum_tensor('p_T16', [16, P], F32))
        pAs = [p_A0, p_A1]
        ypair = [[p_A0[:, 0:512], p_A0[:, 512:1024]], [p_A1[:, 0:512], p_A1[:, 512:1024]]]

        def _(sync):
            sync.wait_ge(sems.hdl('ld'), sems.val('ld'))
            sync.dma_start(out=s_xnfm[:].rearrange('p (b c l) -> p b c l', b=B, c=DC),
                           in_=xnfm_in.rearrange('b c p l -> p b c l')).then_inc(*sems.inc('ld', 16))
            sync.wait_ge(sems.hdl('ld'), sems.val('ld'))
            sync.dma_start(out=s_b1[:], in_=b1_in).then_inc(*sems.inc('ld', 16))
            sync.wait_ge(sems.hdl('ld'), sems.val('ld'))
            sync.dma_start(out=s_b2[:], in_=b2_in).then_inc(*sems.inc('ld', 16))
            sync.wait_ge(sems.hdl('ld'), sems.val('ld'))
            sync.dma_start(out=s_w8[:].rearrange('p (b i e) -> p b i e', b=B, i=NT),
                           in_=w8_in.rearrange('b i p e -> p b i e')).then_inc(*sems.inc('ld', 16))
            sync.wait_ge(sems.hdl('ld'), sems.val('ld'))
            sync.dma_start(out=s_b3[:], in_=b3_in).then_inc(*sems.inc('ld', 16))
        block.sync(_)
        ld0 = sems.val('ld')
        gd0 = sems.val('gd')

        for eng in ('tensor', 'vector', 'scalar'):
            def _(e):
                e.wait_ge(sems.hdl('ld'), ld0)
                e.wait_ge(sems.hdl('gd'), gd0)
            getattr(block, eng)(_)

        for b in range(B):
            for i in range(NT):
                def _(t, b=b, i=i):
                    T = EngOps(t, sems, 'tv')
                    t.wait_ge(sems.hdl('vv'), sems.val('vv'))
                    T(lambda: t.transpose(p_T16[:], s_w8[:, (b * NT + i) * 16:(b * NT + i + 1) * 16],
                                          cm['ident_f'][:]))
                block.tensor(_)
                tv_t = sems.val('tv')

                def _(v, b=b, i=i, tv_t=tv_t):
                    V = EngOps(v, sems, 'vv')
                    v.wait_ge(sems.hdl('tv'), tv_t)
                    V(lambda: v.tensor_copy(out=s_w8t[:, (b * NT + i) * P:(b * NT + i + 1) * P],
                                            in_=p_T16[:]))
                block.vector(_)
        vv_w8t = sems.val('vv')

        for b in range(B):
            for i in range(NT):
                def _(t, b=b, i=i, vv_w8t=vv_w8t):
                    T = EngOps(t, sems, 'tv')
                    t.wait_ge(sems.hdl('vv'), vv_w8t)
                    t.wait_ge(sems.hdl('vv'), sems.val('vv'))
                    wt = s_w8t[:, (b * NT + i) * P:(b * NT + i + 1) * P]
                    t.matmul(p_B0[:, 0:512], wt, s_b3[:, 0:512], start=True, stop=True)
                    T(lambda wt=wt: t.matmul(p_B0[:, 512:1024], wt, s_b3[:, 512:1024],
                                             start=True, stop=True))
                block.tensor(_)
                tv_b3 = sems.val('tv')

                def _(v, b=b, i=i, tv_b3=tv_b3):
                    V = EngOps(v, sems, 'vv')
                    v.wait_ge(sems.hdl('tv'), tv_b3)
                    V(lambda: v.tensor_copy(out=s_acc[:, (b * NT + i) * D:(b * NT + i + 1) * D],
                                            in_=p_B0[:, 0:1024]))
                block.vector(_)

        w12_guard = [0, 0]
        mkA = {'silu_sv': [0, 0], 'g_vv': sems.val('vv'), 'y_vv': [0, 0], 'h1_tv': 0,
               'w3_free': 0}
        for e in range(NEXP):
            def _(g, e=e, w3f=mkA['w3_free']):
                g.wait_ge(sems.hdl('tv'), w3f)
                g.wait_ge(sems.hdl('ldv'), sems.val('ldv'))
                g.dma_start(out=s_w3[:].rearrange('p (c f) -> p c f', c=HC),
                            in_=w3_in[e].rearrange('(c p) f -> p c f', p=P)
                            ).then_inc(*sems.inc('ldv', 16))
            block.gpsimd(_)
            ldv_w3 = sems.val('ldv')
            for b in range(B):
                for hc in range(HC):
                    sl = hc % 2
                    def _(g, e=e, hc=hc, sl=sl, guard=w12_guard[sl]):
                        g.wait_ge(sems.hdl('tv'), guard)
                        g.wait_ge(sems.hdl('gd'), sems.val('gd'))
                        g.dma_start(out=s_w12[:, sl * 2 * D: sl * 2 * D + D]
                                    .rearrange('p (c q) -> p c q', c=DC),
                                    in_=w1_in[e].rearrange('(c p) f -> p c f', p=P)[:, :, hc * P:(hc + 1) * P]
                                    ).then_inc(*sems.inc('gd', 16))
                        g.wait_ge(sems.hdl('gd'), sems.val('gd'))
                        g.dma_start(out=s_w12[:, sl * 2 * D + D: sl * 2 * D + 2 * D]
                                    .rearrange('p (c q) -> p c q', c=DC),
                                    in_=w2_in[e].rearrange('(c p) f -> p c f', p=P)[:, :, hc * P:(hc + 1) * P]
                                    ).then_inc(*sems.inc('gd', 16))
                    block.gpsimd(_)
                    gd_c = sems.val('gd')

                    def _(t, b=b, sl=sl, gd_c=gd_c, sa=mkA['silu_sv'][sl], gv=mkA['g_vv'],
                          yv=mkA['y_vv'][sl]):
                        T = EngOps(t, sems, 'tv')
                        t.wait_ge(sems.hdl('gd'), gd_c)
                        t.wait_ge(sems.hdl('sv'), sa)
                        t.wait_ge(sems.hdl('vv'), yv)
                        for dc in range(DC):
                            w1t = s_w12[:, sl * 2 * D + dc * P: sl * 2 * D + (dc + 1) * P]
                            xb = s_xnfm[:, (b * DC + dc) * L: (b * DC + dc) * L + L]
                            t.ldweights(w1t)
                            t.matmul(pAs[sl][:, 0:512], w1t, xb[:, 0:512],
                                     start=(dc == 0), stop=(dc == DC - 1))
                            op = lambda dc=dc, w1t=w1t, xb=xb: t.matmul(
                                pAs[sl][:, 512:1024], w1t, xb[:, 512:1024],
                                start=(dc == 0), stop=(dc == DC - 1))
                            if dc == DC - 1:
                                T(op)
                            else:
                                op()
                        mkA['h1_tv'] = sems.val('tv')
                        t.wait_ge(sems.hdl('vv'), gv)
                        for dc in range(DC):
                            w2t = s_w12[:, sl * 2 * D + D + dc * P: sl * 2 * D + D + (dc + 1) * P]
                            xb = s_xnfm[:, (b * DC + dc) * L: (b * DC + dc) * L + L]
                            t.ldweights(w2t)
                            t.matmul(p_B0[:, 0:512], w2t, xb[:, 0:512],
                                     start=(dc == 0), stop=(dc == DC - 1))
                            op = lambda dc=dc, w2t=w2t, xb=xb: t.matmul(
                                p_B0[:, 512:1024], w2t, xb[:, 512:1024],
                                start=(dc == 0), stop=(dc == DC - 1))
                            if dc == DC - 1:
                                T(op)
                            else:
                                op()
                    block.tensor(_)
                    tv_h = sems.val('tv')
                    w12_guard[sl] = tv_h

                    def _(s, e=e, hc=hc, sl=sl, h1tv=mkA['h1_tv'], gprev=mkA['g_vv']):
                        S = EngOps(s, sems, 'sv')
                        s.wait_ge(sems.hdl('tv'), h1tv)
                        s.wait_ge(sems.hdl('vv'), gprev)
                        S(lambda: s.activation(out=s_h1[:], in_=pAs[sl][:], func=AF.Silu,
                                               bias=s_b1[:, e * HC + hc: e * HC + hc + 1], scale=1.0))
                    block.scalar(_)
                    mkA['silu_sv'][sl] = sems.val('sv')
                    sv_h = sems.val('sv')

                    def _(v, e=e, hc=hc, sv_h=sv_h, h2tv=tv_h):
                        V = EngOps(v, sems, 'vv')
                        v.wait_ge(sems.hdl('sv'), sv_h)
                        v.wait_ge(sems.hdl('tv'), h2tv)
                        V(lambda: v.scalar_tensor_tensor(
                            out=s_g[:, hc * L:(hc + 1) * L], in0=p_B0[:],
                            scalar=s_b2[:, e * HC + hc: e * HC + hc + 1], in1=s_h1[:],
                            op0=ALU.add, op1=ALU.mult))
                    block.vector(_)
                    mkA['g_vv'] = sems.val('vv')

                vv_gall = sems.val('vv')

                for i in range(NT):
                    yp = i % 2
                    def _(t, b=b, i=i, yp=yp, ldv_w3=ldv_w3, vv_gall=vv_gall,
                          yv=mkA['y_vv'][yp], sa=mkA['silu_sv'][yp]):
                        T = EngOps(t, sems, 'tv')
                        t.wait_ge(sems.hdl('ldv'), ldv_w3)
                        t.wait_ge(sems.hdl('vv'), vv_gall)
                        t.wait_ge(sems.hdl('vv'), yv)
                        t.wait_ge(sems.hdl('sv'), sa)
                        ys = ypair[yp]
                        for hcx in range(HC):
                            gt = s_g[:, hcx * L + i * P: hcx * L + (i + 1) * P]
                            t.ldweights(gt)
                            t.matmul(ys[0], gt, s_w3[:, hcx * D: hcx * D + 512],
                                     start=(hcx == 0), stop=(hcx == HC - 1))
                            op = lambda hcx=hcx, gt=gt, ys=ys: t.matmul(
                                ys[1], gt, s_w3[:, hcx * D + 512: hcx * D + 1024],
                                start=(hcx == 0), stop=(hcx == HC - 1))
                            if hcx == HC - 1:
                                T(op)
                            else:
                                op()
                    block.tensor(_)
                    tv_y = sems.val('tv')
                    if b == 1 and i == NT - 1:
                        mkA['w3_free'] = tv_y

                    def _(v, e=e, b=b, i=i, yp=yp, tv_y=tv_y):
                        V = EngOps(v, sems, 'vv')
                        v.wait_ge(sems.hdl('tv'), tv_y)
                        w8col = s_w8[:, (b * NT + i) * 16 + e: (b * NT + i) * 16 + e + 1]
                        yfull = pAs[yp][:, 0:1024]
                        accsl = s_acc[:, (b * NT + i) * D: (b * NT + i + 1) * D]
                        V(lambda: v.scalar_tensor_tensor(
                            out=accsl, in0=yfull, scalar=w8col, in1=accsl,
                            op0=ALU.mult, op1=ALU.add))
                    block.vector(_)
                    mkA['y_vv'][yp] = sems.val('vv')

        for b in range(B):
            for i in range(NT):
                def _(sync, b=b, i=i):
                    sync.wait_ge(sems.hdl('vv'), sems.val('vv'))
                    sync.wait_ge(sems.hdl('ld'), sems.val('ld'))
                    sync.dma_start(out=s_xt[:], in_=x_in[b, i]).then_inc(*sems.inc('ld', 16))
                    sync.wait_ge(sems.hdl('ld'), sems.val('ld'))
                    sync.dma_start(out=s_at[:], in_=attn_in[b, i]).then_inc(*sems.inc('ld', 16))
                block.sync(_)
                ld_c = sems.val('ld')

                def _(v, b=b, i=i, ld_c=ld_c):
                    V = EngOps(v, sems, 'vv')
                    v.wait_ge(sems.hdl('ld'), ld_c)
                    v.wait_ge(sems.hdl('st'), sems.val('st'))
                    V(lambda: v.tensor_tensor(out=s_x1[:], in0=s_xt[:], in1=s_at[:], op=ALU.add))
                    V(lambda: v.tensor_tensor(out=s_x1[:], in0=s_x1[:],
                                              in1=s_acc[:, (b * NT + i) * D:(b * NT + i + 1) * D],
                                              op=ALU.add))
                block.vector(_)
                vv_c = sems.val('vv')

                def _(sync, b=b, i=i, vv_c=vv_c):
                    sync.wait_ge(sems.hdl('vv'), vv_c)
                    sync.dma_start(out=x1_o[b, i], in_=s_x1[:]).then_inc(*sems.inc('st', 16))
                block.sync(_)

        def _(sync):
            sync.wait_ge(sems.hdl('st'), sems.val('st'))
        block.sync(_)
    return nc


def moe_inputs(x_parts, prep_results, attn_results, I):
    w1all = np.ascontiguousarray(np.concatenate([I['L0_ffn_rw1'], I['L0_ffn_sw1']], 0))
    w2all = np.ascontiguousarray(np.concatenate([I['L0_ffn_rw2'], I['L0_ffn_sw2']], 0))
    w3all = np.ascontiguousarray(np.concatenate([I['L0_ffn_rw3'], I['L0_ffn_sw3']], 0))
    b1all = np.concatenate([I['L0_ffn_rb1'], I['L0_ffn_sb1']], 0)
    b2all = np.concatenate([I['L0_ffn_rb2'], I['L0_ffn_sb2']], 0)
    b3all = np.zeros((16, D), np.float32)
    b3all[:NEXP] = np.concatenate([I['L0_ffn_rb3'], I['L0_ffn_sb3']], 0)
    b1fm = np.ascontiguousarray(b1all.reshape(NEXP, HC, P).transpose(2, 0, 1).reshape(P, NEXP * HC))
    b2fm = np.ascontiguousarray(b2all.reshape(NEXP, HC, P).transpose(2, 0, 1).reshape(P, NEXP * HC))
    maps = []
    for c in range(NCORE):
        m = dict(common_inputs())
        m['x'] = np.ascontiguousarray(x_parts[c])
        m['attn'] = np.ascontiguousarray(np.asarray(attn_results[c]['attn']))
        m['xnfm'] = np.ascontiguousarray(np.asarray(prep_results[c]['xnfm']))
        w8 = np.asarray(prep_results[c]['w8'], np.float32)
        w8e = np.zeros((B, NT, P, 16), np.float32)
        w8e[..., :8] = w8
        w8e[..., 8:10] = 0.5
        m['w8e'] = w8e
        m.update(w1all=w1all, w2all=w2all, w3all=w3all, b1fm=b1fm, b2fm=b2fm, b3all=b3all)
        maps.append(m)
    return maps




def emit_router_bis(nc, E, sems, cm, block, wfull_ap, pr, cc_after=None):
    bb = {}
    wpos = nc.dram_tensor('rb_wpos', [B, 64, P], F32, kind='Internal')
    s_w64 = E(nc.sbuf_tensor('rb_w64', [64, P], F32))
    s_wqT = E(nc.sbuf_tensor('rb_wqT', [P, 128], F32))
    s_whi = E(nc.sbuf_tensor('rb_whi', [P, 128], BF16))
    s_wlo = E(nc.sbuf_tensor('rb_wlo', [P, 128], BF16))
    s_wlof = E(nc.sbuf_tensor('rb_wlof', [P, 64], F32))
    s_bp = E(nc.sbuf_tensor('rb_bp', [P, 8], F32))
    s_onespp = E(nc.sbuf_tensor('rb_onespp', [P, P], F32))
    s_mid2 = E(nc.sbuf_tensor('rb_mid2', [P, 2], F32))
    s_cnt2 = E(nc.sbuf_tensor('rb_cnt2', [P, 2], F32))
    s_cmp = E(nc.sbuf_tensor('rb_cmp', [P, 64], F32))
    s_cnt = E(nc.sbuf_tensor('rb_cnt', [P, 1], F32))
    s_bis = E(nc.sbuf_tensor('rb_bis', [1, 4], F32))
    bb.update(s_wqT=s_wqT, s_whi=s_whi, s_wlo=s_wlo, s_bp=s_bp)
    st = {'rnd': 0, 'tv': None}
    NRB = 50

    def loads():
        def _(sync):
            if cc_after is not None:
                sync.wait_ge(sems.hdl('cc'), cc_after)
            sync.wait_ge(sems.hdl('ld'), sems.val('ld'))
            sync.dma_start(out=wpos.ap().rearrange('b (i r) q -> b i r q', r=NCORE),
                           in_=wfull_ap.rearrange('r b o (i q) -> b i r (o q)', q=P)
                           ).then_inc(*sems.inc('ld', 16))
        block.sync(_)

        for b in range(B):
            def _(sync, b=b):
                sync.wait_ge(sems.hdl('vv'), sems.val('vv'))
                sync.wait_ge(sems.hdl('tv'), sems.val('tv'))
                sync.wait_ge(sems.hdl('ld'), sems.val('ld'))
                sync.dma_start(out=s_w64[:], in_=wpos.ap()[b]).then_inc(*sems.inc('ld', 16))
            block.sync(_)
            ld_w = sems.val('ld')

            def _(t, ld_w=ld_w):
                T = EngOps(t, sems, 'tv')
                t.wait_ge(sems.hdl('ld'), ld_w)
                t.wait_ge(sems.hdl('vv'), sems.val('vv'))
                T(lambda: t.transpose(pr[:, 0:64], s_w64[:], cm['ident_f'][0:64, 0:64]))
            block.tensor(_)
            tv_w = sems.val('tv')

            def _(v, b=b, tv_w=tv_w):
                V = EngOps(v, sems, 'vv')
                v.wait_ge(sems.hdl('tv'), tv_w)
                wq = s_wqT[:, 64 * b:64 * b + 64]
                wh = s_whi[:, 64 * b:64 * b + 64]
                wl = s_wlo[:, 64 * b:64 * b + 64]
                V(lambda: v.tensor_copy(out=wq, in_=pr[:, 0:64]))
                V(lambda: v.tensor_copy(out=wh, in_=wq))
                V(lambda: v.tensor_copy(out=s_wlof[:], in_=wh))
                V(lambda: v.tensor_tensor(out=s_wlof[:], in0=wq, in1=s_wlof[:], op=ALU.subtract))
                V(lambda: v.tensor_copy(out=wl, in_=s_wlof[:]))
                V(lambda: v.reduce_max(out=s_cnt[:], in_=wq, axis=AX.X))
            block.vector(_)
            vv_hi0 = sems.val('vv')

            def _(t, vv_hi0=vv_hi0):
                T = EngOps(t, sems, 'tv')
                t.wait_ge(sems.hdl('vv'), vv_hi0)
                T(lambda: t.transpose(pr[0:1, 64:192], s_cnt[:], cm['ident_f'][:]))
            block.tensor(_)
            tv_hi = sems.val('tv')

            def _(v, b=b, tv_hi=tv_hi):
                V = EngOps(v, sems, 'vv')
                v.wait_ge(sems.hdl('tv'), tv_hi)
                V(lambda: v.reduce_max(out=s_bis[:, 2 + b:3 + b], in_=pr[0:1, 64:192], axis=AX.X))
                V(lambda: v.reduce_max(out=s_cnt[:], in_=s_wqT[:, 64 * b:64 * b + 64],
                                       axis=AX.X, negate=True))
            block.vector(_)
            vv_lo0 = sems.val('vv')

            def _(t, vv_lo0=vv_lo0):
                T = EngOps(t, sems, 'tv')
                t.wait_ge(sems.hdl('vv'), vv_lo0)
                T(lambda: t.transpose(pr[0:1, 64:192], s_cnt[:], cm['ident_f'][:]))
            block.tensor(_)
            tv_lo = sems.val('tv')

            def _(v, b=b, tv_lo=tv_lo):
                V = EngOps(v, sems, 'vv')
                v.wait_ge(sems.hdl('tv'), tv_lo)
                lob = s_bis[:, b:b + 1]
                V(lambda: v.reduce_max(out=lob, in_=pr[0:1, 64:192], axis=AX.X))
                V(lambda: v.tensor_scalar(out=lob, in0=lob,
                                          scalar1=cm['cols'][0:1, COLIDX['neg_one']:COLIDX['neg_one'] + 1],
                                          scalar2=None, op0=ALU.mult))
                V(lambda: v.tensor_scalar(out=lob, in0=lob,
                                          scalar1=cm['cols'][0:1, COLIDX['neg_one']:COLIDX['neg_one'] + 1],
                                          scalar2=None, op0=ALU.add))
            block.vector(_)

        def _(t):
            T = EngOps(t, sems, 'tv')
            t.wait_ge(sems.hdl('vv'), sems.val('vv'))
            t.matmul(pr[:, 256:260], cm['onesrow_f'][:], s_bis[0:1, 0:4], start=True, stop=True)
            T(lambda: t.matmul(pr[:, 320:448], cm['onesrow_f'][:], cm['onesrow_f'][:],
                               start=True, stop=True))
        block.tensor(_)
        tv_b4 = sems.val('tv')

        def _(v, tv_b4=tv_b4):
            V = EngOps(v, sems, 'vv')
            v.wait_ge(sems.hdl('tv'), tv_b4)
            V(lambda: v.tensor_copy(out=s_bp[:, 0:4], in_=pr[:, 256:260]))
            V(lambda: v.tensor_copy(out=s_onespp[:], in_=pr[:, 320:448]))
        block.vector(_)

    def round():
        rnd = st['rnd']
        if rnd > NRB:
            return False
        last = rnd == NRB

        def _(v, rnd=rnd, last=last, tvt=st['tv']):
            V = EngOps(v, sems, 'vv')
            if tvt is not None:
                v.wait_ge(sems.hdl('tv'), tvt)
            ck = col(cm, 'count_k')
            if rnd > 0:
                V(lambda: v.tensor_scalar(out=s_bp[:, 6:8], in0=pr[:, 260:262],
                                          scalar1=ck, scalar2=None, op0=ALU.is_ge))
                V(lambda: v.tensor_tensor(out=s_mid2[:], in0=s_bp[:, 4:6],
                                          in1=s_bp[:, 0:2], op=ALU.subtract))
                for b in range(B):
                    V(lambda b=b: v.scalar_tensor_tensor(
                        out=s_bp[:, b:b + 1], in0=s_mid2[:, b:b + 1],
                        scalar=s_bp[:, 6 + b:7 + b], in1=s_bp[:, b:b + 1],
                        op0=ALU.mult, op1=ALU.add))
                V(lambda: v.tensor_scalar(out=s_bp[:, 6:8], in0=pr[:, 260:262],
                                          scalar1=ck, scalar2=None, op0=ALU.is_lt))
                V(lambda: v.tensor_tensor(out=s_mid2[:], in0=s_bp[:, 4:6],
                                          in1=s_bp[:, 2:4], op=ALU.subtract))
                for b in range(B):
                    V(lambda b=b: v.scalar_tensor_tensor(
                        out=s_bp[:, 2 + b:3 + b], in0=s_mid2[:, b:b + 1],
                        scalar=s_bp[:, 6 + b:7 + b], in1=s_bp[:, 2 + b:3 + b],
                        op0=ALU.mult, op1=ALU.add))
            if not last:
                V(lambda: v.tensor_tensor(out=s_bp[:, 4:6], in0=s_bp[:, 0:2],
                                          in1=s_bp[:, 2:4], op=ALU.add))
                V(lambda: v.tensor_scalar(out=s_bp[:, 4:6], in0=s_bp[:, 4:6],
                                          scalar1=col(cm, 'half'), scalar2=None, op0=ALU.mult))
                for b in range(B):
                    V(lambda b=b: v.tensor_scalar(
                        out=s_cmp[:], in0=s_wqT[:, 64 * b:64 * b + 64],
                        scalar1=s_bp[:, 4 + b:5 + b],
                        scalar2=col(cm, 'zero'), op0=ALU.is_gt, op1=ALU.add,
                        accum_out=s_cnt2[:, b:b + 1]))
        block.vector(_)
        vv_r = sems.val('vv')

        if not last:
            def _(t, vv_r=vv_r):
                T = EngOps(t, sems, 'tv')
                t.wait_ge(sems.hdl('vv'), vv_r)
                T(lambda: t.matmul(pr[:, 260:262], s_onespp[:], s_cnt2[:, 0:2],
                                   start=True, stop=True))
            block.tensor(_)
            st['tv'] = sems.val('tv')
        st['rnd'] += 1
        return True

    def finish():
        while round():
            pass

    bb['loads'] = loads
    bb['round'] = round
    bb['finish'] = finish
    return bb



def build_router(F=None):
    nc = bass.Bass('TRN2', target_bir_lowering=False, num_devices=NCORE) if F is None else F['nc']
    with ExitStack() as ctx:
        E = ctx.enter_context
        wfull_in = _dio(nc, F, 'wfull', [NCORE, B, 1, L], F32, 'ExternalInput')
        xntm_in = _dio(nc, F, 'xntmfull', [NCORE, B, NT, P, D], BF16, 'ExternalInput')
        pos_in = _dio(nc, F, 'posids', [4, 4096], mybir.dt.int32, 'ExternalInput')
        uincl_in = _dio(nc, F, 'uincl', [P, P], F32, 'ExternalInput')
        jrow_in = _dio(nc, F, 'jrowbc', [P, P], F32, 'ExternalInput')
        selc_in = _dio(nc, F, 'selcols', [P, 65], BF16, 'ExternalInput')
        ucols_in = _dio(nc, F, 'ucols', [P, 32], F32, 'ExternalInput')
        selmat_in = _dio(nc, F, 'selmat', [64, NT], F32, 'ExternalInput')

        selc_o = _dio(nc, F, 'selc', [B, 1, P], F32, 'ExternalOutput')
        wsel_o = _dio(nc, F, 'wsel', [B, 1, P], F32, 'ExternalOutput')
        possel_o = _dio(nc, F, 'possel', [B, 1, P], F32, 'ExternalOutput')
        xsel_o = _dio(nc, F, 'xsel', [B, P, D], BF16, 'ExternalOutput')
        down_o = _dio(nc, F, 'down', [B, NT, P], F32, 'ExternalOutput')
        dcols_o = _dio(nc, F, 'dcols', [B, P, NT], F32, 'ExternalOutput')

        if F is None:
            sems, cm = open_common(nc, ctx)
            block = E(nc.Block())
            emit_common_loads(block, sems, cm)
        else:
            sems, cm, block = F['sems'], F['cm'], F['block']

        s_uincl = E(nc.sbuf_tensor('s_uincl', [P, P], F32))
        s_jrow = E(nc.sbuf_tensor('s_jrow', [P, P], F32))
        s_selcols = E(nc.sbuf_tensor('s_selcols', [P, 65], BF16))
        s_ucols = E(nc.sbuf_tensor('s_ucols', [P, 32], F32))
        s_selmat = E(nc.sbuf_tensor('s_selmat', [64, NT], F32))
        s_pos32 = E(nc.sbuf_tensor('s_pos32', [32, P], mybir.dt.int32))
        s_posf = E(nc.sbuf_tensor('s_posf', [32, P], F32))
        s_posqT = E(nc.sbuf_tensor('s_posqT', [P, 32], F32))
        s_poshi = E(nc.sbuf_tensor('s_poshi', [P, 32], BF16))
        s_poslo = E(nc.sbuf_tensor('s_poslo', [P, 32], BF16))
        s_poslof = E(nc.sbuf_tensor('s_poslof', [P, 32], F32))
        s_cmp = E(nc.sbuf_tensor('s_cmp', [P, 64], F32))
        s_m = E(nc.sbuf_tensor('s_m', [P, 64], F32))
        s_inc = E(nc.sbuf_tensor('s_inc', [P, 64], F32))
        s_cs = E(nc.sbuf_tensor('s_cs', [1, 64], F32))
        s_cs64 = E(nc.sbuf_tensor('s_cs64', [64, 1], F32))
        s_css = E(nc.sbuf_tensor('s_css', [1, 64], F32))
        s_exbc = E(nc.sbuf_tensor('s_exbc', [P, 64], F32))
        s_d = E(nc.sbuf_tensor('s_d', [P, 64], F32))
        s_dT = E(nc.sbuf_tensor('s_dT', [64, P], F32))
        s_down = E(nc.sbuf_tensor('s_down', [NT, P], F32))
        s_ST = E(nc.sbuf_tensor('s_ST', [P, 64 * P], BF16))
        s_xtile = E(nc.sbuf_tensor('s_xtile', [P, 2 * D], BF16))
        s_selcol = E(nc.sbuf_tensor('s_selcol', [P, 1], F32))
        s_selcc = E(nc.sbuf_tensor('s_selcc', [P, 1], F32))
        s_selrow = E(nc.sbuf_tensor('s_selrow', [1, P], F32))
        s_selbc = E(nc.sbuf_tensor('s_selbc', [P, P], F32))
        s_gp = E(nc.sbuf_tensor('s_gp', [P, P], BF16))
        s_out1 = E(nc.sbuf_tensor('s_out1', [P, 1], F32))
        s_xsel = E(nc.sbuf_tensor('s_xsel', [P, D], BF16))

        p_w = E(nc.psum_tensor('p_w', [P, 512], F32))
        p_x = E(nc.psum_tensor('p_x', [P, 1024], F32))
        p_c0 = E(nc.psum_tensor('p_c0', [P, 1], F32))
        p_c1 = E(nc.psum_tensor('p_c1', [P, 1], F32))
        p_c2 = E(nc.psum_tensor('p_c2', [P, 1], F32))
        p_c3 = E(nc.psum_tensor('p_c3', [P, 1], F32))
        p_t64 = p_w[0:64, 128:256]
        p_t8 = p_w[0:NT, 256:384]
        p_t8b = p_w[:, 440:448]

        def _(sync):
            for ap_s, ap_d in ((s_uincl, uincl_in), (s_jrow, jrow_in), (s_ucols, ucols_in),
                               (s_selmat, selmat_in)):
                sync.wait_ge(sems.hdl('ld'), sems.val('ld'))
                sync.dma_start(out=ap_s[:], in_=ap_d).then_inc(*sems.inc('ld', 16))
        block.sync(_)

        if F is not None and 'bis' in F:
            bb = F['bis']
        else:
            bb = emit_router_bis(nc, E, sems, cm, block, wfull_in, p_w)
            bb['loads']()
            bb['finish']()
        s_wqT = bb['s_wqT']
        s_whi = bb['s_whi']
        s_wlo = bb['s_wlo']
        s_bp = bb['s_bp']

        def _(g):
            g.wait_ge(sems.hdl('gd'), sems.val('gd'))
            g.dma_start(out=s_selcols[:], in_=selc_in).then_inc(*sems.inc('gd', 16))
        block.gpsimd(_)
        ld0 = sems.val('ld')
        gd0 = sems.val('gd')

        for eng in ('tensor', 'vector', 'scalar'):
            def _(e):
                e.wait_ge(sems.hdl('ld'), ld0)
                e.wait_ge(sems.hdl('gd'), gd0)
            getattr(block, eng)(_)

        for b in range(B):
            def _(g, b=b):
                g.wait_ge(sems.hdl('vv'), sems.val('vv'))
                g.wait_ge(sems.hdl('gd'), sems.val('gd'))
                g.dma_start(out=s_pos32[:],
                            in_=pos_in[b].rearrange('(u q) -> u q', q=P)).then_inc(*sems.inc('gd', 16))
            block.gpsimd(_)
            gd_pos = sems.val('gd')

            def _(v, b=b):
                V = EngOps(v, sems, 'vv')
                v.wait_ge(sems.hdl('tv'), sems.val('tv'))
                V(lambda: v.tensor_scalar(out=s_m[:], in0=s_wqT[:, 64 * b:64 * b + 64],
                                          scalar1=s_bp[:, 2 + b:3 + b],
                                          scalar2=None, op0=ALU.is_gt))
            block.vector(_)
            vv_m = sems.val('vv')

            def _(t, vv_m=vv_m):
                T = EngOps(t, sems, 'tv')
                t.wait_ge(sems.hdl('vv'), vv_m)
                t.matmul(p_w[:, 0:64], s_uincl[:], s_m[:], start=True, stop=True)
                T(lambda: t.matmul(p_w[0:64, 64:65], s_m[:], cm['onescol_f'][:], start=True, stop=True))
            block.tensor(_)
            tv_cs = sems.val('tv')

            def _(v, tv_cs=tv_cs):
                V = EngOps(v, sems, 'vv')
                v.wait_ge(sems.hdl('tv'), tv_cs)
                V(lambda: v.tensor_copy(out=s_inc[:], in_=p_w[:, 0:64]))
            block.vector(_)
            vv_inc = sems.val('vv')

            def _(v, vv_inc=vv_inc):
                V = EngOps(v, sems, 'vv')
                v.wait_ge(sems.hdl('vv'), vv_inc)
                V(lambda: v.tensor_copy(out=s_cs64[:], in_=p_w[0:64, 64:65]))
            block.vector(_)
            vv_fix = sems.val('vv')

            def _(t, vv_fix=vv_fix):
                T = EngOps(t, sems, 'tv')
                t.wait_ge(sems.hdl('vv'), vv_fix)
                T(lambda: t.transpose(p_w[0:1, 128:192], s_cs64[:], cm['ident_f'][0:64, 0:64]))
            block.tensor(_)
            tv_cst = sems.val('tv')

            def _(v, tv_cst=tv_cst):
                V = EngOps(v, sems, 'vv')
                v.wait_ge(sems.hdl('tv'), tv_cst)
                V(lambda: v.tensor_copy(out=s_cs[0:1, 0:64], in_=p_w[0:1, 128:192]))
                V(lambda: v.tensor_tensor_scan(out=s_css[0:1, 0:64], data0=s_cs[0:1, 0:64],
                                               data1=s_cs[0:1, 0:64], initial=0.0,
                                               op0=ALU.add, op1=ALU.bypass))
                V(lambda: v.tensor_tensor(out=s_css[0:1, 0:64], in0=s_css[0:1, 0:64],
                                          in1=s_cs[0:1, 0:64], op=ALU.subtract))
            block.vector(_)
            vv_scan = sems.val('vv')

            def _(t, vv_scan=vv_scan):
                T = EngOps(t, sems, 'tv')
                t.wait_ge(sems.hdl('vv'), vv_scan)
                T(lambda: t.matmul(p_w[:, 192:256], cm['onesrow_f'][:], s_css[0:1, 0:64],
                                   start=True, stop=True))
            block.tensor(_)
            tv_ex = sems.val('tv')

            def _(v, tv_ex=tv_ex, b=b):
                V = EngOps(v, sems, 'vv')
                v.wait_ge(sems.hdl('tv'), tv_ex)
                V(lambda: v.tensor_copy(out=s_exbc[:], in_=p_w[:, 192:256]))
                V(lambda: v.tensor_tensor(out=s_d[:], in0=s_inc[:], in1=s_exbc[:], op=ALU.add))
                V(lambda: v.tensor_tensor(out=s_d[:], in0=s_d[:], in1=s_m[:], op=ALU.mult))
            block.vector(_)
            vv_d = sems.val('vv')

            def _(t, vv_d=vv_d):
                T = EngOps(t, sems, 'tv')
                t.wait_ge(sems.hdl('vv'), vv_d)
                T(lambda: t.transpose(p_t64, s_d[:], cm['ident_f'][:]))
            block.tensor(_)
            tv_dT = sems.val('tv')

            def _(v, tv_dT=tv_dT):
                V = EngOps(v, sems, 'vv')
                v.wait_ge(sems.hdl('tv'), tv_dT)
                V(lambda: v.tensor_copy(out=s_dT[:], in_=p_t64))
            block.vector(_)
            vv_dT = sems.val('vv')

            def _(t, vv_dT=vv_dT):
                T = EngOps(t, sems, 'tv')
                t.wait_ge(sems.hdl('vv'), vv_dT)
                T(lambda: t.matmul(p_t8, s_selmat[:], s_dT[:], start=True, stop=True))
            block.tensor(_)
            tv_d8 = sems.val('tv')

            def _(v, tv_d8=tv_d8):
                V = EngOps(v, sems, 'vv')
                v.wait_ge(sems.hdl('tv'), tv_d8)
                V(lambda: v.tensor_copy(out=s_down[:], in_=p_t8))
            block.vector(_)
            vv_d8 = sems.val('vv')

            def _(sync, b=b, vv_d8=vv_d8):
                sync.wait_ge(sems.hdl('vv'), vv_d8)
                sync.wait_ge(sems.hdl('st'), sems.val('st'))
                sync.dma_start(out=down_o[b], in_=s_down[:]).then_inc(*sems.inc('st', 16))
            block.sync(_)

            def _(t, vv_d8=vv_d8):
                T = EngOps(t, sems, 'tv')
                t.wait_ge(sems.hdl('vv'), vv_d8)
                T(lambda: t.transpose(p_t8b, s_down[:], cm['ident_f'][0:NT, 0:NT]))
            block.tensor(_)
            tv_dc = sems.val('tv')

            def _(v, tv_dc=tv_dc):
                V = EngOps(v, sems, 'vv')
                v.wait_ge(sems.hdl('tv'), tv_dc)
                V(lambda: v.tensor_copy(out=s_cmp[:, 0:8], in_=p_t8b))
            block.vector(_)
            vv_dc = sems.val('vv')

            def _(sync, b=b, vv_dc=vv_dc):
                sync.wait_ge(sems.hdl('vv'), vv_dc)
                sync.wait_ge(sems.hdl('st'), sems.val('st'))
                sync.dma_start(out=dcols_o[b], in_=s_cmp[:, 0:8]).then_inc(*sems.inc('st', 16))
            block.sync(_)

            def _(v, vv_d=vv_d):
                V = EngOps(v, sems, 'vv')
                v.wait_ge(sems.hdl('vv'), vv_d)
                for ch in range(64):
                    V(lambda ch=ch: v.tensor_scalar(out=s_ST[:, ch * P:(ch + 1) * P],
                                                    in0=s_jrow[:], scalar1=s_d[:, ch:ch + 1],
                                                    scalar2=None, op0=ALU.is_equal))
            block.vector(_)
            vv_ST = sems.val('vv')

            tv_ch = [0, 0]
            for ch in range(64):
                sl = ch % 2
                def _(g, b=b, ch=ch, sl=sl, guard=tv_ch[sl]):
                    g.wait_ge(sems.hdl('tv'), guard)
                    g.wait_ge(sems.hdl('gd'), sems.val('gd'))
                    g.dma_start(out=s_xtile[:, sl * D:(sl + 1) * D],
                                in_=xntm_in[ch % NCORE, b, ch // NCORE]).then_inc(*sems.inc('gd', 16))
                block.gpsimd(_)
                gd_x = sems.val('gd')

                def _(t, ch=ch, sl=sl, gd_x=gd_x, vv_ST=vv_ST):
                    T = EngOps(t, sems, 'tv')
                    t.wait_ge(sems.hdl('gd'), gd_x)
                    t.wait_ge(sems.hdl('vv'), vv_ST)
                    if ch == 0:
                        t.wait_ge(sems.hdl('vv'), sems.val('vv'))
                    st_t = s_ST[:, ch * P:(ch + 1) * P]
                    t.ldweights(st_t)
                    t.matmul(p_x[:, 0:512], st_t, s_xtile[:, sl * D: sl * D + 512],
                             start=(ch == 0), stop=(ch == 63))
                    t.matmul(p_x[:, 512:1024], st_t, s_xtile[:, sl * D + 512: sl * D + 1024],
                             start=(ch == 0), stop=(ch == 63))
                    t.matmul(p_c0[:], st_t, s_selcols[:, 0:1], start=(ch == 0), stop=(ch == 63))
                    t.matmul(p_c1[:], st_t, s_selcols[:, 1 + ch:2 + ch], start=(ch == 0), stop=(ch == 63))
                    t.matmul(p_c2[:], st_t, s_whi[:, 64 * b + ch:64 * b + ch + 1],
                             start=(ch == 0), stop=(ch == 63))
                    T(lambda st_t=st_t, ch=ch: t.matmul(p_c3[:], st_t,
                                                        s_wlo[:, 64 * b + ch:64 * b + ch + 1],
                                                        start=(ch == 0), stop=(ch == 63)))
                block.tensor(_)
                tv_ch[sl] = sems.val('tv')
            tv_gath = sems.val('tv')

            def _(v, b=b, tv_gath=tv_gath):
                V = EngOps(v, sems, 'vv')
                v.wait_ge(sems.hdl('tv'), tv_gath)
                v.wait_ge(sems.hdl('st'), sems.val('st'))
                V(lambda: v.tensor_copy(out=s_xsel[:], in_=p_x[:, 0:1024]))
                V(lambda: v.tensor_copy(out=s_selcol[:], in_=p_c0[:]))
                V(lambda: v.scalar_tensor_tensor(out=s_selcol[:], in0=p_c1[:],
                                                 scalar=col(cm, 'c64'), in1=s_selcol[:],
                                                 op0=ALU.mult, op1=ALU.add))
                V(lambda: v.tensor_copy(out=s_out1[:], in_=p_c2[:]))
                V(lambda: v.tensor_tensor(out=s_out1[:], in0=s_out1[:], in1=p_c3[:], op=ALU.add))
                V(lambda: v.tensor_scalar(out=s_selcc[:], in0=s_selcol[:],
                                          scalar1=col(cm, 'clamp4095'), scalar2=None, op0=ALU.min))
            block.vector(_)
            vv_gath = sems.val('vv')

            def _(sync, b=b, vv_gath=vv_gath):
                sync.wait_ge(sems.hdl('vv'), vv_gath)
                sync.wait_ge(sems.hdl('st'), sems.val('st'))
                sync.dma_start(out=xsel_o[b], in_=s_xsel[:]).then_inc(*sems.inc('st', 16))
                sync.wait_ge(sems.hdl('st'), sems.val('st'))
                sync.dma_start(out=wsel_o[b].rearrange('o p -> p o'), in_=s_out1[:]).then_inc(*sems.inc('st', 16))
            block.sync(_)

            def _(t, vv_gath=vv_gath):
                T = EngOps(t, sems, 'tv')
                t.wait_ge(sems.hdl('vv'), vv_gath)
                T(lambda: t.transpose(p_w[0:1, 256:384], s_selcc[:], cm['ident_f'][:]))
            block.tensor(_)
            tv_scr = sems.val('tv')

            def _(v, tv_scr=tv_scr):
                V = EngOps(v, sems, 'vv')
                v.wait_ge(sems.hdl('tv'), tv_scr)
                V(lambda: v.tensor_copy(out=s_selrow[:], in_=p_w[0:1, 256:384]))
            block.vector(_)
            vv_srow = sems.val('vv')

            def _(sync, b=b, vv_srow=vv_srow):
                sync.wait_ge(sems.hdl('vv'), vv_srow)
                sync.wait_ge(sems.hdl('st'), sems.val('st'))
                sync.dma_start(out=selc_o[b], in_=s_selrow[:]).then_inc(*sems.inc('st', 16))
            block.sync(_)

            def _(t, vv_srow=vv_srow):
                T = EngOps(t, sems, 'tv')
                t.wait_ge(sems.hdl('vv'), vv_srow)
                T(lambda: t.matmul(p_w[:, 256:384], cm['onesrow_f'][:], s_selrow[:],
                                   start=True, stop=True))
            block.tensor(_)
            tv_sbc = sems.val('tv')

            def _(v, b=b, tv_sbc=tv_sbc):
                V = EngOps(v, sems, 'vv')
                v.wait_ge(sems.hdl('tv'), tv_sbc)
                V(lambda: v.tensor_copy(out=s_selbc[:], in_=p_w[:, 256:384]))
                v.wait_ge(sems.hdl('gd'), gd_pos)
                V(lambda: v.tensor_copy(out=s_posf[:], in_=s_pos32[:]))
            block.vector(_)
            vv_sbc = sems.val('vv')


            def _(t, b=b, vv_sbc=vv_sbc):
                T = EngOps(t, sems, 'tv')
                t.wait_ge(sems.hdl('vv'), vv_sbc)
                T(lambda: t.transpose(p_w[:, 384:416], s_posf[0:32, :], cm['ident_f'][0:32, 0:32]))
            block.tensor(_)
            tv_pT = sems.val('tv')

            def _(v, tv_pT=tv_pT):
                V = EngOps(v, sems, 'vv')
                v.wait_ge(sems.hdl('tv'), tv_pT)
                V(lambda: v.tensor_copy(out=s_posqT[:], in_=p_w[:, 384:416]))
                V(lambda: v.tensor_copy(out=s_poshi[:], in_=s_posqT[:]))
                V(lambda: v.tensor_copy(out=s_poslof[:], in_=s_poshi[:]))
                V(lambda: v.tensor_tensor(out=s_poslof[:], in0=s_posqT[:], in1=s_poslof[:],
                                          op=ALU.subtract))
                V(lambda: v.tensor_copy(out=s_poslo[:], in_=s_poslof[:]))
            block.vector(_)
            vv_pos = sems.val('vv')

            for uc in range(32):
                def _(v, uc=uc, vv_pos=vv_pos, tv_prev=sems.val('tv')):
                    V = EngOps(v, sems, 'vv')
                    v.wait_ge(sems.hdl('vv'), vv_pos)
                    v.wait_ge(sems.hdl('tv'), tv_prev)
                    V(lambda: v.tensor_scalar(out=s_gp[:], in0=s_selbc[:],
                                              scalar1=s_ucols[:, uc:uc + 1], scalar2=None,
                                              op0=ALU.is_equal))
                block.vector(_)
                vv_gp = sems.val('vv')

                def _(t, uc=uc, vv_gp=vv_gp):
                    T = EngOps(t, sems, 'tv')
                    t.wait_ge(sems.hdl('vv'), vv_gp)
                    t.ldweights(s_gp[:])
                    t.matmul(p_c0[:], s_gp[:], s_poshi[:, uc:uc + 1],
                             start=(uc == 0), stop=(uc == 31))
                    T(lambda uc=uc: t.matmul(p_c1[:], s_gp[:], s_poslo[:, uc:uc + 1],
                                             start=(uc == 0), stop=(uc == 31)))
                block.tensor(_)
            tv_pos = sems.val('tv')

            def _(v, b=b, tv_pos=tv_pos):
                V = EngOps(v, sems, 'vv')
                v.wait_ge(sems.hdl('tv'), tv_pos)
                v.wait_ge(sems.hdl('st'), sems.val('st'))
                V(lambda: v.tensor_copy(out=s_out1[:], in_=p_c0[:]))
                V(lambda: v.tensor_tensor(out=s_out1[:], in0=s_out1[:], in1=p_c1[:], op=ALU.add))
            block.vector(_)
            vv_psel = sems.val('vv')

            def _(sync, b=b, vv_psel=vv_psel):
                sync.wait_ge(sems.hdl('vv'), vv_psel)
                sync.dma_start(out=possel_o[b].rearrange('o p -> p o'),
                               in_=s_out1[:]).then_inc(*sems.inc('st', 16))
            block.sync(_)

        def _(sync):
            sync.wait_ge(sems.hdl('st'), sems.val('st'))
        block.sync(_)
    return nc


def router_inputs(prep_results, I):
    wfull = np.ascontiguousarray(np.stack([np.asarray(r['wrow'], np.float32)
                                           for r in prep_results]))
    xntmfull = np.ascontiguousarray(np.stack([np.asarray(r['xntm']) for r in prep_results]))
    uincl = np.triu(np.ones((P, P), np.float32))
    ucols = np.ascontiguousarray(
        (np.arange(P)[:, None] + 128.0 * np.arange(32)[None, :]).astype(np.float32))
    import ml_dtypes
    selcols = np.zeros((P, 65), np.float32)
    selcols[:, 0] = np.arange(P)
    for ch in range(64):
        selcols[:, 1 + ch] = 2.0 * ch
    selcols = selcols.astype(ml_dtypes.bfloat16)
    maps = []
    for c in range(NCORE):
        m = dict(common_inputs())
        m['wfull'] = wfull
        m['xntmfull'] = xntmfull
        m['posids'] = np.ascontiguousarray(I['L1_ffn_position_ids'])
        m['uincl'] = uincl
        jbase = c * P
        m['jrowbc'] = np.tile(np.arange(jbase + 1, jbase + P + 1, dtype=np.float32), (P, 1))
        m['selcols'] = selcols
        m['ucols'] = ucols
        sel = np.zeros((64, NT), np.float32)
        for i in range(NT):
            sel[NCORE * i + c, i] = 1.0
        m['selmat'] = sel
        maps.append(m)
    return maps



def build_maskc(F=None):
    nc = bass.Bass('TRN2', target_bir_lowering=False, num_devices=NCORE) if F is None else F['nc']
    with ExitStack() as ctx:
        E = ctx.enter_context
        selc_in = _dio(nc, F, 'selcfull', [NCORE, B, 1, P], F32, 'ExternalInput')
        mvsl_in = _dio(nc, F, 'maskvsl', [B, 4096, 512], F32, 'ExternalInput')
        ucols_in = _dio(nc, F, 'ucols', [P, 32], F32, 'ExternalInput')
        cvsl_o = _dio(nc, F, 'cvsl', [B, 512, 8 * P], F32, 'ExternalOutput')

        if F is None:
            sems, cm = open_common(nc, ctx)
            block = E(nc.Block())
            emit_common_loads(block, sems, cm)
        else:
            sems, cm, block = F['sems'], F['cm'], F['block']

        JW = 8 * P
        s_ucols = E(nc.sbuf_tensor('s_ucols', [P, 32], F32))
        s_selrow = E(nc.sbuf_tensor('s_selrow', [1, JW], F32))
        s_selbc = E(nc.sbuf_tensor('s_selbc', [P, JW], F32))
        s_G = E(nc.sbuf_tensor('s_G', [P, JW], BF16))
        s_M = E(nc.sbuf_tensor('s_M', [P, 2 * 512], BF16))
        s_C = E(nc.sbuf_tensor('s_C', [P, JW], F32))

        p_c0 = E(nc.psum_tensor('p_c0', [P, 1024], F32))
        p_c1 = E(nc.psum_tensor('p_c1', [P, 1024], F32))
        p_c2 = E(nc.psum_tensor('p_c2', [P, 1024], F32))
        p_c3 = E(nc.psum_tensor('p_c3', [P, 1024], F32))
        pcs = [p_c0, p_c1, p_c2, p_c3]
        p_bc = p_c0

        def _(sync):
            sync.wait_ge(sems.hdl('ld'), sems.val('ld'))
            sync.dma_start(out=s_ucols[:], in_=ucols_in).then_inc(*sems.inc('ld', 16))
        block.sync(_)
        ld0 = sems.val('ld')
        gd0 = sems.val('gd')
        for eng in ('tensor', 'vector', 'scalar'):
            def _(e):
                e.wait_ge(sems.hdl('ld'), ld0)
                e.wait_ge(sems.hdl('gd'), gd0)
            getattr(block, eng)(_)

        tv_mch = [0, 0]
        ccw = (F or {}).get('ccw')
        for b in range(B):
            def _(sync, b=b):
                sync.wait_ge(sems.hdl('vv'), sems.val('vv'))
                if ccw is not None:
                    sync.wait_ge(sems.hdl('cc'), ccw)
                sync.wait_ge(sems.hdl('ld'), sems.val('ld'))
                sync.dma_start(out=s_selrow[:].rearrange('o (r p) -> o r p', r=NCORE),
                               in_=selc_in[:, b].rearrange('r o p -> o r p')
                               ).then_inc(*sems.inc('ld', 16))
            block.sync(_)
            ld_s = sems.val('ld')

            def _(t, ld_s=ld_s):
                T = EngOps(t, sems, 'tv')
                t.wait_ge(sems.hdl('ld'), ld_s)
                t.wait_ge(sems.hdl('vv'), sems.val('vv'))
                t.matmul(p_bc[:, 0:512], cm['onesrow_f'][:], s_selrow[0:1, 0:512],
                         start=True, stop=True)
                T(lambda: t.matmul(p_bc[:, 512:1024], cm['onesrow_f'][:], s_selrow[0:1, 512:1024],
                                   start=True, stop=True))
            block.tensor(_)
            tv_bc = sems.val('tv')

            def _(v, tv_bc=tv_bc):
                V = EngOps(v, sems, 'vv')
                v.wait_ge(sems.hdl('tv'), tv_bc)
                V(lambda: v.tensor_copy(out=s_selbc[:], in_=p_bc[:, 0:1024]))
            block.vector(_)
            vv_bc = sems.val('vv')

            for uc in range(32):
                sl = uc % 2
                def _(v, uc=uc, vv_bc=vv_bc, tv_prev=sems.val('tv')):
                    V = EngOps(v, sems, 'vv')
                    v.wait_ge(sems.hdl('vv'), vv_bc)
                    v.wait_ge(sems.hdl('tv'), tv_prev)
                    V(lambda: v.tensor_scalar(out=s_G[:], in0=s_selbc[:],
                                              scalar1=s_ucols[:, uc:uc + 1], scalar2=None,
                                              op0=ALU.is_equal))
                block.vector(_)
                vv_G = sems.val('vv')

                def _(g, b=b, uc=uc, sl=sl, guard=tv_mch[sl]):
                    g.wait_ge(sems.hdl('tv'), guard)
                    g.wait_ge(sems.hdl('gd'), sems.val('gd'))
                    g.dma_start(out=s_M[:, sl * 512:(sl + 1) * 512],
                                in_=mvsl_in[b, uc * P:(uc + 1) * P, :]).then_inc(*sems.inc('gd', 16))
                block.gpsimd(_)
                gd_M = sems.val('gd')

                def _(t, uc=uc, sl=sl, gd_M=gd_M, vv_G=vv_G):
                    T = EngOps(t, sems, 'tv')
                    t.wait_ge(sems.hdl('gd'), gd_M)
                    t.wait_ge(sems.hdl('vv'), vv_G)
                    if uc == 0:
                        t.wait_ge(sems.hdl('vv'), sems.val('vv'))
                    for vt in range(4):
                        mt = s_M[:, sl * 512 + vt * P: sl * 512 + (vt + 1) * P]
                        t.ldweights(mt)
                        t.matmul(pcs[vt][:, 0:512], mt, s_G[:, 0:512],
                                 start=(uc == 0), stop=(uc == 31))
                        op = lambda vt=vt, mt=mt, uc=uc: t.matmul(
                            pcs[vt][:, 512:1024], mt, s_G[:, 512:1024],
                            start=(uc == 0), stop=(uc == 31))
                        if vt == 3:
                            T(op)
                        else:
                            op()
                block.tensor(_)
                tv_mch[sl] = sems.val('tv')
            tv_C = sems.val('tv')

            for vt in range(4):
                def _(v, vt=vt, tv_C=tv_C):
                    V = EngOps(v, sems, 'vv')
                    v.wait_ge(sems.hdl('tv'), tv_C)
                    v.wait_ge(sems.hdl('st'), sems.val('st'))
                    V(lambda: v.tensor_copy(out=s_C[:], in_=pcs[vt][:, 0:1024]))
                block.vector(_)
                vv_C = sems.val('vv')

                def _(sync, b=b, vt=vt, vv_C=vv_C):
                    sync.wait_ge(sems.hdl('vv'), vv_C)
                    sync.dma_start(out=cvsl_o[b, vt * P:(vt + 1) * P, :],
                                   in_=s_C[:]).then_inc(*sems.inc('st', 16))
                block.sync(_)

        def _(sync):
            sync.wait_ge(sems.hdl('st'), sems.val('st'))
        block.sync(_)
    return nc


def maskc_inputs(router_results, I):
    selcfull = np.ascontiguousarray(
        np.concatenate([np.asarray(r['selc'], np.float32) for r in router_results], axis=2))
    ucols = np.ascontiguousarray(
        (np.arange(P)[:, None] + 128.0 * np.arange(32)[None, :]).astype(np.float32))
    mask = I['L1_ffn_attention_mask']
    maps = []
    for c in range(NCORE):
        m = dict(common_inputs())
        m['selcfull'] = selcfull
        m['maskvsl'] = np.ascontiguousarray(mask[0:B, 0, :, c * 512:(c + 1) * 512])
        m['ucols'] = ucols
        maps.append(m)
    return maps



def build_msel(F=None):
    nc = bass.Bass('TRN2', target_bir_lowering=False, num_devices=NCORE) if F is None else F['nc']
    with ExitStack() as ctx:
        E = ctx.enter_context
        JW = 8 * P
        cown_in = _dio(nc, F, 'cown', [B, 512, JW], F32, 'ExternalInput')
        selc_in = _dio(nc, F, 'selcfull', [NCORE, B, 1, P], F32, 'ExternalInput')
        vcols_in = _dio(nc, F, 'vcols', [P, 4], F32, 'ExternalInput')
        padm_in = _dio(nc, F, 'padmask', [P, JW], F32, 'ExternalInput')
        msel_o = _dio(nc, F, 'mselp', [B, JW, JW], F32, 'ExternalOutput')

        if F is None:
            sems, cm = open_common(nc, ctx)
            block = E(nc.Block())
            emit_common_loads(block, sems, cm)
        else:
            sems, cm, block = F['sems'], F['cm'], F['block']

        s_vcols = E(nc.sbuf_tensor('s_vcols', [P, 4], F32))
        s_selrow = E(nc.sbuf_tensor('s_selrow', [1, JW], F32))
        s_selbc = E(nc.sbuf_tensor('s_selbc', [P, JW], F32))
        s_G4 = E(nc.sbuf_tensor('s_G4', [P, 4 * JW], BF16))
        s_C4 = E(nc.sbuf_tensor('s_C4', [P, 4 * JW], BF16))
        s_pad = E(nc.sbuf_tensor('s_pad', [P, JW], F32))
        s_out = E(nc.sbuf_tensor('s_out', [P, JW], F32))

        p_bc = E(nc.psum_tensor('p_bc', [P, 1024], F32))
        p_m = E(nc.psum_tensor('p_m', [P, 1024], F32))

        def _(sync):
            sync.wait_ge(sems.hdl('ld'), sems.val('ld'))
            sync.dma_start(out=s_vcols[:], in_=vcols_in).then_inc(*sems.inc('ld', 16))
            sync.wait_ge(sems.hdl('ld'), sems.val('ld'))
            sync.dma_start(out=s_pad[:], in_=padm_in).then_inc(*sems.inc('ld', 16))
        block.sync(_)
        ld0 = sems.val('ld')
        for eng in ('tensor', 'vector', 'scalar'):
            def _(e):
                e.wait_ge(sems.hdl('ld'), ld0)
                e.wait_ge(sems.hdl('gd'), sems.val('gd'))
            getattr(block, eng)(_)

        for b in range(B):
            def _(sync, b=b):
                sync.wait_ge(sems.hdl('vv'), sems.val('vv'))
                sync.wait_ge(sems.hdl('tv'), sems.val('tv'))
                sync.wait_ge(sems.hdl('ld'), sems.val('ld'))
                sync.dma_start(out=s_selrow[:].rearrange('o (r p) -> o r p', r=NCORE),
                               in_=selc_in[:, b].rearrange('r o p -> o r p')
                               ).then_inc(*sems.inc('ld', 16))
            block.sync(_)
            ld_s = sems.val('ld')

            def _(g, b=b):
                g.wait_ge(sems.hdl('tv'), sems.val('tv'))
                g.wait_ge(sems.hdl('gd'), sems.val('gd'))
                g.dma_start(out=s_C4[:].rearrange('p (c j) -> p c j', c=4),
                            in_=cown_in[b].rearrange('(c p) j -> p c j', p=P)
                            ).then_inc(*sems.inc('gd', 16))
            block.gpsimd(_)
            gd_C = sems.val('gd')

            def _(t, ld_s=ld_s):
                T = EngOps(t, sems, 'tv')
                t.wait_ge(sems.hdl('ld'), ld_s)
                t.wait_ge(sems.hdl('vv'), sems.val('vv'))
                t.matmul(p_bc[:, 0:512], cm['onesrow_f'][:], s_selrow[0:1, 0:512],
                         start=True, stop=True)
                T(lambda: t.matmul(p_bc[:, 512:1024], cm['onesrow_f'][:], s_selrow[0:1, 512:1024],
                                   start=True, stop=True))
            block.tensor(_)
            tv_bc = sems.val('tv')

            def _(v, tv_bc=tv_bc, gd_C=gd_C):
                V = EngOps(v, sems, 'vv')
                v.wait_ge(sems.hdl('tv'), tv_bc)
                v.wait_ge(sems.hdl('gd'), gd_C)
                V(lambda: v.tensor_copy(out=s_selbc[:], in_=p_bc[:, 0:1024]))
                for vc in range(4):
                    V(lambda vc=vc: v.tensor_scalar(out=s_G4[:, vc * JW:(vc + 1) * JW],
                                                    in0=s_selbc[:], scalar1=s_vcols[:, vc:vc + 1],
                                                    scalar2=None, op0=ALU.is_equal))
            block.vector(_)
            vv_G = sems.val('vv')

            for jt in range(NT):
                def _(t, jt=jt, vv_G=vv_G):
                    T = EngOps(t, sems, 'tv')
                    t.wait_ge(sems.hdl('vv'), vv_G)
                    t.wait_ge(sems.hdl('vv'), sems.val('vv'))
                    for vc in range(4):
                        ct = s_C4[:, vc * JW + jt * P: vc * JW + (jt + 1) * P]
                        t.ldweights(ct)
                        t.matmul(p_m[:, 0:512], ct, s_G4[:, vc * JW: vc * JW + 512],
                                 start=(vc == 0), stop=(vc == 3))
                        op = lambda vc=vc, ct=ct: t.matmul(
                            p_m[:, 512:1024], ct, s_G4[:, vc * JW + 512: vc * JW + 1024],
                            start=(vc == 0), stop=(vc == 3))
                        if vc == 3:
                            T(op)
                        else:
                            op()
                block.tensor(_)
                tv_m = sems.val('tv')

                def _(v, tv_m=tv_m):
                    V = EngOps(v, sems, 'vv')
                    v.wait_ge(sems.hdl('tv'), tv_m)
                    v.wait_ge(sems.hdl('st'), sems.val('st'))
                    V(lambda: v.tensor_tensor(out=s_out[:], in0=p_m[:, 0:1024], in1=s_pad[:],
                                              op=ALU.add))
                block.vector(_)
                vv_m = sems.val('vv')

                def _(sync, b=b, jt=jt, vv_m=vv_m):
                    sync.wait_ge(sems.hdl('vv'), vv_m)
                    sync.dma_start(out=msel_o[b, jt * P:(jt + 1) * P, :],
                                   in_=s_out[:]).then_inc(*sems.inc('st', 16))
                block.sync(_)

        def _(sync):
            sync.wait_ge(sems.hdl('st'), sems.val('st'))
        block.sync(_)
    return nc


def msel_inputs(maskc_results, router_results, I):
    JW = 8 * P
    cfull = np.ascontiguousarray(
        np.concatenate([np.asarray(r['cvsl'], np.float32) for r in maskc_results], axis=1))
    selcfull = np.ascontiguousarray(
        np.concatenate([np.asarray(r['selc'], np.float32) for r in router_results], axis=2))
    ucols = np.ascontiguousarray(
        (np.arange(P)[:, None] + 128.0 * np.arange(32)[None, :]).astype(np.float32))
    padmask = np.zeros((P, JW), np.float32)
    padmask[:, JW - 1] = -1e9
    maps = []
    for c in range(NCORE):
        m = dict(common_inputs())
        m['cfull'] = np.ascontiguousarray(cfull[:, :, c * P:(c + 1) * P])
        m['selcfull'] = selcfull
        m['ucols'] = ucols
        m['jslice'] = np.zeros((1, 4), np.float32)
        m['padmask'] = padmask
        maps.append(m)
    return maps



def build_subattn(F=None):
    nc = bass.Bass('TRN2', target_bir_lowering=False, num_devices=NCORE) if F is None else F['nc']
    with ExitStack() as ctx:
        E = ctx.enter_context
        JW = 8 * P
        xs_in = _dio(nc, F, 'xself', [NCORE, B, P, D], BF16, 'ExternalInput')
        msel_in = _dio(nc, F, 'mselfull', [NCORE, B, P, JW], F32, 'ExternalInput')
        pos_in = _dio(nc, F, 'posfull', [NCORE, B, 1, P], F32, 'ExternalInput')
        wq_in = _dio(nc, F, 'wqsl', [D, P], F32, 'ExternalInput')
        wk_in = _dio(nc, F, 'wksl', [D, P], F32, 'ExternalInput')
        wv_in = _dio(nc, F, 'wvsl', [D, P], F32, 'ExternalInput')
        wo_in = _dio(nc, F, 'wosl', [P, D], F32, 'ExternalInput')
        ln1_in = _dio(nc, F, 'ln1', [1, D], F32, 'ExternalInput')
        inv_in = _dio(nc, F, 'invbc', [P, 32], F32, 'ExternalInput')
        part_o = _dio(nc, F, 'partial', [NT, B, P, D], F32, 'ExternalOutput')

        if F is None:
            sems, cm = open_common(nc, ctx)
            block = E(nc.Block())
            emit_common_loads(block, sems, cm)
        else:
            sems, cm, block = F['sems'], F['cm'], F['block']

        s_xs = E(nc.sbuf_tensor('s_xs', [P, NT * D], BF16))
        s_sq = E(nc.sbuf_tensor('s_sq', [P, D], F32))
        s_hn = [E(nc.sbuf_tensor('s_hn%d' % p, [P, D], BF16)) for p in range(2)]
        s_hnfm = E(nc.sbuf_tensor('s_hnfm', [P, DC * JW], BF16))
        s_ln1b = E(nc.sbuf_tensor('s_ln1b', [P, D], F32))
        s_ln1r = E(nc.sbuf_tensor('s_ln1r', [1, D], F32))
        s_inv = E(nc.sbuf_tensor('s_inv', [P, 32], F32))
        s_wq = E(nc.sbuf_tensor('s_wq', [P, DC * P], BF16))
        s_wk = E(nc.sbuf_tensor('s_wk', [P, DC * P], BF16))
        s_wv = E(nc.sbuf_tensor('s_wv', [P, DC * P], BF16))
        s_wo = E(nc.sbuf_tensor('s_wo', [P, D], BF16))
        s_q = E(nc.sbuf_tensor('s_q', [P, NT * P], F32))
        s_k = E(nc.sbuf_tensor('s_k', [P, NT * P], F32))
        s_v = E(nc.sbuf_tensor('s_v', [P, NT * P], BF16))
        s_qr = [E(nc.sbuf_tensor('s_qr%d' % p, [P, P], BF16)) for p in range(2)]
        s_kr = [E(nc.sbuf_tensor('s_kr%d' % p, [P, P], BF16)) for p in range(2)]
        s_qf = E(nc.sbuf_tensor('s_qf', [64, 2 * NT * P], BF16))
        s_kf = E(nc.sbuf_tensor('s_kf', [64, 2 * NT * P], BF16))
        s_pos8 = E(nc.sbuf_tensor('s_pos8', [NT, P], F32))
        s_posc = E(nc.sbuf_tensor('s_posc', [P, NT], F32))
        s_ang = E(nc.sbuf_tensor('s_ang', [P, 9 * 32], F32))
        s_S = E(nc.sbuf_tensor('s_S', [P, JW], F32))
        s_msel = E(nc.sbuf_tensor('s_msel', [P, JW], F32))
        s_l = E(nc.sbuf_tensor('s_l', [P, 4], F32))
        s_P = E(nc.sbuf_tensor('s_P', [P, JW], BF16))
        s_PT = E(nc.sbuf_tensor('s_PT', [P, 2 * NT * JW], BF16))
        s_of = E(nc.sbuf_tensor('s_of', [P, JW], F32))
        s_ofb = E(nc.sbuf_tensor('s_ofb', [P, JW], BF16))
        s_out = [E(nc.sbuf_tensor('s_outp%d' % p, [P, D], F32)) for p in range(2)]

        p_a = [E(nc.psum_tensor('p_a%d' % p, [P, 1024], F32)) for p in range(2)]
        p_PT = E(nc.psum_tensor('p_PT', [P, 1024], BF16))
        p_O = E(nc.psum_tensor('p_O', [64, 1024], F32))
        p_tr8 = E(nc.psum_tensor('p_tr8', [P, 1024], BF16))
        p_p8 = p_a[0][:, 0:NT]

        def _(g):
            for wsb, wdr in ((s_wq, wq_in), (s_wk, wk_in), (s_wv, wv_in)):
                g.wait_ge(sems.hdl('gd'), sems.val('gd'))
                g.dma_start(out=wsb[:].rearrange('p (c q) -> p c q', c=DC),
                            in_=wdr.rearrange('(c p) q -> p c q', p=P)).then_inc(*sems.inc('gd', 16))
            g.wait_ge(sems.hdl('gd'), sems.val('gd'))
            g.dma_start(out=s_wo[:], in_=wo_in).then_inc(*sems.inc('gd', 16))
        block.gpsimd(_)

        def _(sync):
            sync.wait_ge(sems.hdl('ld'), sems.val('ld'))
            sync.dma_start(out=s_ln1r[:], in_=ln1_in).then_inc(*sems.inc('ld', 16))
            sync.wait_ge(sems.hdl('ld'), sems.val('ld'))
            sync.dma_start(out=s_inv[:], in_=inv_in).then_inc(*sems.inc('ld', 16))
        block.sync(_)
        ld0 = sems.val('ld')
        gd0 = sems.val('gd')
        for eng in ('tensor', 'vector', 'scalar'):
            def _(e):
                e.wait_ge(sems.hdl('ld'), ld0)
                e.wait_ge(sems.hdl('gd'), gd0)
            getattr(block, eng)(_)

        def _(t):
            T = EngOps(t, sems, 'tv')
            t.matmul(p_a[0][:, 0:512], cm['onesrow_f'][:], s_ln1r[0:1, 0:512], start=True, stop=True)
            T(lambda: t.matmul(p_a[0][:, 512:1024], cm['onesrow_f'][:], s_ln1r[0:1, 512:1024],
                               start=True, stop=True))
        block.tensor(_)
        tv_ln = sems.val('tv')

        def _(v, tv_ln=tv_ln):
            V = EngOps(v, sems, 'vv')
            v.wait_ge(sems.hdl('tv'), tv_ln)
            V(lambda: v.tensor_copy(out=s_ln1b[:], in_=p_a[0][:, 0:1024]))
        block.vector(_)
        mkS = {'tr_vv': sems.val('vv'), 'hnfree': [sems.val('tv')] * 2,
               'qkv_vv': [sems.val('vv')] * 2, 'ropefree': [sems.val('tv')] * 2,
               'out_st': [sems.val('st')] * 2}

        for b in range(B):
            def _(sync, b=b):
                sync.wait_ge(sems.hdl('tv'), sems.val('tv'))
                sync.wait_ge(sems.hdl('sv'), sems.val('sv'))
                sync.wait_ge(sems.hdl('ld'), sems.val('ld'))
                sync.dma_start(out=s_xs[:].rearrange('p (t d) -> p t d', t=NT),
                               in_=xs_in[:, b].rearrange('t p d -> p t d')
                               ).then_inc(*sems.inc('ld', 16))
                sync.wait_ge(sems.hdl('ld'), sems.val('ld'))
                sync.dma_start(out=s_pos8[:], in_=pos_in[:, b, 0]
                               ).then_inc(*sems.inc('ld', 16))
            block.sync(_)
            ld_b = sems.val('ld')

            def _(t, ld_b=ld_b):
                T = EngOps(t, sems, 'tv')
                t.wait_ge(sems.hdl('ld'), ld_b)
                t.wait_ge(sems.hdl('vv'), sems.val('vv'))
                T(lambda: t.transpose(p_p8[:], s_pos8[:], cm['ident_f'][0:NT, 0:NT]))
            block.tensor(_)
            tv_p8 = sems.val('tv')

            def _(v, tv_p8=tv_p8):
                V = EngOps(v, sems, 'vv')
                v.wait_ge(sems.hdl('tv'), tv_p8)
                V(lambda: v.tensor_copy(out=s_posc[:], in_=p_p8[:]))
            block.vector(_)

            for jt in range(NT):
                def _(s, jt=jt, ld_b=ld_b):
                    S = EngOps(s, sems, 'sv')
                    s.wait_ge(sems.hdl('ld'), ld_b)
                    s.wait_ge(sems.hdl('vv'), sems.val('vv'))
                    S(lambda: s.activation(out=s_sq[:], in_=s_xs[:, jt * D:(jt + 1) * D],
                                           func=AF.Square, bias=col(cm, 'zero'), scale=1.0,
                                           accum_out=s_l[:, 0:1]))
                    S(lambda: s.activation(out=s_l[:, 1:2], in_=s_l[:, 0:1], func=AF.Sqrt,
                                           bias=col(cm, 'eps_rms'), scale=1.0 / D))
                block.scalar(_)
                sv_r = sems.val('sv')

                pr = jt % 2
                def _(v, jt=jt, pr=pr, sv_r=sv_r, hf=mkS['hnfree'][pr]):
                    V = EngOps(v, sems, 'vv')
                    v.wait_ge(sems.hdl('sv'), sv_r)
                    v.wait_ge(sems.hdl('tv'), hf)
                    V(lambda: v.reciprocal(out=s_l[:, 2:3], in_=s_l[:, 1:2]))
                    V(lambda: v.tensor_scalar(out=s_hn[pr][:], in0=s_xs[:, jt * D:(jt + 1) * D],
                                              scalar1=s_l[:, 2:3], scalar2=None, op0=ALU.mult))
                    V(lambda: v.tensor_tensor(out=s_hn[pr][:], in0=s_hn[pr][:], in1=s_ln1b[:],
                                              op=ALU.mult))
                block.vector(_)
                vv_hn = sems.val('vv')

                def _(t, jt=jt, pr=pr, vv_hn=vv_hn, gtr=mkS['tr_vv']):
                    T = EngOps(t, sems, 'tv')
                    t.wait_ge(sems.hdl('vv'), vv_hn)
                    t.wait_ge(sems.hdl('vv'), gtr)
                    for dc in range(DC):
                        op = lambda dc=dc: t.transpose(p_tr8[:, dc * P:(dc + 1) * P],
                                                       s_hn[pr][:, dc * P:(dc + 1) * P],
                                                       cm['ident_b'][:])
                        if dc == DC - 1:
                            T(op)
                        else:
                            op()
                block.tensor(_)
                mkS['hnfree'][pr] = sems.val('tv')
                tv_tr8 = sems.val('tv')

                def _(v, jt=jt, tv_tr8=tv_tr8):
                    V = EngOps(v, sems, 'vv')
                    v.wait_ge(sems.hdl('tv'), tv_tr8)
                    V(lambda: v.tensor_copy(
                        out=s_hnfm[:].rearrange('p (c t) -> p c t', c=DC)[:, :, jt * P:(jt + 1) * P],
                        in_=p_tr8[:].rearrange('p (c q) -> p c q', c=DC)))
                block.vector(_)
                mkS['tr_vv'] = sems.val('vv')
            vv_fm = sems.val('vv')

            for jt in range(NT):
                pr = jt % 2
                def _(t, jt=jt, pr=pr, vv_fm=vv_fm, gq=mkS['qkv_vv'][pr]):
                    T = EngOps(t, sems, 'tv')
                    t.wait_ge(sems.hdl('vv'), vv_fm)
                    t.wait_ge(sems.hdl('vv'), gq)
                    for wsb, off in ((s_wq, 0), (s_wk, 256), (s_wv, 512)):
                        for dc in range(DC):
                            hn_t = s_hnfm[:, dc * JW + jt * P: dc * JW + (jt + 1) * P]
                            t.ldweights(hn_t)
                            op = lambda dc=dc, wsb=wsb, off=off, hn_t=hn_t: t.matmul(
                                p_a[pr][:, off:off + P], hn_t, wsb[:, dc * P:(dc + 1) * P],
                                start=(dc == 0), stop=(dc == DC - 1))
                            if dc == DC - 1:
                                T(op)
                            else:
                                op()
                block.tensor(_)
                tv_qkv = sems.val('tv')

                def _(v, jt=jt, pr=pr, tv_qkv=tv_qkv):
                    V = EngOps(v, sems, 'vv')
                    v.wait_ge(sems.hdl('tv'), tv_qkv)
                    V(lambda: v.tensor_copy(out=s_q[:, jt * P:(jt + 1) * P], in_=p_a[pr][:, 0:P]))
                    V(lambda: v.tensor_copy(out=s_k[:, jt * P:(jt + 1) * P], in_=p_a[pr][:, 256:256 + P]))
                    V(lambda: v.tensor_copy(out=s_v[:, jt * P:(jt + 1) * P], in_=p_a[pr][:, 512:512 + P]))
                    V(lambda: v.tensor_scalar(out=s_ang[:, 0:32], in0=s_inv[:],
                                              scalar1=s_posc[:, jt:jt + 1], scalar2=None, op0=ALU.mult))
                    V(lambda: v.tensor_scalar(out=s_ang[:, 32:64], in0=s_ang[:, 0:32],
                                              scalar1=col(cm, 'inv_2pi'), scalar2=col(cm, 'magic'),
                                              op0=ALU.mult, op1=ALU.add))
                    V(lambda: v.tensor_scalar(out=s_ang[:, 32:64], in0=s_ang[:, 32:64],
                                              scalar1=col(cm, 'magic'), scalar2=None, op0=ALU.subtract))
                    V(lambda: v.scalar_tensor_tensor(out=s_ang[:, 64:96], in0=s_ang[:, 32:64],
                                                     scalar=col(cm, 'neg_chi'), in1=s_ang[:, 0:32],
                                                     op0=ALU.mult, op1=ALU.add))
                    V(lambda: v.scalar_tensor_tensor(out=s_ang[:, 64:96], in0=s_ang[:, 32:64],
                                                     scalar=col(cm, 'neg_cmid'), in1=s_ang[:, 64:96],
                                                     op0=ALU.mult, op1=ALU.add))
                    V(lambda: v.tensor_scalar(out=s_ang[:, 96:128], in0=s_ang[:, 0:32],
                                              scalar1=col(cm, 'half_pi'), scalar2=None, op0=ALU.add))
                    V(lambda: v.tensor_scalar(out=s_ang[:, 128:160], in0=s_ang[:, 96:128],
                                              scalar1=col(cm, 'inv_2pi'), scalar2=col(cm, 'magic'),
                                              op0=ALU.mult, op1=ALU.add))
                    V(lambda: v.tensor_scalar(out=s_ang[:, 128:160], in0=s_ang[:, 128:160],
                                              scalar1=col(cm, 'magic'), scalar2=None, op0=ALU.subtract))
                    V(lambda: v.scalar_tensor_tensor(out=s_ang[:, 160:192], in0=s_ang[:, 128:160],
                                                     scalar=col(cm, 'neg_chi'), in1=s_ang[:, 96:128],
                                                     op0=ALU.mult, op1=ALU.add))
                    V(lambda: v.scalar_tensor_tensor(out=s_ang[:, 160:192], in0=s_ang[:, 128:160],
                                                     scalar=col(cm, 'neg_cmid'), in1=s_ang[:, 160:192],
                                                     op0=ALU.mult, op1=ALU.add))
                block.vector(_)
                mkS['qkv_vv'][pr] = sems.val('vv')
                vv_ang = sems.val('vv')

                def _(s, vv_ang=vv_ang):
                    S = EngOps(s, sems, 'sv')
                    s.wait_ge(sems.hdl('vv'), vv_ang)
                    S(lambda: s.activation(out=s_ang[:, 192:224], in_=s_ang[:, 64:96],
                                           func=AF.Sin, bias=col(cm, 'zero'), scale=1.0))
                    S(lambda: s.activation(out=s_ang[:, 224:256], in_=s_ang[:, 160:192],
                                           func=AF.Sin, bias=col(cm, 'zero'), scale=1.0))
                block.scalar(_)
                sv_sc = sems.val('sv')

                def _(v, jt=jt, pr=pr, sv_sc=sv_sc, rf=mkS['ropefree'][pr]):
                    V = EngOps(v, sems, 'vv')
                    v.wait_ge(sems.hdl('sv'), sv_sc)
                    v.wait_ge(sems.hdl('tv'), rf)
                    sin = s_ang[:, 192:224]
                    cos = s_ang[:, 224:256]
                    tmp = s_ang[:, 256:288]
                    for (srct, dst) in ((s_q, s_qr[pr]), (s_k, s_kr[pr])):
                        for ho in (0, 64):
                            a1 = srct[:, jt * P + ho: jt * P + ho + 32]
                            a2 = srct[:, jt * P + ho + 32: jt * P + ho + 64]
                            o1 = dst[:, ho: ho + 32]
                            o2 = dst[:, ho + 32: ho + 64]
                            V(lambda a2=a2, o2=o2: v.tensor_tensor(out=o2, in0=a2, in1=cos, op=ALU.mult))
                            V(lambda a1=a1: v.tensor_tensor(out=tmp, in0=a1, in1=sin, op=ALU.mult))
                            V(lambda o2=o2: v.tensor_tensor(out=o2, in0=o2, in1=tmp, op=ALU.add))
                            V(lambda a1=a1, o1=o1: v.tensor_tensor(out=o1, in0=a1, in1=cos, op=ALU.mult))
                            V(lambda a2=a2: v.tensor_tensor(out=tmp, in0=a2, in1=sin, op=ALU.mult))
                            V(lambda o1=o1: v.tensor_tensor(out=o1, in0=o1, in1=tmp, op=ALU.subtract))
                block.vector(_)
                vv_rope = sems.val('vv')

                def _(t, jt=jt, pr=pr, vv_rope=vv_rope, gtr=mkS['tr_vv']):
                    T = EngOps(t, sems, 'tv')
                    t.wait_ge(sems.hdl('vv'), vv_rope)
                    t.wait_ge(sems.hdl('vv'), gtr)
                    sls = [(s_qr[pr], 0), (s_qr[pr], 1), (s_kr[pr], 2), (s_kr[pr], 3)]
                    for srct, sx in sls:
                        h = sx % 2
                        op = lambda srct=srct, sx=sx, h=h: t.transpose(
                            p_tr8[0:64, sx * P:(sx + 1) * P],
                            srct[:, h * 64:(h + 1) * 64], cm['ident_b'][:])
                        if sx == 3:
                            T(op)
                        else:
                            op()
                block.tensor(_)
                mkS['ropefree'][pr] = sems.val('tv')
                tv_rt = sems.val('tv')

                def _(v, jt=jt, tv_rt=tv_rt):
                    V = EngOps(v, sems, 'vv')
                    v.wait_ge(sems.hdl('tv'), tv_rt)
                    V(lambda: v.tensor_copy(
                        out=s_qf[0:64, :].rearrange('p (h t) -> p h t', h=2)[:, :, jt * P:(jt + 1) * P],
                        in_=p_tr8[0:64, 0:256].rearrange('p (h q) -> p h q', h=2)))
                    V(lambda: v.tensor_copy(
                        out=s_kf[0:64, :].rearrange('p (h t) -> p h t', h=2)[:, :, jt * P:(jt + 1) * P],
                        in_=p_tr8[0:64, 256:512].rearrange('p (h q) -> p h q', h=2)))
                block.vector(_)
                mkS['tr_vv'] = sems.val('vv')
            vv_qkf = sems.val('vv')

            ccwS = (F or {}).get('ccw')
            for jt in range(NT):
                def _(sync, b=b, jt=jt):
                    sync.wait_ge(sems.hdl('vv'), sems.val('vv'))
                    if ccwS is not None:
                        sync.wait_ge(sems.hdl('cc'), ccwS)
                    sync.wait_ge(sems.hdl('ld'), sems.val('ld'))
                    sync.dma_start(out=s_msel[:], in_=msel_in[jt, b]
                                   ).then_inc(*sems.inc('ld', 16))
                block.sync(_)
                ld_m = sems.val('ld')

                for h in range(2):
                    spr = (jt * 2 + h) % 2
                    def _(t, jt=jt, h=h, spr=spr, vv_qkf=vv_qkf, gsc=mkS['qkv_vv'][spr]):
                        T = EngOps(t, sems, 'tv')
                        t.wait_ge(sems.hdl('vv'), vv_qkf)
                        t.wait_ge(sems.hdl('sv'), sems.val('sv'))
                        t.wait_ge(sems.hdl('vv'), gsc)
                        qt = s_qf[0:64, h * NT * P + jt * P: h * NT * P + (jt + 1) * P]
                        t.ldweights(qt)
                        t.matmul(p_a[spr][:, 0:512], qt, s_kf[0:64, h * NT * P: h * NT * P + 512],
                                 start=True, stop=True)
                        T(lambda qt=qt, h=h, spr=spr: t.matmul(
                            p_a[spr][:, 512:1024], qt,
                            s_kf[0:64, h * NT * P + 512: h * NT * P + 1024],
                            start=True, stop=True))
                    block.tensor(_)
                    tv_sc = sems.val('tv')

                    def _(s, spr=spr, tv_sc=tv_sc):
                        S = EngOps(s, sems, 'sv')
                        s.wait_ge(sems.hdl('tv'), tv_sc)
                        s.wait_ge(sems.hdl('vv'), sems.val('vv'))
                        S(lambda: s.activation(out=s_S[:], in_=p_a[spr][:, 0:1024], func=AF.Copy,
                                               bias=0.0, scale=0.125))
                    block.scalar(_)
                    sv_ev = sems.val('sv')

                    def _(v, ld_m=ld_m, sv_ev=sv_ev):
                        V = EngOps(v, sems, 'vv')
                        v.wait_ge(sems.hdl('sv'), sv_ev)
                        v.wait_ge(sems.hdl('ld'), ld_m)
                        V(lambda: v.tensor_tensor(out=s_S[:], in0=s_S[:], in1=s_msel[:], op=ALU.add))
                        V(lambda: v.reduce_max(out=s_l[:, 0:1], in_=s_S[:], axis=AX.X, negate=True))
                    block.vector(_)
                    vv_mx = sems.val('vv')

                    def _(s, vv_mx=vv_mx):
                        S = EngOps(s, sems, 'sv')
                        s.wait_ge(sems.hdl('vv'), vv_mx)
                        S(lambda: s.activation(out=s_P[:], in_=s_S[:], func=AF.Exp,
                                               bias=s_l[:, 0:1], scale=1.0, accum_out=s_l[:, 1:2]))
                    block.scalar(_)
                    sv_P = sems.val('sv')

                    def _(v, sv_P=sv_P, tv_prev=sems.val('tv')):
                        V = EngOps(v, sems, 'vv')
                        v.wait_ge(sems.hdl('sv'), sv_P)
                        v.wait_ge(sems.hdl('tv'), tv_prev)
                        V(lambda: v.reciprocal(out=s_l[:, 2:3], in_=s_l[:, 1:2]))
                        V(lambda: v.tensor_scalar(out=s_P[:], in0=s_P[:], scalar1=s_l[:, 2:3],
                                                  scalar2=None, op0=ALU.mult))
                    block.vector(_)
                    mkS['qkv_vv'][spr] = sems.val('vv')
                    vv_Pn = sems.val('vv')

                    def _(t, vv_Pn=vv_Pn):
                        T = EngOps(t, sems, 'tv')
                        t.wait_ge(sems.hdl('vv'), vv_Pn)
                        for kc in range(NT):
                            op = lambda kc=kc: t.transpose(p_PT[:, kc * P:(kc + 1) * P],
                                                           s_P[:, kc * P:(kc + 1) * P],
                                                           cm['ident_b'][:])
                            if kc == NT - 1:
                                T(op)
                            else:
                                op()
                    block.tensor(_)
                    tv_PT = sems.val('tv')

                    def _(v, jt=jt, h=h, tv_PT=tv_PT):
                        V = EngOps(v, sems, 'vv')
                        v.wait_ge(sems.hdl('tv'), tv_PT)
                        reg = s_PT[:, h * NT * JW: (h + 1) * NT * JW]
                        V(lambda: v.tensor_copy(
                            out=reg.rearrange('p (k q) -> p k q', k=NT)[:, :, jt * P:(jt + 1) * P],
                            in_=p_PT[:].rearrange('p (k q) -> p k q', k=NT)))
                    block.vector(_)
            vv_PTall = sems.val('vv')

            for h in range(2):
                def _(t, h=h, vv_PTall=vv_PTall):
                    T = EngOps(t, sems, 'tv')
                    t.wait_ge(sems.hdl('vv'), vv_PTall)
                    t.wait_ge(sems.hdl('vv'), sems.val('vv'))
                    for kc in range(NT):
                        vt = s_v[:, kc * P + h * 64: kc * P + h * 64 + 64]
                        base = h * NT * JW + kc * JW
                        t.ldweights(vt)
                        t.matmul(p_O[:, 0:512], vt, s_PT[:, base: base + 512],
                                 start=(kc == 0), stop=(kc == NT - 1))
                        op = lambda kc=kc, vt=vt, base=base: t.matmul(
                            p_O[:, 512:1024], vt, s_PT[:, base + 512: base + 1024],
                            start=(kc == 0), stop=(kc == NT - 1))
                        if kc == NT - 1:
                            T(op)
                        else:
                            op()
                block.tensor(_)
                tv_O = sems.val('tv')

                def _(v, h=h, tv_O=tv_O):
                    V = EngOps(v, sems, 'vv')
                    v.wait_ge(sems.hdl('tv'), tv_O)
                    V(lambda: v.tensor_copy(out=s_of[h * 64:(h + 1) * 64, :], in_=p_O[:, 0:1024]))
                block.vector(_)
            vv_O = sems.val('vv')

            def _(v, vv_O=vv_O):
                V = EngOps(v, sems, 'vv')
                v.wait_ge(sems.hdl('vv'), vv_O)
                V(lambda: v.tensor_copy(out=s_ofb[:], in_=s_of[:]))
            block.vector(_)
            vv_ofb = sems.val('vv')

            for jt in range(NT):
                pr = jt % 2
                def _(t, jt=jt, pr=pr, vv_ofb=vv_ofb, gq=mkS['qkv_vv'][pr]):
                    T = EngOps(t, sems, 'tv')
                    t.wait_ge(sems.hdl('vv'), vv_ofb)
                    t.wait_ge(sems.hdl('vv'), gq)
                    ot = s_ofb[:, jt * P:(jt + 1) * P]
                    t.ldweights(ot)
                    t.matmul(p_a[pr][:, 0:512], ot, s_wo[:, 0:512], start=True, stop=True)
                    T(lambda ot=ot, pr=pr: t.matmul(p_a[pr][:, 512:1024], ot, s_wo[:, 512:1024],
                                                    start=True, stop=True))
                block.tensor(_)
                tv_op = sems.val('tv')

                def _(v, pr=pr, tv_op=tv_op, st_pr=mkS['out_st'][pr]):
                    V = EngOps(v, sems, 'vv')
                    v.wait_ge(sems.hdl('tv'), tv_op)
                    v.wait_ge(sems.hdl('st'), st_pr)
                    V(lambda: v.tensor_copy(out=s_out[pr][:], in_=p_a[pr][:, 0:1024]))
                block.vector(_)
                mkS['qkv_vv'][pr] = sems.val('vv')
                vv_op = sems.val('vv')

                def _(sync, b=b, jt=jt, pr=pr, vv_op=vv_op, st_c=sems.val('st')):
                    sync.wait_ge(sems.hdl('vv'), vv_op)
                    sync.wait_ge(sems.hdl('st'), st_c)
                    sync.dma_start(out=part_o[jt, b], in_=s_out[pr][:]).then_inc(*sems.inc('st', 16))
                block.sync(_)
                mkS['out_st'][pr] = sems.val('st')

        def _(sync):
            sync.wait_ge(sems.hdl('st'), sems.val('st'))
        block.sync(_)
    return nc


def subattn_inputs(router_results, msel_results, I):
    JW = 8 * P
    xself = np.ascontiguousarray(
        np.stack([np.asarray(r['xsel']) for r in router_results], axis=0))
    mselfull = np.ascontiguousarray(
        np.stack([np.asarray(r['mselsl'], np.float32) for r in msel_results], axis=0))
    posfull = np.ascontiguousarray(
        np.stack([np.asarray(r['possel'], np.float32) for r in router_results], axis=0))
    inv = (1.0 / 10000.0 ** (np.arange(0, 64, 2, dtype=np.float32) / 64)).astype(np.float32)
    invbc = np.tile(inv, (P, 1))
    maps = []
    for c in range(NCORE):
        m = dict(common_inputs())
        m['xself'] = xself
        m['mselfull'] = mselfull
        m['posfull'] = posfull
        hc0 = 2 * c * 64
        m['wqsl'] = np.ascontiguousarray(I['L1_ffn_wq'][:, hc0:hc0 + P])
        m['wksl'] = np.ascontiguousarray(I['L1_ffn_wk'][:, hc0:hc0 + P])
        m['wvsl'] = np.ascontiguousarray(I['L1_ffn_wv'][:, hc0:hc0 + P])
        m['wosl'] = np.ascontiguousarray(I['L1_ffn_wo'][hc0:hc0 + P, :])
        m['ln1'] = I['L1_ffn_ln1'].reshape(1, D)
        m['invbc'] = invbc
        maps.append(m)
    return maps



def build_subffn(F=None):
    nc = bass.Bass('TRN2', target_bir_lowering=False, num_devices=NCORE) if F is None else F['nc']
    with ExitStack() as ctx:
        E = ctx.enter_context
        F4 = 4096
        FC = F4 // 512
        part_in = _dio(nc, F, 'partsum', [B, P, D], F32, 'ExternalInput')
        xsl_in = _dio(nc, F, 'xselown', [B, P, D], BF16, 'ExternalInput')
        wsel_in = _dio(nc, F, 'wselown', [B, 1, P], F32, 'ExternalInput')
        ln2_in = _dio(nc, F, 'ln2', [1, D], F32, 'ExternalInput')
        w1_in = _dio(nc, F, 'w1', [D, F4], F32, 'ExternalInput')
        w2_in = _dio(nc, F, 'w2', [F4, D], F32, 'ExternalInput')
        proc_o = _dio(nc, F, 'procsl', [B, P, D], F32, 'ExternalOutput')

        if F is None:
            sems, cm = open_common(nc, ctx)
            block = E(nc.Block())
            emit_common_loads(block, sems, cm)
        else:
            sems, cm, block = F['sems'], F['cm'], F['block']

        s_w1 = E(nc.sbuf_tensor('s_w1', [P, DC * F4], BF16))
        s_w2 = E(nc.sbuf_tensor('s_w2', [P, 32 * D], BF16))
        s_part = E(nc.sbuf_tensor('s_part', [P, D], F32))
        s_as = E(nc.sbuf_tensor('s_as', [P, D], F32))
        s_x1 = E(nc.sbuf_tensor('s_x1', [P, D], F32))
        s_x1n = E(nc.sbuf_tensor('s_x1n', [P, D], BF16))
        s_x1fm = E(nc.sbuf_tensor('s_x1fm', [P, D], BF16))
        s_ln2b = E(nc.sbuf_tensor('s_ln2b', [P, D], F32))
        s_ln2r = E(nc.sbuf_tensor('s_ln2r', [1, D], F32))
        s_wselc = E(nc.sbuf_tensor('s_wselc', [P, 1], F32))
        s_l = E(nc.sbuf_tensor('s_l', [P, 4], F32))
        s_sq = E(nc.sbuf_tensor('s_sq', [P, 512], F32))
        s_g2 = E(nc.sbuf_tensor('s_g2', [P, F4], BF16))
        s_g2T = E(nc.sbuf_tensor('s_g2T', [P, 32 * P], BF16))
        s_x2 = E(nc.sbuf_tensor('s_x2', [P, D], F32))

        p_h = E(nc.psum_tensor('p_h', [P, 2048], F32))
        p_tr = E(nc.psum_tensor('p_tr', [P, P], BF16))

        def _(g):
            g.wait_ge(sems.hdl('gd'), sems.val('gd'))
            g.dma_start(out=s_w1[:].rearrange('p (c f) -> p c f', c=DC),
                        in_=w1_in.rearrange('(c p) f -> p c f', p=P)).then_inc(*sems.inc('gd', 16))
            g.wait_ge(sems.hdl('gd'), sems.val('gd'))
            g.dma_start(out=s_w2[:].rearrange('p (c f) -> p c f', c=32),
                        in_=w2_in.rearrange('(c p) f -> p c f', p=P)).then_inc(*sems.inc('gd', 16))
        block.gpsimd(_)

        def _(sync):
            sync.wait_ge(sems.hdl('ld'), sems.val('ld'))
            sync.dma_start(out=s_ln2r[:], in_=ln2_in).then_inc(*sems.inc('ld', 16))
        block.sync(_)
        ld0 = sems.val('ld')
        gd0 = sems.val('gd')
        for eng in ('tensor', 'vector', 'scalar'):
            def _(e):
                e.wait_ge(sems.hdl('ld'), ld0)
                e.wait_ge(sems.hdl('gd'), gd0)
            getattr(block, eng)(_)

        def _(t):
            T = EngOps(t, sems, 'tv')
            t.matmul(p_h[:, 0:512], cm['onesrow_f'][:], s_ln2r[0:1, 0:512], start=True, stop=True)
            T(lambda: t.matmul(p_h[:, 512:1024], cm['onesrow_f'][:], s_ln2r[0:1, 512:1024],
                               start=True, stop=True))
        block.tensor(_)
        tv_ln = sems.val('tv')

        def _(v, tv_ln=tv_ln):
            V = EngOps(v, sems, 'vv')
            v.wait_ge(sems.hdl('tv'), tv_ln)
            V(lambda: v.tensor_copy(out=s_ln2b[:], in_=p_h[:, 0:1024]))
        block.vector(_)

        ccwF = (F or {}).get('ccw')
        for b in range(B):
            def _(sync, b=b):
                sync.wait_ge(sems.hdl('vv'), sems.val('vv'))
                if ccwF is not None:
                    sync.wait_ge(sems.hdl('cc'), ccwF)
                sync.wait_ge(sems.hdl('ld'), sems.val('ld'))
                sync.dma_start(out=s_as[:], in_=part_in[b]).then_inc(*sems.inc('ld', 16))
            block.sync(_)

            def _(sync, b=b):
                sync.wait_ge(sems.hdl('vv'), sems.val('vv'))
                sync.wait_ge(sems.hdl('ld'), sems.val('ld'))
                sync.dma_start(out=s_x1n[:], in_=xsl_in[b]).then_inc(*sems.inc('ld', 16))
                sync.wait_ge(sems.hdl('ld'), sems.val('ld'))
                sync.dma_start(out=s_wselc[:], in_=wsel_in[b].rearrange('o p -> p o')
                               ).then_inc(*sems.inc('ld', 16))
            block.sync(_)
            ld_x = sems.val('ld')

            def _(v, ld_x=ld_x):
                V = EngOps(v, sems, 'vv')
                v.wait_ge(sems.hdl('ld'), ld_x)
                V(lambda: v.tensor_tensor(out=s_x1[:], in0=s_as[:], in1=s_x1n[:], op=ALU.add))
            block.vector(_)
            vv_x1 = sems.val('vv')

            def _(s, vv_x1=vv_x1):
                S = EngOps(s, sems, 'sv')
                s.wait_ge(sems.hdl('vv'), vv_x1)
                S(lambda: s.activation(out=s_sq[:], in_=s_x1[:, 0:512], func=AF.Square,
                                       bias=col(cm, 'zero'), scale=1.0, accum_out=s_l[:, 0:1]))
                S(lambda: s.activation(out=s_sq[:], in_=s_x1[:, 512:1024], func=AF.Square,
                                       bias=col(cm, 'zero'), scale=1.0, accum_out=s_l[:, 1:2]))
            block.scalar(_)
            sv_sq = sems.val('sv')

            def _(v, sv_sq=sv_sq):
                V = EngOps(v, sems, 'vv')
                v.wait_ge(sems.hdl('sv'), sv_sq)
                V(lambda: v.tensor_tensor(out=s_l[:, 0:1], in0=s_l[:, 0:1], in1=s_l[:, 1:2],
                                          op=ALU.add))
            block.vector(_)
            vv_ssq = sems.val('vv')

            def _(s, vv_ssq=vv_ssq):
                S = EngOps(s, sems, 'sv')
                s.wait_ge(sems.hdl('vv'), vv_ssq)
                S(lambda: s.activation(out=s_l[:, 2:3], in_=s_l[:, 0:1], func=AF.Sqrt,
                                       bias=col(cm, 'eps_rms'), scale=1.0 / D))
            block.scalar(_)
            sv_sd = sems.val('sv')

            def _(v, sv_sd=sv_sd, tv_prev=sems.val('tv')):
                V = EngOps(v, sems, 'vv')
                v.wait_ge(sems.hdl('sv'), sv_sd)
                v.wait_ge(sems.hdl('tv'), tv_prev)
                V(lambda: v.reciprocal(out=s_l[:, 3:4], in_=s_l[:, 2:3]))
                V(lambda: v.tensor_scalar(out=s_x1n[:], in0=s_x1[:], scalar1=s_l[:, 3:4],
                                          scalar2=None, op0=ALU.mult))
                V(lambda: v.tensor_tensor(out=s_x1n[:], in0=s_x1n[:], in1=s_ln2b[:], op=ALU.mult))
            block.vector(_)
            vv_x1n = sems.val('vv')

            for dc in range(DC):
                def _(t, dc=dc, vv_x1n=vv_x1n):
                    T = EngOps(t, sems, 'tv')
                    t.wait_ge(sems.hdl('vv'), vv_x1n)
                    t.wait_ge(sems.hdl('vv'), sems.val('vv'))
                    T(lambda: t.transpose(p_tr[:], s_x1n[:, dc * P:(dc + 1) * P], cm['ident_b'][:]))
                block.tensor(_)
                tv_tr = sems.val('tv')

                def _(v, dc=dc, tv_tr=tv_tr):
                    V = EngOps(v, sems, 'vv')
                    v.wait_ge(sems.hdl('tv'), tv_tr)
                    V(lambda: v.tensor_copy(out=s_x1fm[:, dc * P:(dc + 1) * P], in_=p_tr[:]))
                block.vector(_)
            vv_fm = sems.val('vv')

            for hf in range(2):
                def _(t, hf=hf, vv_fm=vv_fm):
                    T = EngOps(t, sems, 'tv')
                    t.wait_ge(sems.hdl('vv'), vv_fm)
                    t.wait_ge(sems.hdl('vv'), sems.val('vv'))
                    t.wait_ge(sems.hdl('sv'), sems.val('sv'))
                    for dc in range(DC):
                        xt = s_x1fm[:, dc * P:(dc + 1) * P]
                        t.ldweights(xt)
                        for fc in range(4):
                            fcg = hf * 4 + fc
                            op = lambda dc=dc, fc=fc, fcg=fcg, xt=xt: t.matmul(
                                p_h[:, fc * 512:(fc + 1) * 512], xt,
                                s_w1[:, dc * F4 + fcg * 512: dc * F4 + (fcg + 1) * 512],
                                start=(dc == 0), stop=(dc == DC - 1))
                            if dc == DC - 1 and fc == 3:
                                T(op)
                            else:
                                op()
                block.tensor(_)
                tv_h = sems.val('tv')

                for fc in range(4):
                    fcg = hf * 4 + fc
                    def _(s, fc=fc, tv_h=tv_h):
                        S = EngOps(s, sems, 'sv')
                        s.wait_ge(sems.hdl('tv'), tv_h)
                        s.wait_ge(sems.hdl('vv'), sems.val('vv'))
                        S(lambda: s.activation(out=s_sq[:], in_=p_h[:, fc * 512:(fc + 1) * 512],
                                               func=AF.Square, bias=col(cm, 'zero'), scale=1.0))
                    block.scalar(_)
                    sv_g1 = sems.val('sv')

                    def _(v, fc=fc, sv_g1=sv_g1):
                        V = EngOps(v, sems, 'vv')
                        v.wait_ge(sems.hdl('sv'), sv_g1)
                        hs = p_h[:, fc * 512:(fc + 1) * 512]
                        V(lambda: v.tensor_tensor(out=s_sq[:], in0=s_sq[:], in1=hs, op=ALU.mult))
                        V(lambda: v.scalar_tensor_tensor(out=s_sq[:], in0=s_sq[:],
                                                         scalar=col(cm, 'c_g1'), in1=hs,
                                                         op0=ALU.mult, op1=ALU.add))
                    block.vector(_)
                    vv_u = sems.val('vv')

                    def _(s, vv_u=vv_u):
                        S = EngOps(s, sems, 'sv')
                        s.wait_ge(sems.hdl('vv'), vv_u)
                        S(lambda: s.activation(out=s_sq[:], in_=s_sq[:], func=AF.Tanh,
                                               bias=col(cm, 'zero'), scale=0.7978845608028654))
                    block.scalar(_)
                    sv_t = sems.val('sv')

                    def _(v, fc=fc, fcg=fcg, sv_t=sv_t):
                        V = EngOps(v, sems, 'vv')
                        v.wait_ge(sems.hdl('sv'), sv_t)
                        hs = p_h[:, fc * 512:(fc + 1) * 512]
                        V(lambda: v.tensor_scalar(out=s_sq[:], in0=s_sq[:],
                                                  scalar1=col(cm, 'half'), scalar2=col(cm, 'half'),
                                                  op0=ALU.mult, op1=ALU.add))
                        V(lambda: v.tensor_tensor(out=s_g2[:, fcg * 512:(fcg + 1) * 512],
                                                  in0=s_sq[:], in1=hs, op=ALU.mult))
                    block.vector(_)
            vv_g2 = sems.val('vv')

            for hc in range(32):
                def _(t, hc=hc, vv_g2=vv_g2):
                    T = EngOps(t, sems, 'tv')
                    t.wait_ge(sems.hdl('vv'), vv_g2)
                    t.wait_ge(sems.hdl('vv'), sems.val('vv'))
                    T(lambda: t.transpose(p_tr[:], s_g2[:, hc * P:(hc + 1) * P], cm['ident_b'][:]))
                block.tensor(_)
                tv_g2t = sems.val('tv')

                def _(v, hc=hc, tv_g2t=tv_g2t):
                    V = EngOps(v, sems, 'vv')
                    v.wait_ge(sems.hdl('tv'), tv_g2t)
                    V(lambda: v.tensor_copy(out=s_g2T[:, hc * P:(hc + 1) * P], in_=p_tr[:]))
                block.vector(_)
            vv_g2T = sems.val('vv')

            def _(t, vv_g2T=vv_g2T):
                T = EngOps(t, sems, 'tv')
                t.wait_ge(sems.hdl('vv'), vv_g2T)
                t.wait_ge(sems.hdl('vv'), sems.val('vv'))
                for hc in range(32):
                    gt = s_g2T[:, hc * P:(hc + 1) * P]
                    t.ldweights(gt)
                    t.matmul(p_h[:, 0:512], gt, s_w2[:, hc * D: hc * D + 512],
                             start=(hc == 0), stop=(hc == 31))
                    op = lambda hc=hc, gt=gt: t.matmul(
                        p_h[:, 512:1024], gt, s_w2[:, hc * D + 512: hc * D + 1024],
                        start=(hc == 0), stop=(hc == 31))
                    if hc == 31:
                        T(op)
                    else:
                        op()
            block.tensor(_)
            tv_y = sems.val('tv')

            def _(v, b=b, tv_y=tv_y):
                V = EngOps(v, sems, 'vv')
                v.wait_ge(sems.hdl('tv'), tv_y)
                v.wait_ge(sems.hdl('st'), sems.val('st'))
                V(lambda: v.tensor_tensor(out=s_x2[:], in0=s_x1[:], in1=p_h[:, 0:1024], op=ALU.add))
                V(lambda: v.tensor_scalar(out=s_x2[:], in0=s_x2[:], scalar1=s_wselc[:],
                                          scalar2=None, op0=ALU.mult))
            block.vector(_)
            vv_p = sems.val('vv')

            def _(sync, b=b, vv_p=vv_p):
                sync.wait_ge(sems.hdl('vv'), vv_p)
                sync.dma_start(out=proc_o[b], in_=s_x2[:]).then_inc(*sems.inc('st', 16))
            block.sync(_)

        def _(sync):
            sync.wait_ge(sems.hdl('st'), sems.val('st'))
        block.sync(_)
    return nc


def subffn_inputs(subattn_results, router_results, I, core):
    maps = []
    partial = np.stack([np.asarray(r['partial'], np.float32) for r in subattn_results])
    for c in range(NCORE):
        m = dict(common_inputs())
        m['partown'] = np.ascontiguousarray(partial[:, c])
        m['xselown'] = np.ascontiguousarray(np.asarray(router_results[c]['xsel']))
        m['wselown'] = np.ascontiguousarray(np.asarray(router_results[c]['wsel'], np.float32))
        m['ln2'] = I['L1_ffn_ln2'].reshape(1, D)
        m['w1'] = np.ascontiguousarray(I['L1_ffn_w1'])
        m['w2'] = np.ascontiguousarray(I['L1_ffn_w2'])
        maps.append(m)
    return maps



def build_combine(F=None):
    nc = bass.Bass('TRN2', target_bir_lowering=False, num_devices=NCORE) if F is None else F['nc']
    with ExitStack() as ctx:
        E = ctx.enter_context
        proc_in = _dio(nc, F, 'procfull', [NCORE, B, P, D], F32, 'ExternalInput')
        down_in = _dio(nc, F, 'down', [B, NT, P], F32, 'ExternalInput')
        dcols_in = _dio(nc, F, 'dcols', [B, P, NT], F32, 'ExternalInput')
        xn_in = _dio(nc, F, 'xnf32', [B, NT, P, D], F32, 'ExternalInput')
        attn_in = _dio(nc, F, 'attn1', [B, NT, P, D], F32, 'ExternalInput')
        x1_in = _dio(nc, F, 'x1', [B, NT, P, D], F32, 'ExternalInput')
        jplus_in = _dio(nc, F, 'jplus', [P, NT], F32, 'ExternalInput')
        out_o = _dio(nc, F, 'out', [B, NT, P, D], F32, 'ExternalOutput')

        if F is None:
            sems, cm = open_common(nc, ctx)
            block = E(nc.Block())
            emit_common_loads(block, sems, cm)
        else:
            sems, cm, block = F['sems'], F['cm'], F['block']

        s_proc = E(nc.sbuf_tensor('s_proc', [P, NCORE * D], BF16))
        s_jplus = E(nc.sbuf_tensor('s_jplus', [P, NT], F32))
        s_drow = E(nc.sbuf_tensor('s_drow', [1, NT * P], F32))
        s_dcol = E(nc.sbuf_tensor('s_dcol', [P, NT], F32))
        s_dbc = E(nc.sbuf_tensor('s_dbc', [P, P], F32))
        s_DT = E(nc.sbuf_tensor('s_DT', [P, NCORE * P], BF16))
        s_m01 = E(nc.sbuf_tensor('s_m01', [P, 2], F32))
        s_xn = E(nc.sbuf_tensor('s_xn', [P, D], F32))
        s_at = E(nc.sbuf_tensor('s_at', [P, D], F32))
        s_x1 = E(nc.sbuf_tensor('s_x1', [P, D], F32))
        s_ffn = E(nc.sbuf_tensor('s_ffn', [P, D], F32))
        s_out = E(nc.sbuf_tensor('s_out', [P, D], F32))

        p_g = E(nc.psum_tensor('p_g', [P, 1024], F32))
        p_bc = E(nc.psum_tensor('p_bc', [P, P], F32))

        def _(sync):
            sync.wait_ge(sems.hdl('ld'), sems.val('ld'))
            sync.dma_start(out=s_jplus[:], in_=jplus_in).then_inc(*sems.inc('ld', 16))
        block.sync(_)
        ld0 = sems.val('ld')
        for eng in ('tensor', 'vector', 'scalar'):
            def _(e):
                e.wait_ge(sems.hdl('ld'), ld0)
                e.wait_ge(sems.hdl('gd'), sems.val('gd'))
            getattr(block, eng)(_)

        for b in range(B):
            def _(g, b=b):
                g.wait_ge(sems.hdl('tv'), sems.val('tv'))
                for r in range(NCORE):
                    if F is not None and 'ccw' in F:
                        g.wait_ge(sems.hdl('cc'), F['ccw'])
                    g.wait_ge(sems.hdl('gd'), sems.val('gd'))
                    g.dma_start(out=s_proc[:, r * D:(r + 1) * D],
                                in_=proc_in[r, b]).then_inc(*sems.inc('gd', 16))
            block.gpsimd(_)
            gd_p = sems.val('gd')

            def _(sync, b=b):
                sync.wait_ge(sems.hdl('tv'), sems.val('tv'))
                sync.wait_ge(sems.hdl('ld'), sems.val('ld'))
                sync.dma_start(out=s_drow[:].rearrange('o (i p) -> o i p', i=NT),
                               in_=down_in[b].rearrange('i p -> i p').rearrange(
                                   '(o i) p -> o i p', o=1)).then_inc(*sems.inc('ld', 16))
                sync.wait_ge(sems.hdl('ld'), sems.val('ld'))
                sync.dma_start(out=s_dcol[:], in_=dcols_in[b]).then_inc(*sems.inc('ld', 16))
            block.sync(_)
            ld_d = sems.val('ld')

            for i in range(NT):
                def _(t, b=b, i=i, ld_d=ld_d):
                    T = EngOps(t, sems, 'tv')
                    t.wait_ge(sems.hdl('ld'), ld_d)
                    t.wait_ge(sems.hdl('vv'), sems.val('vv'))
                    T(lambda: t.matmul(p_bc[:], cm['onesrow_f'][:],
                                       s_drow[0:1, i * P:(i + 1) * P],
                                       start=True, stop=True))
                block.tensor(_)
                tv_bc = sems.val('tv')

                def _(v, tv_bc=tv_bc):
                    V = EngOps(v, sems, 'vv')
                    v.wait_ge(sems.hdl('tv'), tv_bc)
                    V(lambda: v.tensor_copy(out=s_dbc[:], in_=p_bc[:]))
                    for jc in range(NCORE):
                        V(lambda jc=jc: v.tensor_scalar(out=s_DT[:, jc * P:(jc + 1) * P],
                                                        in0=s_dbc[:], scalar1=s_jplus[:, jc:jc + 1],
                                                        scalar2=None, op0=ALU.is_equal))
                block.vector(_)
                vv_DT = sems.val('vv')

                def _(t, i=i, gd_p=gd_p, vv_DT=vv_DT):
                    T = EngOps(t, sems, 'tv')
                    t.wait_ge(sems.hdl('gd'), gd_p)
                    t.wait_ge(sems.hdl('vv'), vv_DT)
                    t.wait_ge(sems.hdl('vv'), sems.val('vv'))
                    for jc in range(NCORE):
                        dt_t = s_DT[:, jc * P:(jc + 1) * P]
                        t.ldweights(dt_t)
                        t.matmul(p_g[:, 0:512], dt_t, s_proc[:, jc * D: jc * D + 512],
                                 start=(jc == 0), stop=(jc == NCORE - 1))
                        op = lambda jc=jc, dt_t=dt_t: t.matmul(
                            p_g[:, 512:1024], dt_t, s_proc[:, jc * D + 512: jc * D + 1024],
                            start=(jc == 0), stop=(jc == NCORE - 1))
                        if jc == NCORE - 1:
                            T(op)
                        else:
                            op()
                block.tensor(_)
                tv_g = sems.val('tv')

                def _(sync, b=b, i=i):
                    sync.wait_ge(sems.hdl('vv'), sems.val('vv'))
                    sync.wait_ge(sems.hdl('ld'), sems.val('ld'))
                    sync.dma_start(out=s_xn[:], in_=xn_in[b, i]).then_inc(*sems.inc('ld', 16))
                    sync.wait_ge(sems.hdl('ld'), sems.val('ld'))
                    sync.dma_start(out=s_at[:], in_=attn_in[b, i]).then_inc(*sems.inc('ld', 16))
                    sync.wait_ge(sems.hdl('ld'), sems.val('ld'))
                    sync.dma_start(out=s_x1[:], in_=x1_in[b, i]).then_inc(*sems.inc('ld', 16))
                block.sync(_)
                ld_t = sems.val('ld')

                def _(v, b=b, i=i, tv_g=tv_g, ld_t=ld_t):
                    V = EngOps(v, sems, 'vv')
                    v.wait_ge(sems.hdl('tv'), tv_g)
                    v.wait_ge(sems.hdl('ld'), ld_t)
                    v.wait_ge(sems.hdl('st'), sems.val('st'))
                    V(lambda: v.tensor_scalar(out=s_m01[:, 0:1], in0=s_dcol[:, i:i + 1],
                                              scalar1=col(cm, 'zero'), scalar2=None, op0=ALU.is_gt))
                    V(lambda: v.tensor_scalar(out=s_m01[:, 1:2], in0=s_m01[:, 0:1],
                                              scalar1=col(cm, 'neg_one'), scalar2=col(cm, 'one'),
                                              op0=ALU.mult, op1=ALU.add))
                    V(lambda: v.scalar_tensor_tensor(out=s_ffn[:], in0=s_xn[:],
                                                     scalar=s_m01[:, 1:2], in1=p_g[:, 0:1024],
                                                     op0=ALU.mult, op1=ALU.add))
                    V(lambda: v.tensor_tensor(out=s_out[:], in0=s_x1[:], in1=s_at[:], op=ALU.add))
                    V(lambda: v.tensor_tensor(out=s_out[:], in0=s_out[:], in1=s_ffn[:], op=ALU.add))
                block.vector(_)
                vv_o = sems.val('vv')

                def _(sync, b=b, i=i, vv_o=vv_o):
                    sync.wait_ge(sems.hdl('vv'), vv_o)
                    sync.dma_start(out=out_o[b, i], in_=s_out[:]).then_inc(*sems.inc('st', 16))
                block.sync(_)

        def _(sync):
            sync.wait_ge(sems.hdl('st'), sems.val('st'))
        block.sync(_)
    return nc


def combine_inputs(subffn_results, router_results, prep1_results, attn1_results, x1_parts):
    procfull = np.ascontiguousarray(
        np.stack([np.asarray(r['procsl'], np.float32) for r in subffn_results]))
    jplus = np.ascontiguousarray(
        (np.arange(P)[:, None] + 128.0 * np.arange(NT)[None, :] + 1.0).astype(np.float32))
    maps = []
    for c in range(NCORE):
        m = dict(common_inputs())
        m['procfull'] = procfull
        m['down'] = np.ascontiguousarray(np.asarray(router_results[c]['down'], np.float32))
        m['dcols'] = np.ascontiguousarray(np.asarray(router_results[c]['dcols'], np.float32))
        m['xnf32'] = np.ascontiguousarray(np.asarray(prep1_results[c]['xnf32'], np.float32))
        m['attn1'] = np.ascontiguousarray(np.asarray(attn1_results[c]['attn'], np.float32))
        m['x1'] = np.ascontiguousarray(x1_parts[c])
        m['jplus'] = jplus
        maps.append(m)
    return maps




def _flat2d(ap, rows_dims):
    nd = len(ap.shape)
    names = [chr(ord('a') + i) for i in range(nd)]
    lhs = ' '.join(names)
    rhs = '(' + ' '.join(names[:rows_dims]) + ') (' + ' '.join(names[rows_dims:]) + ')'
    return ap.rearrange(f'{lhs} -> {rhs}')


def build_fused(dbg=()):
    nc = bass.Bass('TRN2', target_bir_lowering=False, num_devices=NCORE)
    JW = 8 * P
    with ExitStack() as ctx:
        sems, cm = open_common(nc, ctx, extra=('cc',))
        block = ctx.enter_context(nc.Block())
        emit_common_loads(block, sems, cm)
        out_ext = nc.dram_tensor('out', [B, NT, P, D], F32, kind='ExternalOutput').ap()
        R = {}

        def F(pfx, io):
            return {'nc': _NcProxy(nc, pfx), 'sems': sems, 'cm': cm, 'block': block,
                    'reg': R, 'pfx': pfx, 'io': io, 'dbg': dbg}

        def ags(pairs, barrier=True):
            ccv = []
            def _(g):
                for n in sems.c:
                    g.wait_ge(sems.hdl(n), sems.val(n))
                for kind, op, i2, o2 in pairs:
                    g.collective_compute(kind, op, replica_groups=[list(range(NCORE))],
                                         ins=[i2.opt()], outs=[o2.opt()]
                                         ).then_inc(*sems.inc('cc'))
                    ccv.append(sems.val('cc'))
            block.gpsimd(_)
            if barrier:
                stage_barrier(block, sems)
            return ccv

        def dram(name, shape, dtype):
            return nc.dram_tensor(name, shape, dtype, kind='Internal').ap()

        def dbg_copy(name, ap):
            if name not in dbg:
                return
            d = nc.dram_tensor('dbg_' + name, list(ap.shape), ap.dtype, kind='ExternalOutput').ap()

            def _(sync):
                for n in sems.c:
                    sync.wait_ge(sems.hdl(n), sems.val(n))
                sync.dma_start(out=d, in_=ap).then_inc(*sems.inc('st', 16))
            block.sync(_)
            stage_barrier(block, sems)

        def kv_ab(pfx, kag, vag, ccv):
            def cb(b):
                v = ags([('AllGather', ALU.bypass,
                          _flat2d(R[pfx + 'kfm'][b], 2), _flat2d(kag[b], 3)),
                         ('AllGather', ALU.bypass,
                          _flat2d(R[pfx + 'vtm'][b], 2), _flat2d(vag[b], 3))],
                        barrier=False)
                ccv.append(v[-1])
            return cb

        kag0 = dram('kag0', [B, NCORE, NT, P, DC, P], BF16)
        vag0 = dram('vag0', [B, NCORE, NT, P, D], BF16)
        ccv0 = []
        fp0 = F('p0_', {})
        fp0['after_batch'] = kv_ab('p0_', kag0, vag0, ccv0)
        build_prep(0, fp0)
        stage_barrier(block, sems, skip=('cc',))
        f0 = F('a0_', {'qfm': R['p0_qfm'],
                       'kag': kag0.rearrange('b r i p c q -> r b i p c q'),
                       'vfull': vag0.rearrange('b r i p d -> r b i p d')})
        f0['ccw'] = {0: ccv0[0], 1: ccv0[1]}
        build_attn(0, f0)
        stage_barrier(block, sems)
        build_moe(F('m_', {'x': R['p0_x'], 'attn': R['a0_attn'],
                           'xnfm': R['p0_xnfm'], 'w8e': R['p0_w8']}))
        stage_barrier(block, sems)

        kag1 = dram('kag1', [B, NCORE, NT, P, DC, P], BF16)
        vag1 = dram('vag1', [B, NCORE, NT, P, D], BF16)
        wag = dram('wag', [NCORE, B, 1, L], F32)
        xntmag = dram('xntmag', [NCORE, B, NT, P, D], BF16)
        ccv1 = {}

        def ab1(b):
            pairs = []
            if b == 1:
                pairs.append(('AllGather', ALU.bypass, _flat2d(R['p1_wrow'], 2),
                              _flat2d(wag, 3)))
            pairs.append(('AllGather', ALU.bypass, _flat2d(R['p1_kfm'][b], 2),
                          _flat2d(kag1[b], 3)))
            pairs.append(('AllGather', ALU.bypass, _flat2d(R['p1_vtm'][b], 2),
                          _flat2d(vag1[b], 3)))
            if b == 1:
                pairs.append(('AllGather', ALU.bypass, _flat2d(R['p1_xntm'], 3),
                              _flat2d(xntmag, 4)))
            v = ags(pairs, barrier=False)
            if b == 0:
                ccv1[0] = v[1]
            else:
                ccv1['wag'] = v[0]
                ccv1[1] = v[2]

        fp1 = F('p1_', {'x': R['m_x1']})
        fp1['after_batch'] = ab1
        build_prep(1, fp1)
        stage_barrier(block, sems, skip=('cc',))
        with ExitStack() as rctx:
            p_rt = rctx.enter_context(nc.psum_tensor('rb_prt', [P, 512], F32))
            bis = emit_router_bis(nc, rctx.enter_context, sems, cm, block,
                                  wag, p_rt, cc_after=ccv1['wag'])
            bis['loads']()
            f1 = F('a1_', {'qfm': R['p1_qfm'],
                           'kag': kag1.rearrange('b r i p c q -> r b i p c q'),
                           'vfull': vag1.rearrange('b r i p d -> r b i p d')})
            f1['ccw'] = {0: ccv1[0], 1: ccv1[1]}
            f1['ileave'] = bis['round']
            build_attn(1, f1)
            bis['finish']()
            stage_barrier(block, sems)
            fr = F('r_', {'wfull': wag, 'xntmfull': xntmag})
            fr['bis'] = bis
            build_router(fr)
            stage_barrier(block, sems)
        selcag = dram('selcag', [NCORE, B, 1, P], F32)
        posag = dram('posag', [NCORE, B, 1, P], F32)
        xselag = dram('xselag', [NCORE, B, P, D], BF16)
        ccg1 = ags([('AllGather', ALU.bypass, _flat2d(R['r_selc'], 2), _flat2d(selcag, 3)),
                    ('AllGather', ALU.bypass, _flat2d(R['r_possel'], 2), _flat2d(posag, 3)),
                    ('AllGather', ALU.bypass, _flat2d(R['r_xsel'], 2), _flat2d(xselag, 3))],
                   barrier=False)
        fc = F('c_', {'selcfull': selcag})
        fc['ccw'] = ccg1[0]
        build_maskc(fc)
        stage_barrier(block, sems)
        build_msel(F('s_', {'cown': R['c_cvsl'], 'selcfull': selcag}))
        stage_barrier(block, sems)
        mselsum = dram('mselsum', [B, JW, JW], F32)
        ccar = ags([('AllReduce', ALU.add, _flat2d(R['s_mselp'], 2), _flat2d(mselsum, 2))],
                   barrier=False)
        fsa = F('sa_', {'xself': xselag,
                        'mselfull': mselsum.rearrange('b (t p) k -> t b p k', p=P),
                        'posfull': posag})
        fsa['ccw'] = ccar[0]
        build_subattn(fsa)
        stage_barrier(block, sems)
        partsum = dram('partsum', [B, P, D], F32)
        ccrs = ags([('ReduceScatter', ALU.add, _flat2d(R['sa_partial'], 3), _flat2d(partsum, 2))],
                   barrier=False)
        fsf = F('sf_', {'partsum': partsum, 'xselown': R['r_xsel'],
                        'wselown': R['r_wsel']})
        fsf['ccw'] = ccrs[0]
        build_subffn(fsf)
        stage_barrier(block, sems)
        procag = dram('procag', [NCORE, B, P, D], F32)
        ccpr = ags([('AllGather', ALU.bypass, _flat2d(R['sf_procsl'], 2), _flat2d(procag, 3))],
                   barrier=False)
        fcb = F('cb_', {'procfull': procag, 'down': R['r_down'],
                        'dcols': R['r_dcols'], 'xnf32': R['p1_xnf32'],
                        'attn1': R['a1_attn'], 'x1': R['m_x1'], 'out': out_ext})
        fcb['ccw'] = ccpr[0]
        build_combine(fcb)
        stage_barrier(block, sems)

        def _(sync):
            sync.wait_ge(sems.hdl('st'), sems.val('st'))
        block.sync(_)
    return nc


def fused_inputs(I):
    x_parts = shard_x(I['L0_attn_x'])
    inv = (1.0 / 10000.0 ** (np.arange(0, 64, 2, dtype=np.float32) / 64)).astype(np.float32)
    mask = I['L1_ffn_attention_mask']
    uincl = np.triu(np.ones((P, P), np.float32))
    ucols = np.ascontiguousarray(
        (np.arange(P)[:, None] + 128.0 * np.arange(32)[None, :]).astype(np.float32))
    import ml_dtypes
    selcols = np.zeros((P, 65), np.float32)
    selcols[:, 0] = np.arange(P)
    for ch in range(64):
        selcols[:, 1 + ch] = 2.0 * ch
    selcols = selcols.astype(ml_dtypes.bfloat16)
    padmask = np.zeros((P, 8 * P), np.float32)
    padmask[:, 8 * P - 1] = -1e9
    jplus = np.ascontiguousarray(
        (np.arange(P)[:, None] + 128.0 * np.arange(NT)[None, :] + 1.0).astype(np.float32))
    w8tail = np.tile(np.array([0.5, 0.5, 0, 0, 0, 0, 0, 0], np.float32), (P, 1))
    o = np.arange(1024)[None, :]
    pq = np.arange(P)[:, None]

    w1all = np.ascontiguousarray(np.concatenate([I['L0_ffn_rw1'], I['L0_ffn_sw1']], 0))
    w2all = np.ascontiguousarray(np.concatenate([I['L0_ffn_rw2'], I['L0_ffn_sw2']], 0))
    w3all = np.ascontiguousarray(np.concatenate([I['L0_ffn_rw3'], I['L0_ffn_sw3']], 0))
    b1all = np.concatenate([I['L0_ffn_rb1'], I['L0_ffn_sb1']], 0)
    b2all = np.concatenate([I['L0_ffn_rb2'], I['L0_ffn_sb2']], 0)
    b3all = np.zeros((16, D), np.float32)
    b3all[:NEXP] = np.concatenate([I['L0_ffn_rb3'], I['L0_ffn_sb3']], 0)
    b1fm = np.ascontiguousarray(b1all.reshape(NEXP, HC, P).transpose(2, 0, 1).reshape(P, NEXP * HC))
    b2fm = np.ascontiguousarray(b2all.reshape(NEXP, HC, P).transpose(2, 0, 1).reshape(P, NEXP * HC))

    maps = []
    for c in range(NCORE):
        m = dict(common_inputs())
        m['p0_x'] = np.ascontiguousarray(x_parts[c])
        m['p0_wqt'] = np.ascontiguousarray(I['L0_attn_Wq'].T)
        m['p0_wkt'] = np.ascontiguousarray(I['L0_attn_Wk'].T)
        m['p0_wvt'] = np.ascontiguousarray(I['L0_attn_Wv'].T)
        m['p0_bq'] = np.ascontiguousarray(I['L0_attn_bq'].reshape(DC, P).T)
        m['p0_bk'] = np.ascontiguousarray(I['L0_attn_bk'].reshape(DC, P).T)
        m['p0_bv'] = I['L0_attn_bv'].reshape(1, D)
        m['p0_gw'] = np.ascontiguousarray(I['L0_ffn_gate_w'])
        m['p0_gb'] = I['L0_ffn_gate_b'].reshape(1, 8)
        m['p0_w8tail'] = w8tail
        dmask = np.where(o > 128 * c + pq, -1e9, 0.0).astype(np.float32)
        m['a0_dmask'] = dmask
        m['a1_dmask'] = dmask
        m['a1_wot'] = np.ascontiguousarray(I['L1_attn_Wo'].T)
        m.update(m_w1all=w1all, m_w2all=w2all, m_w3all=w3all,
                 m_b1fm=b1fm, m_b2fm=b2fm, m_b3all=b3all)
        m['p1_wqt'] = np.ascontiguousarray(I['L1_attn_Wq'].T)
        m['p1_wkt'] = np.ascontiguousarray(I['L1_attn_Wk'].T)
        m['p1_wvt'] = np.ascontiguousarray(I['L1_attn_Wv'].T)
        m['p1_rw'] = np.ascontiguousarray(I['L1_ffn_router_w'].reshape(DC, P).T)
        m['p1_rb'] = I['L1_ffn_router_b'].reshape(1, 1)
        m['r_posids'] = np.ascontiguousarray(I['L1_ffn_position_ids'])
        m['r_uincl'] = uincl
        jbase = c * P
        m['r_jrowbc'] = np.tile(np.arange(jbase + 1, jbase + P + 1, dtype=np.float32), (P, 1))
        m['r_selcols'] = selcols
        m['r_ucols'] = ucols
        sel = np.zeros((64, NT), np.float32)
        for i in range(NT):
            sel[NCORE * i + c, i] = 1.0
        m['r_selmat'] = sel
        m['c_maskvsl'] = np.ascontiguousarray(mask[0:B, 0, :, c * 512:(c + 1) * 512])
        m['c_ucols'] = ucols
        m['s_vcols'] = np.ascontiguousarray(
            (np.arange(P)[:, None] + 512.0 * c + 128.0 * np.arange(4)[None, :]).astype(np.float32))
        m['s_padmask'] = padmask if c == 0 else np.zeros_like(padmask)
        hc0 = 2 * c * 64
        m['sa_wqsl'] = np.ascontiguousarray(I['L1_ffn_wq'][:, hc0:hc0 + P])
        m['sa_wksl'] = np.ascontiguousarray(I['L1_ffn_wk'][:, hc0:hc0 + P])
        m['sa_wvsl'] = np.ascontiguousarray(I['L1_ffn_wv'][:, hc0:hc0 + P])
        m['sa_wosl'] = np.ascontiguousarray(I['L1_ffn_wo'][hc0:hc0 + P, :])
        m['sa_ln1'] = I['L1_ffn_ln1'].reshape(1, D)
        m['sa_invbc'] = np.tile(inv, (P, 1))
        m['sf_ln2'] = I['L1_ffn_ln2'].reshape(1, D)
        m['sf_w1'] = np.ascontiguousarray(I['L1_ffn_w1'])
        m['sf_w2'] = np.ascontiguousarray(I['L1_ffn_w2'])
        m['cb_jplus'] = jplus
        maps.append(m)
    return maps


def kernel(**inputs):
    nc = build_fused()
    maps = fused_inputs(inputs)
    res = run_bass_kernel_spmd(nc, maps, core_ids=list(range(NCORE)))
    out_parts = [np.asarray(r['out'], np.float32) for r in res.results]
    return unshard_x(out_parts).astype(np.float32)
